# version 20
# baseline (speedup 1.0000x reference)
import contextlib
import numpy as np
import ml_dtypes
import concourse.bass as bass
import concourse.mybir as mybir
from concourse.bass_utils import run_bass_kernel_spmd

F32 = mybir.dt.float32
BF16 = mybir.dt.bfloat16
I32 = mybir.dt.int32
AF = mybir.ActivationFunctionType
ALU = mybir.AluOpType
AX = mybir.AxisListType.X
BFD = ml_dtypes.bfloat16

D = 1024
S = 4096
SO = 2048
H = 16
HD = 64
NHP = 8
NT = 32
NTO = 16
QB = 4
KC = 32
INNER = 4096
NTB = 4
GRP = [(i, 1) for i in range(32)]
MAGIC = 256.5
SET_LNEXP = 6
SET_SILU = 18


def load_act_set(sc, nc, set_id):
    return sc.add_instruction(mybir.InstLoadActFuncSet(
        name=nc.get_next_instruction_name(), act_func_set_id=set_id, ins=[], outs=[]))


class Ch:

    def __init__(self, sem, serialize=True):
        self.sem = sem
        self.serialize = serialize
        self.n = 0

    def __call__(self, inst):
        if self.serialize and self.n:
            inst._wait_ge(self.sem, self.n)
        inst.then_inc(self.sem, 1)
        self.n += 1
        return self.n

    def gate(self, inst):
        if self.n:
            inst._wait_ge(self.sem, self.n)
        return inst


def build(debug=()):
    nc = bass.Bass("TRN2", target_bir_lowering=False,
                   detect_race_conditions=False)
    x_d = nc.dram_tensor("x", [S, D], F32, kind="ExternalInput").ap()
    wq_d = nc.dram_tensor("wq", [D, D], F32, kind="ExternalInput").ap()
    wk_d = nc.dram_tensor("wk", [D, D], F32, kind="ExternalInput").ap()
    wv_d = nc.dram_tensor("wv", [D, D], F32, kind="ExternalInput").ap()
    lnsc_d = nc.dram_tensor("lnsc", [1, D], F32, kind="ExternalInput").ap()
    lnbi_d = nc.dram_tensor("lnbi", [1, D], F32, kind="ExternalInput").ap()
    wg_d = nc.dram_tensor("wg", [INNER, D], F32, kind="ExternalInput").ap()
    wu_d = nc.dram_tensor("wu", [INNER, D], F32, kind="ExternalInput").ap()
    wd_d = nc.dram_tensor("wd", [D, INNER], F32, kind="ExternalInput").ap()
    idb_d = nc.dram_tensor("idb", [128, 128], BF16, kind="ExternalInput").ap()
    idf_d = nc.dram_tensor("idf", [128, 128], F32, kind="ExternalInput").ap()
    out_d = nc.dram_tensor("out", [SO, D], F32, kind="ExternalOutput").ap()

    def scratch(name, shape, dtype):
        kind = "ExternalOutput" if name in debug else "Internal"
        return nc.dram_tensor(name, shape, dtype, kind=kind).ap()

    z_dram = scratch("zd", [S, D], BF16)
    wg_nat = scratch("wgn", [INNER, D], BF16)
    wu_nat = scratch("wun", [INNER, D], BF16)
    wd_nat = scratch("wdn", [D, INNER], BF16)
    wgsp_d = nc.dram_tensor("wgsp", [2, 8, 128, 8 * 512], BF16, kind="Internal").ap()
    wdsp_d = nc.dram_tensor("wdsp", [8, 128, 4 * D], BF16, kind="Internal").ap()
    pqd = scratch("pqd", [NTB, 32, 128, 512], BF16)
    y1_dram = scratch("y1d", [SO, D], BF16)
    y1q_dram = scratch("y1qd", [SO, D], BF16)
    atn_dbg = scratch("atn", [SO, D], BF16) if "atn" in debug else None
    q_sp = scratch("qsp", [NHP, 128, SO], BF16)
    k_sp = scratch("ksp", [NHP, 128, S], BF16)
    v_sp = scratch("vsp", [NHP, 128, KC * 130], BF16)
    sres_dbg = scratch("sres", [SO, D], F32) if "sres" in debug else None
    n_dbg = [0]

    es = contextlib.ExitStack()
    sem = lambda n: es.enter_context(nc.semaphore(n))

    CV = Ch(sem("c_v"))
    CA = Ch(sem("c_a"))
    CP = Ch(sem("c_p"))
    CT = Ch(sem("c_t"), serialize=False)

    s_cst = sem("s_cst")
    s_cst2 = sem("s_cst2")
    s_w0 = sem("s_w0")
    s_xs = [sem(f"s_xs{i}") for i in range(3)]
    s_zds = [sem(f"s_zd{i}") for i in range(2)]
    s_zfm = sem("s_zfm")
    s_qksp = sem("s_qksp")
    s_vsp = sem("s_vsp")
    s_bar1 = sem("s_bar1")
    s_out = sem("s_out")
    s_dbg = sem("s_dbg")
    s_kq = [sem(f"s_kq{i}") for i in range(3)]
    s_vv = [sem(f"s_vv{i}") for i in range(2)]
    s_wp = sem("s_wp")
    s_wpb = sem("s_wpb")
    s_gu2 = [sem(f"s_gu2{i}") for i in range(2)]
    s_wq = sem("s_wq")
    s_wt = sem("s_wt")
    s_wt2 = sem("s_wt2")
    s_y1s = [sem(f"s_y1s{i}") for i in range(2)]
    s_y1t = sem("s_y1t")
    s_gu = [sem(f"s_gu{i}") for i in range(2)]
    s_pqs = sem("s_pqs")
    s_wdt = sem("s_wdt")
    s_pq2 = [sem(f"s_pq2{i}") for i in range(2)]
    s_y1l = [sem(f"s_y1l{i}") for i in range(2)]

    sbE = lambda n, sh, dt: es.enter_context(nc.sbuf_tensor(n, sh, dt))
    onesr = sbE("onesr", [1, 128], F32)
    onesc = sbE("onesc", [128, 1], F32)
    onesb = sbE("onesb", [128, 1], BF16)
    idb = sbE("idb_sb", [128, 128], BF16)
    idf = sbE("idf_sb", [128, 128], F32)
    cbias = sbE("cbias", [128, 10], F32)
    swcols = sbE("swcols", [128, 4], F32)
    binv = sbE("binv", [128, 4], F32)
    rsx = sbE("rsx", [128, NTO], F32)
    rs2 = sbE("rs2", [128, NTO], F32)
    rsg = sbE("rsg", [128, NTO], F32)
    rsu = sbE("rsu", [128, NTO], F32)
    rs2w = sbE("rs2w", [128, NTO], F32)
    st = sbE("st", [128, 14], F32)
    pA = contextlib.ExitStack()
    sres = pA.enter_context(nc.sbuf_tensor("sres_sb", [128, NTO * D], F32))
    p1 = contextlib.ExitStack()
    sbt = lambda n, sh, dt: p1.enter_context(nc.sbuf_tensor(n, sh, dt))
    pst = lambda n, sh: p1.enter_context(nc.psum_tensor(n, sh, F32))
    gb = sbt("gb", [128, D], F32)
    bbt = sbt("bbt", [128, D], F32)
    wslot = sbt("wslot", [128, 8 * D], BF16)
    xr = sbt("xr", [128, 3 * D], F32)
    xnr = sbt("xnr", [128, 2 * D], F32)
    sqjunk = sbt("sqjunk", [128, D], F32)
    zf32 = sbt("zf32", [128, D], F32)
    zg32 = sbt("zg32", [128, D], F32)
    zfm = sbt("zfm", [128, 8 * S], BF16)
    qkstage = sbt("qkstage", [128, S], BF16)
    vstage = sbt("vstage", [128, 4 * 1040], BF16)
    ps = [pst(f"ps{i}", [128, 512]) for i in range(8)]
    ps_qk, ps_v = ps[:4], ps[4:]

    with nc.Block() as blk:
        @blk.sync
        def _(sy):
            sy.dma_start(out=gb[0:1, :], in_=lnsc_d).then_inc(s_cst, 16)
            sy.dma_start(out=bbt[0:1, :], in_=lnbi_d).then_inc(s_cst, 16)
            sy.dma_start(out=idb[:], in_=idb_d).then_inc(s_cst2, 16)
            sy.dma_start(out=idf[:], in_=idf_d).then_inc(s_cst2, 16)

        @blk.gpsimd
        def _(g):
            for ic in range(8):
                g.dma_start(out=wslot[:, ic * D:(ic + 1) * D],
                            in_=wq_d[ic * 128:(ic + 1) * 128, :]).then_inc(s_w0, 16)

        @blk.vector
        def _(vv):
            CV(vv.memset(onesr[:], 1.0))
            CV(vv.memset(onesc[:], 1.0))
            CV(vv.memset(onesb[:], 1.0))
            CV(vv.memset(cbias[:, 0:1], 1e-6))
            CV(vv.memset(cbias[:, 1:2], 1e-5))
            CV(vv.memset(cbias[:, 2:3], -257.0))
            CV(vv.memset(cbias[:, 3:4], -256.0))
            CV(vv.memset(cbias[:, 4:5], 1e-4))
            CV(vv.memset(cbias[:, 5:6], -256.5))
            CV(vv.memset(cbias[:, 6:7], -1.0))
            CV(vv.memset(cbias[:, 7:8], -0.5))
            CV(vv.memset(cbias[:, 8:9], 0.5))
            for sl in range(4):
                CV(vv.memset(vstage[:, sl * 1040:(sl + 1) * 1040].rearrange(
                    "p (h c) -> p h c", h=16)[:, :, 64:65], 1.0))
        m_consts = CV.n

        @blk.scalar
        def _(sc):
            CA(load_act_set(sc, nc, SET_LNEXP))

        @blk.tensor
        def _(te):
            te.wait_ge(s_cst, 32)
            te.wait_ge(CV.sem, m_consts)
            te.matmul(ps[0][:], onesr[:], gb[0:1, 0:512], start=True, stop=True)
            te.matmul(ps[1][:], onesr[:], gb[0:1, 512:1024], start=True, stop=True)
            te.matmul(ps[2][:], onesr[:], bbt[0:1, 0:512], start=True, stop=True)
            CT(te.matmul(ps[3][:], onesr[:], bbt[0:1, 512:1024], start=True, stop=True))
        m_bc = CT.n

        @blk.vector
        def _(vv):
            vv.wait_ge(CT.sem, m_bc)
            CV(vv.tensor_copy(gb[:, 0:512], ps[0][:]))
            CV(vv.tensor_copy(gb[:, 512:1024], ps[1][:]))
            CV(vv.tensor_copy(bbt[:, 0:512], ps[2][:]))
            CV(vv.tensor_copy(bbt[:, 512:1024], ps[3][:]))
        m_bce = CV.n

        m_xn = [0] * NT
        m_xuse = [0] * NT
        m_sq = [0] * NT
        m_var = [0] * NT
        m_rstd = [0] * NT
        for i in range(NT):
            r, e = i % 3, i % 2

            @blk.sync
            def _(sy, i=i, r=r):
                if i >= 3:
                    sy.wait_ge(CV.sem, m_xuse[i - 3])
                    sy.wait_ge(CA.sem, m_sq[i - 3])
                sy.dma_start(out=xr[:, r * D:(r + 1) * D],
                             in_=x_d[i * 128:(i + 1) * 128, :]).then_inc(s_xs[r], 16)

            @blk.vector
            def _(vv, i=i, r=r, e=e):
                xt = xr[:, r * D:(r + 1) * D]
                vv.wait_ge(s_xs[r], 16 * (i // 3 + 1))
                CV(vv.tensor_scalar(zf32[:], xt, 1.0, None, ALU.mult, ALU.add,
                                    accum_out=st[:, e:e + 1]))

            @blk.scalar
            def _(sc, i=i, r=r, e=e):
                sc.wait_ge(s_xs[r], 16 * (i // 3 + 1))
                if i >= 2:
                    sc.wait_ge(CV.sem, m_var[i - 2])
                m_sq[i] = CA(sc.activation(sqjunk[:], xr[:, r * D:(r + 1) * D],
                                           AF.Square, bias=0.0, scale=1.0,
                                           accum_out=st[:, 2 + e:3 + e]))

            @blk.vector
            def _(vv, i=i, e=e):
                vv.wait_ge(CA.sem, m_sq[i])
                if i >= 1:
                    vv.wait_ge(CA.sem, m_rstd[i - 1])
                CV(vv.tensor_scalar(st[:, 10:11], st[:, 2 + e:3 + e], 1.0 / D, None,
                                    ALU.mult))
                CV(vv.tensor_scalar(st[:, 4 + e:5 + e], st[:, e:e + 1], 1.0 / D, None,
                                    ALU.mult))
                CV(vv.tensor_tensor(st[:, 11:12], st[:, 4 + e:5 + e],
                                    st[:, 4 + e:5 + e], ALU.mult))
                m_var[i] = CV(vv.tensor_tensor(st[:, 8 + e:9 + e], st[:, 10:11],
                                               st[:, 11:12], ALU.subtract))

            @blk.scalar
            def _(sc, i=i, e=e):
                sc.wait_ge(CV.sem, m_var[i])
                if i >= 2:
                    sc.wait_ge(CV.sem, m_xuse[i - 2])
                CA(sc.activation(st[:, 12:13], st[:, 8 + e:9 + e], AF.Ln,
                                 bias=cbias[:, 0:1], scale=1.0))
                m_rstd[i] = CA(sc.activation(st[:, 6 + e:7 + e], st[:, 12:13], AF.Exp,
                                             bias=0.0, scale=-0.5))

            @blk.vector
            def _(vv, i=i, r=r, e=e):
                xt = xr[:, r * D:(r + 1) * D]
                xnt = xnr[:, (i % 2) * D:(i % 2 + 1) * D]
                vv.wait_ge(CA.sem, m_rstd[i])
                CV(vv.tensor_scalar(zf32[:], xt, st[:, 4 + e:5 + e],
                                    st[:, 6 + e:7 + e], ALU.subtract, ALU.mult))
                CV(vv.tensor_tensor(zg32[:], zf32[:], gb[:], ALU.mult))
                if i >= 2:
                    vv.wait_ge(s_zds[i % 2], 16 * (i // 2))
                m_xn[i] = CV(vv.tensor_tensor(xnt, zg32[:], bbt[:], ALU.add))
                if i < NTO:
                    CV(vv.tensor_tensor(sres[:, i * D:(i + 1) * D], xt, xnt, ALU.add))
                m_xuse[i] = CV.n

            @blk.gpsimd
            def _(g, i=i):
                g.wait_ge(CV.sem, m_xn[i])
                g.dma_start(out=z_dram[i * 128:(i + 1) * 128, :],
                            in_=xnr[:, (i % 2) * D:(i % 2 + 1) * D]
                            ).then_inc(s_zds[i % 2], 16)

        if sres_dbg is not None:
            @blk.gpsimd
            def _(g):
                g.wait_ge(CV.sem, m_xuse[NT - 1])
                for i in range(NTO):
                    g.dma_start(out=sres_dbg[i * 128:(i + 1) * 128, :],
                                in_=sres[:, i * D:(i + 1) * D]).then_inc(s_dbg, 16)
                    n_dbg[0] += 1

        @blk.sync
        def _(sy):
            sy.wait_ge(s_zds[0], 16 * (NT // 2))
            sy.wait_ge(s_zds[1], 16 * (NT // 2))
            for ic in range(8):
                sy.dma_start_transpose(zfm[:, ic * S:(ic + 1) * S],
                                       z_dram[:, ic * 128:(ic + 1) * 128]
                                       ).then_inc(s_zfm, 16)

        qk_jobs = [("q", oc, col) for oc in range(8) for col in range(4)] + \
                  [("k", oc, col) for oc in range(8) for col in range(8)]
        v_jobs = [(tt, nb) for tt in range(NT) for nb in range(2)]
        m_mmqk = [0] * len(qk_jobs)
        m_evqk = [0] * len(qk_jobs)
        m_mmv = [0] * len(v_jobs)
        m_evv = [0] * len(v_jobs)

        @blk.gpsimd
        def _(g):
            pass

        mark_q_done = [0]
        for j, (kind, oc, col) in enumerate(qk_jobs):
            @blk.tensor
            def _(te, j=j, kind=kind, oc=oc, col=col):
                if j == 0:
                    te.wait_ge(s_zfm, 16 * 8)
                    te.wait_ge(CV.sem, m_bce)
                te.wait_ge(s_w0, 16 * 8 * (1 if kind == "q" else 2))
                if j >= 4:
                    te.wait_ge(CA.sem, m_evqk[j - 4])
                p = ps_qk[j % 4]
                for ic in range(8):
                    lhsT = wslot[:, ic * D + oc * 128:ic * D + (oc + 1) * 128]
                    rhs = zfm[:, ic * S + col * 512:ic * S + (col + 1) * 512]
                    mm = te.matmul(p[:], lhsT, rhs, start=(ic == 0), stop=(ic == 7))
                m_mmqk[j] = CT(mm)

            if kind == "q" and j == 31:
                mark_q_done[0] = m_mmqk[j]

                @blk.gpsimd
                def _(g, mk=m_mmqk[j]):
                    g.wait_ge(CT.sem, mk)
                    for ic in range(8):
                        g.dma_start(out=wslot[:, ic * D:(ic + 1) * D],
                                    in_=wk_d[ic * 128:(ic + 1) * 128, :]
                                    ).then_inc(s_w0, 16)

            @blk.scalar
            def _(sc, j=j, kind=kind, oc=oc, col=col):
                sc.wait_ge(CT.sem, m_mmqk[j])
                if col == 0:
                    nprev = oc if kind == "q" else 8 + oc
                    if nprev:
                        sc.wait_ge(s_qksp, 16 * nprev)
                m_evqk[j] = CA(sc.copy(qkstage[:, col * 512:(col + 1) * 512],
                                       ps_qk[j % 4][:]))

            last_col = 3 if kind == "q" else 7
            if col == last_col:
                @blk.sync
                def _(sy, j=j, kind=kind, oc=oc):
                    sy.wait_ge(CA.sem, m_evqk[j])
                    dst = q_sp[oc] if kind == "q" else k_sp[oc]
                    src = qkstage[:, 0:SO] if kind == "q" else qkstage[:, 0:S]
                    sy.dma_start(out=dst, in_=src).then_inc(s_qksp, 16)

        @blk.gpsimd
        def _(g, mk=m_mmqk[-1]):
            g.wait_ge(CT.sem, mk)
            for ic in range(8):
                g.dma_start(out=wslot[:, ic * D:(ic + 1) * D],
                            in_=wv_d[ic * 128:(ic + 1) * 128, :]).then_inc(s_w0, 16)

        for jv, (tt, nb) in enumerate(v_jobs):
            @blk.tensor
            def _(te, jv=jv, tt=tt, nb=nb):
                if jv == 0:
                    te.wait_ge(s_w0, 16 * 24)
                if jv >= 4:
                    te.wait_ge(CV.sem, m_evv[jv - 4])
                p = ps_v[jv % 4]
                for ic in range(8):
                    lhsT = zfm[:, ic * S + tt * 128:ic * S + (tt + 1) * 128]
                    rhs = wslot[:, ic * D + nb * 512:ic * D + (nb + 1) * 512]
                    mm = te.matmul(p[:], lhsT, rhs, start=(ic == 0), stop=(ic == 7))
                m_mmv[jv] = CT(mm)

            @blk.vector
            def _(vv, jv=jv, tt=tt, nb=nb):
                vv.wait_ge(CT.sem, m_mmv[jv])
                if tt >= 4 and nb == 0:
                    vv.wait_ge(s_vsp, 16 * 8 * (tt // 4))
                base = (tt % 4) * 1040 + nb * 520
                dst = vstage[:, base:base + 520].rearrange(
                    "p (h c) -> p h c", h=8)[:, :, 0:64]
                src = ps_v[jv % 4][:].rearrange("p (h c) -> p h c", h=8)
                m_evv[jv] = CV(vv.tensor_copy(dst, src))

            if nb == 1 and tt % 4 == 3:
                @blk.sync
                def _(sy, jv=jv, tt=tt):
                    sy.wait_ge(CV.sem, m_evv[jv])
                    grp = tt // 4
                    for hp in range(NHP):
                        src = vstage[:].rearrange("p (t c) -> p t c", t=4)[
                            :, :, hp * 130:(hp + 1) * 130]
                        dst = v_sp[hp][:, grp * 4 * 130:(grp + 1) * 4 * 130]
                        sy.dma_start(out=dst, in_=src).then_inc(s_vsp, 16)

        @blk.sync
        def _(sy):
            sy.wait_ge(s_qksp, 16 * 16)
            sy.wait_ge(s_vsp, 16 * 64)
            sy.nop().then_inc(s_bar1, 1)

        @blk.gpsimd
        def _(g):
            CP.gate(g.nop()).then_inc(s_bar1, 1)

        @blk.vector
        def _(vv):
            CV.gate(vv.nop()).then_inc(s_bar1, 1)

        @blk.scalar
        def _(sc):
            CA.gate(sc.nop()).then_inc(s_bar1, 1)

        @blk.tensor
        def _(te):
            CT.gate(te.nop()).then_inc(s_bar1, 1)

    p1.close()

    p2a = contextlib.ExitStack()
    attn_sb = p2a.enter_context(nc.sbuf_tensor("attn_sb", [128, NTO * D], BF16))
    p2 = contextlib.ExitStack()
    sb2 = lambda n, sh, dt: p2.enter_context(nc.sbuf_tensor(n, sh, dt))
    ps2 = lambda n, sh: p2.enter_context(nc.psum_tensor(n, sh, F32))
    kslot = sb2("kslot", [128, 3 * S], BF16)
    vslot = sb2("vslot", [128, 2 * KC * 130], BF16)
    qslot = sb2("qslot", [128, 3 * SO], BF16)
    pt = sb2("pt", [128, 4 * 512], BF16)
    sqbuf = sb2("sqbuf", [128, S], F32)
    csq2 = sb2("csq2", [128, SO], BF16)
    pvf = sb2("pvf", [128, 2 * 512], F32)
    rlb = sb2("rlb", [128, 2 * 4], F32)
    cbc = sb2("cbc", [128, 2], F32)
    srow = sb2("srow", [1, 8], F32)
    acc3 = sb2("acc3", [128, 4], F32)
    wbuf = sb2("wbuf", [128, 2048], F32)
    abuf = sb2("abuf", [128, 2048], BF16)
    wib = sb2("wib", [128, 2048], BF16)
    tstage = sb2("tstage", [128, S], BF16)
    sTp = [ps2(f"sT{s}", [128, 1024]) for s in range(3)]
    sT1 = sTp[0]
    pvps = [ps2(f"pvps{h}", [128, 512]) for h in range(2)]

    with nc.Block() as blk:
        @blk.sync
        def _(sy):
            sy.wait_ge(s_bar1, 5)

        @blk.vector
        def _(vv):
            vv.wait_ge(s_bar1, 5)
            CV(vv.memset(acc3[:], 0.0))

        @blk.scalar
        def _(sc):
            sc.wait_ge(s_bar1, 5)

        @blk.tensor
        def _(te):
            te.wait_ge(s_bar1, 5)
            te.wait_ge(s_cst2, 32)

        @blk.gpsimd
        def _(g):
            g.wait_ge(s_bar1, 5)

        wsrcs = [wg_d, wu_d, wd_d]
        wnats = [wg_nat, wu_nat, wd_nat]
        NW = 4194304.0
        n_wp = [0]
        n_wq = [0]
        n_wt = [0]
        n_wt2 = [0]
        m_wbuf = [0]
        m_wib = [0]

        m_wslot = [0, 0]
        n_piece = [0]
        n_wpb = [0]

        def wchunk_ap(wd_ap, c):
            flat = wd_ap.rearrange("r d -> (r d)")
            return flat[c * 131072:(c + 1) * 131072].rearrange("(p v) -> p v", v=1024)

        def wsl(sl):
            return wbuf[:, sl * 1024:(sl + 1) * 1024]

        def wibsl(sl):
            return wib[:, sl * 1024:(sl + 1) * 1024]

        def emit_jobA(w, c):
            sl = n_piece[0] % 2
            n_piece[0] += 1

            @blk.gpsimd
            def _(g, w=w, c=c, sl=sl, gate=m_wslot[sl]):
                g.wait_ge(CV.sem, gate)
                g.dma_start(out=wsl(sl), in_=wchunk_ap(wsrcs[w], c)).then_inc(s_wp, 16)
            n_wp[0] += 1

            @blk.vector
            def _(vv, w=w, c=c, sl=sl, nwp=n_wp[0]):
                vv.wait_ge(s_wp, 16 * nwp)
                CV(vv.tensor_reduce(acc3[:, 3:4], wsl(sl), AX, ALU.add,
                                    apply_absolute_value=True))
                CV(vv.tensor_tensor(acc3[:, w:w + 1], acc3[:, w:w + 1], acc3[:, 3:4],
                                    ALU.add))
                m_wslot[sl] = CV.n

        def emit_jobF(w, mark_bank_free):
            @blk.tensor
            def _(te, w=w):
                te.wait_ge(CV.sem, max(m_wslot[0], m_wslot[1], mark_bank_free[0]))
                CT(te.matmul(sT1[0:1, 0:1], acc3[:, w:w + 1], onesc[:], start=True,
                             stop=True))
            mk = CT.n

            @blk.vector
            def _(vv, w=w, mk=mk):
                vv.wait_ge(CT.sem, mk)
                CV(vv.tensor_scalar(srow[0:1, 3:4], sT1[0:1, 0:1], 1.0 / NW, None, ALU.mult))
                CV(vv.tensor_scalar(srow[0:1, 3:4], srow[0:1, 3:4], 1e-5, None, ALU.max))
                CV(vv.reciprocal(srow[0:1, w:w + 1], srow[0:1, 3:4]))
                CV(vv.reciprocal(srow[0:1, 4 + w:5 + w], srow[0:1, w:w + 1]))
            mkd = CV.n

            @blk.tensor
            def _(te, w=w, mkd=mkd):
                te.wait_ge(CV.sem, mkd)
                te.matmul(sT1[:, 0:1], onesr[:], srow[0:1, w:w + 1], start=True, stop=True)
                CT(te.matmul(sT1[:, 1:2], onesr[:], srow[0:1, 4 + w:5 + w], start=True,
                             stop=True))
            mk2 = CT.n

            @blk.vector
            def _(vv, w=w, mk2=mk2):
                vv.wait_ge(CT.sem, mk2)
                CV(vv.tensor_copy(swcols[:, w:w + 1], sT1[:, 0:1]))
                CV(vv.tensor_copy(binv[:, w:w + 1], sT1[:, 1:2]))
            mark_bank_free[0] = CV.n

        def emit_jobB(w, c):
            sl = n_piece[0] % 2
            n_piece[0] += 1

            @blk.sync
            def _(sy, w=w, c=c, sl=sl, gate=m_wslot[sl]):
                sy.wait_ge(CV.sem, gate)
                sy.dma_start(out=wsl(sl), in_=wchunk_ap(wsrcs[w], c)
                             ).then_inc(s_wpb, 16)
            n_wpb[0] += 1

            @blk.vector
            def _(vv, w=w, c=c, sl=sl, nwp=n_wpb[0], nwq=n_wq[0]):
                vv.wait_ge(s_wpb, 16 * nwp)
                CV(vv.tensor_scalar(wsl(sl), wsl(sl), swcols[:, w:w + 1], None,
                                    ALU.mult))
                if nwq >= 1:
                    vv.wait_ge(s_wq, 16 * (nwq - 1))
                CV(vv.tensor_scalar(wibsl(sl), wsl(sl), 0.5, None, ALU.is_ge))
                CV(vv.tensor_scalar(abuf[:, 0:1024], wsl(sl), -0.5, None, ALU.is_le))
                m_wslot[sl] = CV.n
                m_wib[0] = CV(vv.tensor_tensor(wibsl(sl), wibsl(sl), abuf[:, 0:1024],
                                               ALU.subtract))

            @blk.gpsimd
            def _(g, w=w, c=c, sl=sl, mwib=m_wib):
                g.wait_ge(CV.sem, mwib[0])
                g.dma_start(out=wchunk_ap(wnats[w], c), in_=wibsl(sl)).then_inc(s_wq, 16)
            n_wq[0] += 1

        def emit_jobD(g4):
            @blk.sync
            def _(sy, g4=g4, nw2=None):
                sy.wait_ge(s_wq, 16 * 32 * 3)
                sy.wait_ge(s_wt2, 16 * n_wt2[0])
                for sub in range(4):
                    ic = g4 * 4 + sub
                    sy.dma_start_transpose(tstage[:, sub * D:(sub + 1) * D],
                                           wd_nat[:, ic * 128:(ic + 1) * 128]
                                           ).then_inc(s_wt, 16)
            n_wt[0] += 4

            @blk.sync
            def _(sy, g4=g4, nwt=n_wt[0]):
                sy.wait_ge(s_wt, 16 * nwt)
                sy.dma_start(out=wdsp_d[g4], in_=tstage[:, 0:4 * D]
                             ).then_inc(s_wt2, 16)
            n_wt2[0] += 1

        def emit_jobC(w, ic):
            @blk.sync
            def _(sy, w=w, ic=ic):
                sy.wait_ge(s_wq, 16 * 32 * (w + 1))
                sy.wait_ge(s_wt2, 16 * n_wt2[0])
                sy.dma_start_transpose(tstage[:], wnats[w][:, ic * 128:(ic + 1) * 128]
                                       ).then_inc(s_wt, 16)
            n_wt[0] += 1

            @blk.sync
            def _(sy, w=w, ic=ic, nwt=n_wt[0]):
                sy.wait_ge(s_wt, 16 * nwt)
                for span in range(8):
                    sy.dma_start(out=wgsp_d[w, span][:, ic * 512:(ic + 1) * 512],
                                 in_=tstage[:, span * 512:(span + 1) * 512]
                                 ).then_inc(s_wt2, 16)
            n_wt2[0] += 8

        jobs = [("A", w, c) for w in range(3) for c in range(32)] + \
               [("B", w, c) for w in range(3) for c in range(32)] + \
               [("C", w, ic) for w in range(2) for ic in range(8)] + \
               [("D", g4, 0) for g4 in range(8)]
        jq = list(jobs)
        f_done = [False]
        bankfree_mark = [0]

        def pop_jobs(k):
            for _ in range(k):
                if not jq:
                    return
                kind, w, c = jq[0]
                if kind == "B" and not f_done[0]:
                    return
                jq.pop(0)
                if kind == "A":
                    emit_jobA(w, c)
                elif kind == "B":
                    emit_jobB(w, c)
                elif kind == "C":
                    emit_jobC(w, c)
                else:
                    emit_jobD(w)

        m_hp_last = [0] * NHP
        last_exp = [0]
        last_exp_set = {}
        last_pv = {}
        m_atm_prev = [0, 0]
        m_tp_prev = [0, 0]
        m_cnorm_read = [0]
        m_cmm_prev = [0]
        m_csq = {}
        m_cb_cur = [0]

        m_cpc = [0]

        def emit_c_piece(hpx, j, s):
            @blk.tensor
            def _(te, hpx=hpx, j=j, s=s):
                te.wait_ge(CV.sem, m_csq[hpx])
                if j < 2:
                    te.wait_ge(CV.sem, m_cnorm_read[0])
                if j < 8:
                    mmi = te.matmul(sTp[s][0:1, 0:512], onesc[:],
                                    sqbuf[:, j * 512:(j + 1) * 512],
                                    start=True, stop=True)
                else:
                    mmi = te.matmul(sTp[s][0:1, 0:512], onesb[:],
                                    csq2[:, (j - 8) * 512:(j - 7) * 512],
                                    start=True, stop=True)
                CT(mmi)
            mpc = CT.n
            m_cmm_prev[0] = mpc

            @blk.vector
            def _(vv, hpx=hpx, j=j, s=s, mpc=mpc):
                vv.wait_ge(CT.sem, mpc)
                if j == 0:
                    CV(vv.tensor_reduce(srow[0:1, 0:1], sTp[s][0:1, 0:512], AX,
                                        ALU.max))
                elif j == 8:
                    CV(vv.tensor_reduce(srow[0:1, 1:2], sTp[s][0:1, 0:512], AX,
                                        ALU.max))
                else:
                    d = 0 if j < 8 else 1
                    CV(vv.tensor_reduce(srow[0:1, 2:3], sTp[s][0:1, 0:512], AX,
                                        ALU.max))
                    CV(vv.tensor_tensor(srow[0:1, d:d + 1], srow[0:1, d:d + 1],
                                        srow[0:1, 2:3], ALU.max))
            m_cpc[0] = CV.n

        def emit_c_final(hpx):
            @blk.vector
            def _(vv, hpx=hpx):
                CV(vv.tensor_tensor(srow[0:1, 5:6], srow[0:1, 0:1], srow[0:1, 1:2],
                                    ALU.mult))
                m_cnorm_read[0] = CV.n

            @blk.scalar
            def _(sc, hpx=hpx):
                sc.wait_ge(CV.sem, m_cnorm_read[0])
                CA(sc.activation(srow[0:1, 6:7], srow[0:1, 5:6], AF.Ln, bias=0.0,
                                 scale=1.0))
                CA(sc.activation(srow[0:1, 7:8], srow[0:1, 6:7], AF.Exp,
                                 bias=cbias[0:1, 4:5], scale=0.5))
            m_cval = CA.n

            @blk.vector
            def _(vv, hpx=hpx, m_cval=m_cval):
                vv.wait_ge(CA.sem, m_cval)
                CV(vv.tensor_scalar(srow[0:1, 6:7], srow[0:1, 7:8], -1.0 / 32.0,
                                    None, ALU.mult))
            m_negc = CV.n

            @blk.tensor
            def _(te, hpx=hpx, m_negc=m_negc):
                te.wait_ge(CV.sem, m_negc)
                CT(te.matmul(sT1[:, 0:1], onesr[:], srow[0:1, 6:7], start=True,
                             stop=True))
            m_cbmm = CT.n

            @blk.vector
            def _(vv, hpx=hpx, m_cbmm=m_cbmm):
                vv.wait_ge(CT.sem, m_cbmm)
                CV(vv.tensor_copy(cbc[:, hpx % 2:hpx % 2 + 1], sT1[:, 0:1]))
            m_cnorm_read[0] = CV.n
            return CV.n

        def emit_c_squares(hpx):
            par = hpx % 3
            kS_ = kslot[:, par * S:(par + 1) * S]
            qS_ = qslot[:, par * SO:(par + 1) * SO]

            @blk.vector
            def _(vv, hpx=hpx, par=par, kS_=kS_, qS_=qS_):
                vv.wait_ge(s_kq[par], 32 * (hpx // 3 + 1))
                if m_cmm_prev[0]:
                    vv.wait_ge(CT.sem, m_cmm_prev[0])
                CV(vv.tensor_tensor(sqbuf[:], kS_, kS_, ALU.mult))
                CV(vv.tensor_tensor(csq2[:], qS_, qS_, ALU.mult))
            m_csq[hpx] = CV.n

        def emit_c_finish(hpx):
            csl = [sTp[s][0:1, c * 512:(c + 1) * 512]
                   for s in range(3) for c in range(2)]

            @blk.tensor
            def _(te, hpx=hpx):
                te.wait_ge(CV.sem, m_csq[hpx])
                if last_exp[0]:
                    te.wait_ge(CA.sem, last_exp[0])
                if m_cnorm_read[0]:
                    te.wait_ge(CV.sem, m_cnorm_read[0])
                for pc in range(6):
                    mmi = te.matmul(csl[pc], onesc[:],
                                    sqbuf[:, pc * 512:(pc + 1) * 512],
                                    start=True, stop=True)
                CT(mmi)
            m1 = CT.n

            @blk.vector
            def _(vv, hpx=hpx, m1=m1):
                vv.wait_ge(CT.sem, m1)
                CV(vv.tensor_reduce(srow[0:1, 0:1], csl[0][0:1, :], AX, ALU.max))
                for pc in range(1, 6):
                    CV(vv.tensor_reduce(srow[0:1, 2:3], csl[pc][0:1, :], AX, ALU.max))
                    CV(vv.tensor_tensor(srow[0:1, 0:1], srow[0:1, 0:1],
                                        srow[0:1, 2:3], ALU.max))
            r1 = CV.n

            @blk.tensor
            def _(te, hpx=hpx, r1=r1):
                te.wait_ge(CV.sem, r1)
                te.matmul(csl[0], onesc[:], sqbuf[:, 6 * 512:7 * 512],
                          start=True, stop=True)
                te.matmul(csl[1], onesc[:], sqbuf[:, 7 * 512:8 * 512],
                          start=True, stop=True)
                for pc in range(4):
                    mmi = te.matmul(csl[2 + pc], onesb[:],
                                    csq2[:, pc * 512:(pc + 1) * 512],
                                    start=True, stop=True)
                CT(mmi)
            m2 = CT.n
            m_cmm_prev[0] = m2

            @blk.vector
            def _(vv, hpx=hpx, m2=m2):
                vv.wait_ge(CT.sem, m2)
                for pc in range(2):
                    CV(vv.tensor_reduce(srow[0:1, 2:3], csl[pc][0:1, :], AX, ALU.max))
                    CV(vv.tensor_tensor(srow[0:1, 0:1], srow[0:1, 0:1],
                                        srow[0:1, 2:3], ALU.max))
                CV(vv.tensor_reduce(srow[0:1, 1:2], csl[2][0:1, :], AX, ALU.max))
                for pc in range(1, 4):
                    CV(vv.tensor_reduce(srow[0:1, 2:3], csl[2 + pc][0:1, :], AX,
                                        ALU.max))
                    CV(vv.tensor_tensor(srow[0:1, 1:2], srow[0:1, 1:2],
                                        srow[0:1, 2:3], ALU.max))
                CV(vv.tensor_tensor(srow[0:1, 5:6], srow[0:1, 0:1], srow[0:1, 1:2],
                                    ALU.mult))
                m_cnorm_read[0] = CV.n

            @blk.scalar
            def _(sc, hpx=hpx):
                sc.wait_ge(CV.sem, m_cnorm_read[0])
                CA(sc.activation(srow[0:1, 6:7], srow[0:1, 5:6], AF.Ln, bias=0.0,
                                 scale=1.0))
                CA(sc.activation(srow[0:1, 7:8], srow[0:1, 6:7], AF.Exp,
                                 bias=cbias[0:1, 4:5], scale=0.5))
            m_cval = CA.n

            @blk.vector
            def _(vv, hpx=hpx, m_cval=m_cval):
                vv.wait_ge(CA.sem, m_cval)
                CV(vv.tensor_scalar(srow[0:1, 6:7], srow[0:1, 7:8], -1.0 / 32.0,
                                    None, ALU.mult))
            m_negc = CV.n

            @blk.tensor
            def _(te, hpx=hpx, m_negc=m_negc):
                te.wait_ge(CV.sem, m_negc)
                CT(te.matmul(sT1[:, 0:1], onesr[:], srow[0:1, 6:7], start=True,
                             stop=True))
            m_cbmm = CT.n

            @blk.vector
            def _(vv, hpx=hpx, m_cbmm=m_cbmm):
                vv.wait_ge(CT.sem, m_cbmm)
                CV(vv.tensor_copy(cbc[:, hpx % 2:hpx % 2 + 1], sT1[:, 0:1]))
            m_cnorm_read[0] = CV.n
            return CV.n

        pending_epi = [None]

        def emit_epi(hp_e, qb_e, m_pvg_e):
            for h in range(2):
                @blk.vector
                def _(vv, hp=hp_e, qb=qb_e, h=h, m_pvg=m_pvg_e):
                    vv.wait_ge(CT.sem, m_pvg[(len(GRP) - 1, h)])
                    if m_tp_prev[h]:
                        vv.wait_ge(CT.sem, m_tp_prev[h])
                    CV(vv.tensor_copy(pvf[:, h * 512:(h + 1) * 512][0:65, :],
                                      pvps[h][0:65, :]))
                m_pvf = CV.n

                @blk.tensor
                def _(te, hp=hp_e, qb=qb_e, h=h, m_pvf=m_pvf):
                    te.wait_ge(CV.sem, m_pvf)
                    te.wait_ge(CV.sem, m_atm_prev[h])
                    pf = pvf[:, h * 512:(h + 1) * 512]
                    for j in range(4):
                        te.transpose(pvps[h][:, j * 64:(j + 1) * 64],
                                     pf[0:64, j * 128:(j + 1) * 128],
                                     idf[0:64, 0:64])
                    for j in range(4):
                        mmi = te.transpose(pvps[h][:, 256 + j:257 + j],
                                           pf[64:65, j * 128:(j + 1) * 128],
                                           idf[64:65, 64:65])
                    m_tp_prev[h] = CT(mmi)

                @blk.vector
                def _(vv, hp=hp_e, qb=qb_e, h=h):
                    vv.wait_ge(CT.sem, m_tp_prev[h])
                    CV(vv.reciprocal(rlb[:, h * 4:(h + 1) * 4],
                                     pvps[h][:, 256:260]))
                    for j in range(4):
                        CV(vv.tensor_scalar(
                            attn_sb[:, (qb * 4 + j) * D + (2 * hp + h) * 64:
                                    (qb * 4 + j) * D + (2 * hp + h) * 64 + 64],
                            pvps[h][:, j * 64:(j + 1) * 64],
                            rlb[:, h * 4 + j:h * 4 + j + 1], None, ALU.mult))
                    m_atm_prev[h] = CV.n

        def flush_epi():
            if pending_epi[0] is not None:
                emit_epi(*pending_epi[0])
                pending_epi[0] = None

        def stream_kq(hp, gate_ct=0, gate_cv=0):
            par = hp % 3

            @blk.sync
            def _(sy, hp=hp, par=par, gate_ct=gate_ct, gate_cv=gate_cv):
                if gate_ct:
                    sy.wait_ge(CT.sem, gate_ct)
                if gate_cv:
                    sy.wait_ge(CV.sem, gate_cv)
                sy.dma_start(out=kslot[:, par * S:(par + 1) * S], in_=k_sp[hp]
                             ).then_inc(s_kq[par], 16)
                sy.dma_start(out=qslot[:, par * SO:(par + 1) * SO], in_=q_sp[hp]
                             ).then_inc(s_kq[par], 16)

        def stream_v(hp, gate_ct=0):
            par = hp % 2

            @blk.sync
            def _(sy, hp=hp, par=par, gate_ct=gate_ct):
                if gate_ct:
                    sy.wait_ge(CT.sem, gate_ct)
                sy.dma_start(out=vslot[:, par * KC * 130:(par + 1) * KC * 130],
                             in_=v_sp[hp]).then_inc(s_vv[par], 16)

        stream_kq(0)
        stream_v(0)
        stream_kq(1)
        stream_v(1)
        stream_kq(2)
        emit_c_squares(0)

        for hp in range(NHP):
            par = hp % 3
            kS = kslot[:, par * S:(par + 1) * S]
            vS = vslot[:, (hp % 2) * KC * 130:(hp % 2 + 1) * KC * 130]
            qS = qslot[:, par * SO:(par + 1) * SO]
            ecb = hp % 2

            if hp == 0:
                m_cb_cur[0] = emit_c_finish(0)
            m_qkgate = m_cb_cur[0]

            if hp == 4:
                bankfree_mark[0] = m_cb_cur[0]
                emit_jobF(0, bankfree_mark)
                emit_jobF(1, bankfree_mark)
                emit_jobF(2, bankfree_mark)
                f_done[0] = True
                m_cnorm_read[0] = max(m_cnorm_read[0], bankfree_mark[0])
                m_qkgate = max(m_qkgate, bankfree_mark[0])

            for qb in range(QB):
                m_exp = {}
                m_pvg = {}

                def emit_qk(gi, hp=hp, qb=qb, kS=kS, qS=qS, m_qkgate=m_qkgate,
                            m_exp=m_exp, extra_cv=0):
                    kc0, glen = GRP[gi]
                    s = gi % 3

                    @blk.tensor
                    def _(te, gi=gi, kc0=kc0, glen=glen, s=s, extra_cv=extra_cv):
                        if gi < 3:
                            pe = last_exp_set.get(s)
                            if pe:
                                te.wait_ge(CA.sem, pe)
                            te.wait_ge(CV.sem, m_qkgate)
                        else:
                            te.wait_ge(CA.sem, m_exp[gi - 3])
                        if extra_cv:
                            te.wait_ge(CV.sem, extra_cv)
                        kc = kc0
                        te.matmul(sTp[s][:, 0:512],
                                  kS[0:64, kc * 128:(kc + 1) * 128],
                                  qS[0:64, qb * 512:(qb + 1) * 512],
                                  start=True, stop=True, tile_position=(0, 0))
                        mmi = te.matmul(sTp[s][:, 512:1024],
                                        kS[64:128, kc * 128:(kc + 1) * 128],
                                        qS[64:128, qb * 512:(qb + 1) * 512],
                                        start=True, stop=True,
                                        tile_position=(64, 0))
                        CT(mmi)
                    return CT.n

                def emit_exp(gi, m_qk, hp=hp, qb=qb, ecb=ecb, m_exp=m_exp):
                    s = gi % 3
                    slot = gi % 2

                    @blk.scalar
                    def _(sc, gi=gi, s=s, slot=slot, m_qk=m_qk):
                        sc.wait_ge(CT.sem, m_qk)
                        prev = last_pv.get(slot)
                        if prev is not None:
                            sc.wait_ge(CT.sem, prev)
                        m_exp[gi] = CA(sc.activation(
                            pt[:, slot * 1024:(slot + 1) * 1024],
                            sTp[s][:], AF.Exp, bias=cbc[:, ecb:ecb + 1],
                            scale=1.0 / 32.0))
                        last_exp[0] = m_exp[gi]
                    last_exp_set[s] = m_exp[gi]

                def emit_pv(gi, hp=hp, qb=qb, vS=vS, m_exp=m_exp, m_pvg=m_pvg):
                    kc0, glen = GRP[gi]
                    slot = gi % 2
                    for h in range(2):
                        @blk.tensor
                        def _(te, gi=gi, h=h, kc0=kc0, slot=slot, hp=hp):
                            if h == 0:
                                te.wait_ge(CA.sem, m_exp[gi])
                            if gi == 0:
                                te.wait_ge(CV.sem, m_atm_prev[h])
                                if h == 0:
                                    te.wait_ge(s_vv[hp % 2], 16 * (hp // 2 + 1))
                            kc = kc0
                            lhsT = vS[:, kc * 130 + h * 65:kc * 130 + (h + 1) * 65]
                            rhs = pt[:, slot * 1024 + h * 512:
                                     slot * 1024 + (h + 1) * 512]
                            mmi = te.matmul(pvps[h][0:65, :], lhsT, rhs,
                                            start=(kc == 0), stop=(kc == KC - 1))
                            m_pvg[(gi, h)] = CT(mmi)
                    last_pv[slot] = m_pvg[(gi, 1)]

                NG = len(GRP)
                mq = {0: emit_qk(0), 1: emit_qk(1), 2: emit_qk(2)}
                emit_exp(0, mq[0])
                flush_epi()
                if qb == 1 and hp + 1 < NHP:
                    emit_c_squares(hp + 1)
                pop_jobs(1)
                for gi in range(NG):
                    emit_pv(gi)
                    extra_cv = 0
                    if qb == 3 and hp + 1 < NHP and gi >= 8 and gi % 2 == 0:
                        emit_c_piece(hp + 1, (gi - 8) // 2, gi % 3)
                        extra_cv = m_cpc[0]
                    if gi + 3 < NG:
                        mq[gi + 3] = emit_qk(gi + 3, extra_cv=extra_cv)
                    if gi + 1 < NG:
                        emit_exp(gi + 1, mq[gi + 1])
                    pop_gis = ((5, 10, 15, 20, 25, 29) if hp < 4 else
                               (4, 8, 12, 16, 20, 24, 28))
                    if gi in pop_gis:
                        pop_jobs(1)

                pending_epi[0] = (hp, qb, m_pvg)

            m_hp_last[hp] = CT.n
            if hp + 1 < NHP:
                m_cb_cur[0] = emit_c_final(hp + 1)
            if hp + 3 < NHP:
                stream_kq(hp + 3, gate_ct=m_hp_last[hp], gate_cv=m_csq[hp])
            if hp + 2 < NHP:
                stream_v(hp + 2, gate_ct=m_hp_last[hp])

        flush_epi()

        while jq:
            pop_jobs(1)

        hb = sqbuf[:, 0:D]
        y1f = sqbuf[:, D:2 * D]
        ub = sqbuf[:, 2 * D:3 * D]
        sqj3 = sqbuf[:, 3 * D:4 * D]
        kfb = wbuf[:, 0:D]
        tb_ = wbuf[:, D:2 * D]
        sg1 = abuf[:, 0:D]
        sg2 = qslot[:, 0:D]
        y1bf = tstage[:, 0:2 * D]
        y1qb = tstage[:, 2 * D:4 * D]
        kib = p2.enter_context(nc.sbuf_tensor("kib", [128, D], I32))
        n_y1s = [0]

        @blk.vector
        def _(vv):
            vv.wait_ge(s_wp, 16 * n_wp[0])
            vv.wait_ge(s_wq, 16 * n_wq[0])
            vv.wait_ge(s_wt, 16 * n_wt[0])
            vv.wait_ge(s_wt2, 16 * n_wt2[0])
        for i in range(NTO):
            e = i % 2

            @blk.vector
            def _(vv, i=i, e=e):
                if i == 0:
                    vv.wait_ge(CV.sem, m_atm_prev[0])
                    vv.wait_ge(CV.sem, m_atm_prev[1])
                CV(vv.tensor_copy(hb, attn_sb[:, i * D:(i + 1) * D]))
                CV(vv.tensor_tensor(hb, hb, sres[:, i * D:(i + 1) * D], ALU.add))
                CV(vv.tensor_scalar(y1f, hb, 1.0, None, ALU.mult, ALU.add,
                                    accum_out=st[:, 0:1]))
            m_h = CV.n

            @blk.scalar
            def _(sc, i=i, m_h=m_h):
                sc.wait_ge(CV.sem, m_h)
                CA(sc.activation(sqj3, hb, AF.Square, bias=0.0, scale=1.0,
                                 accum_out=st[:, 2:3]))
            m_sq3 = CA.n

            @blk.vector
            def _(vv, i=i, m_sq3=m_sq3):
                vv.wait_ge(CA.sem, m_sq3)
                CV(vv.tensor_scalar(st[:, 10:11], st[:, 2:3], 1.0 / D, None, ALU.mult))
                CV(vv.tensor_scalar(st[:, 4:5], st[:, 0:1], 1.0 / D, None, ALU.mult))
                CV(vv.tensor_tensor(st[:, 11:12], st[:, 4:5], st[:, 4:5], ALU.mult))
                CV(vv.tensor_tensor(st[:, 8:9], st[:, 10:11], st[:, 11:12], ALU.subtract))
            m_v3 = CV.n

            @blk.scalar
            def _(sc, i=i, m_v3=m_v3):
                sc.wait_ge(CV.sem, m_v3)
                CA(sc.activation(st[:, 12:13], st[:, 8:9], AF.Ln, bias=cbias[:, 1:2],
                                 scale=1.0))
                CA(sc.activation(st[:, 6:7], st[:, 12:13], AF.Exp, bias=0.0, scale=-0.5))
            m_r3 = CA.n

            @blk.vector
            def _(vv, i=i, e=e, m_r3=m_r3):
                vv.wait_ge(CA.sem, m_r3)
                CV(vv.tensor_scalar(y1f, hb, st[:, 4:5], st[:, 6:7],
                                    ALU.subtract, ALU.mult))
                if i >= 2:
                    vv.wait_ge(s_y1s[i % 2], 32 * (i // 2))
                CV(vv.tensor_copy(y1bf[:, e * D:(e + 1) * D], y1f))
                CV(vv.tensor_reduce(st[:, 13:14], y1f, AX, ALU.max,
                                    apply_absolute_value=True))
                CV(vv.tensor_scalar(st[:, 13:14], st[:, 13:14], 1e-5, None, ALU.max))
                CV(vv.reciprocal(st[:, 10:11], st[:, 13:14]))
                CV(vv.tensor_scalar(st[:, 11:12], st[:, 10:11], 127.0, None, ALU.mult))
                CV(vv.reciprocal(rsx[:, i:i + 1], st[:, 11:12]))
                CV(vv.tensor_scalar(ub, y1f, st[:, 11:12], MAGIC, ALU.mult,
                                    ALU.add))
                CV(vv.tensor_copy(kib[:], ub))
                CV(vv.tensor_scalar(kfb, kib[:], 256.0, None, ALU.subtract))
                CV(vv.tensor_tensor(tb_, ub, kfb, ALU.subtract))
            m_t3 = CV.n

            @blk.scalar
            def _(sc, i=i, m_t3=m_t3):
                sc.wait_ge(CV.sem, m_t3)
                CA(sc.activation(sg1, tb_, AF.Sign, bias=cbias[:, 2:3], scale=1.0))
                CA(sc.activation(sg2, tb_, AF.Sign, bias=cbias[:, 3:4], scale=1.0))
            m_s3 = CA.n

            @blk.vector
            def _(vv, i=i, e=e, m_s3=m_s3):
                vv.wait_ge(CA.sem, m_s3)
                CV(vv.tensor_tensor(sg1, sg1, sg2, ALU.add))
                CV(vv.tensor_scalar(sg1, sg1, 0.5, None, ALU.mult))
                CV(vv.tensor_tensor(kfb, kfb, sg1, ALU.add))
                CV(vv.tensor_scalar(y1qb[:, e * D:(e + 1) * D], kfb, 127.0, -128.0,
                                    ALU.min, ALU.max))
            m_q3 = CV.n

            @blk.gpsimd
            def _(g, i=i, e=e, m_q3=m_q3):
                g.wait_ge(CV.sem, m_q3)
                g.dma_start(out=y1_dram[i * 128:(i + 1) * 128, :],
                            in_=y1bf[:, e * D:(e + 1) * D]).then_inc(s_y1s[e], 16)
                g.dma_start(out=y1q_dram[i * 128:(i + 1) * 128, :],
                            in_=y1qb[:, e * D:(e + 1) * D]).then_inc(s_y1s[e], 16)
            n_y1s[0] += 2

        if atn_dbg is not None:
            @blk.gpsimd
            def _(g):
                g.wait_ge(CV.sem, m_atm_prev[0])
                g.wait_ge(CV.sem, m_atm_prev[1])
                for i in range(NTO):
                    g.dma_start(out=atn_dbg[i * 128:(i + 1) * 128, :],
                                in_=attn_sb[:, i * D:(i + 1) * D]).then_inc(s_dbg, 16)
                    n_dbg[0] += 1

        @blk.sync
        def _(sy):
            sy.wait_ge(s_y1s[0], 16 * NTO)
            sy.wait_ge(s_y1s[1], 16 * NTO)
            sy.wait_ge(s_wt2, 16 * n_wt2[0])
            if n_dbg[0]:
                sy.wait_ge(s_dbg, 16 * n_dbg[0])
            sy.nop().then_inc(s_bar1, 1)

        @blk.gpsimd
        def _(g):
            CP.gate(g.nop()).then_inc(s_bar1, 1)

        @blk.vector
        def _(vv):
            CV.gate(vv.nop()).then_inc(s_bar1, 1)

        @blk.scalar
        def _(sc):
            CA.gate(sc.nop()).then_inc(s_bar1, 1)

        @blk.tensor
        def _(te):
            CT.gate(te.nop()).then_inc(s_bar1, 1)

    p2.close()
    p2a.close()
    pA.close()

    p4 = contextlib.ExitStack()
    sb4 = lambda n, sh, dt: p4.enter_context(nc.sbuf_tensor(n, sh, dt))
    y1qfm = sb4("y1qfm", [128, 8 * SO], BF16)
    wgu = sb4("wgu", [128, 4 * 4096], BF16)
    prodb = sb4("prodb", [128, 8 * INNER], BF16)
    wrkq = sb4("wrkq", [128, 2 * D], F32)
    pqtm = sb4("pqtm", [128, 4 * INNER], BF16)
    gsb = sb4("gsb", [128, 2 * 512], F32)
    usb = sb4("usb", [128, 2 * 512], F32)
    kib4 = p4.enter_context(nc.sbuf_tensor("kib4", [128, 2 * D], I32))
    dbq = sb4("dbq", [128, 2 * D], BF16)
    sgq = sb4("sgq", [128, 2 * D], BF16)
    ctq = sb4("ctq", [128, 2 * D], BF16)
    pqstage = sb4("pqstage", [128, 1024], BF16)
    p4p = contextlib.ExitStack()
    pg = [p4p.enter_context(nc.psum_tensor(f"pg{i}", [128, 512], F32)) for i in range(2)]
    pu = [p4p.enter_context(nc.psum_tensor(f"pu{i}", [128, 512], F32)) for i in range(2)]
    ptrb = [p4p.enter_context(nc.psum_tensor(f"ptrb{i}", [128, 1024], BF16))
            for i in range(2)]

    with nc.Block() as blk:
        @blk.sync
        def _(sy):
            sy.wait_ge(s_bar1, 10)
            for ic in range(8):
                sy.dma_start_transpose(y1qfm[:, ic * SO:(ic + 1) * SO],
                                       y1q_dram[:, ic * 128:(ic + 1) * 128]
                                       ).then_inc(s_y1t, 16)

        @blk.vector
        def _(vv):
            vv.wait_ge(s_bar1, 10)
            for m in range(NTO):
                CV(vv.tensor_tensor(rsg[:, m:m + 1], rsx[:, m:m + 1], binv[:, 0:1],
                                    ALU.mult))
                CV(vv.tensor_tensor(rsu[:, m:m + 1], rsx[:, m:m + 1], binv[:, 1:2],
                                    ALU.mult))
        m_rs = CV.n

        @blk.scalar
        def _(sc):
            sc.wait_ge(s_bar1, 10)
            sc.wait_ge(CV.sem, m_rs)
            CA(load_act_set(sc, nc, SET_SILU))

        @blk.tensor
        def _(te):
            te.wait_ge(s_bar1, 10)
            te.wait_ge(s_y1t, 16 * 8)

        @blk.gpsimd
        def _(g):
            g.wait_ge(s_bar1, 10)

        n_gu = [0]
        m_guuse = {}

        m_silu = {}
        m_produ = {}
        n_pqs = [0]
        s8mark = {}
        m_trmark = {}
        m_ptrb_ev = {}

        def mkstages(ch, prev_mk, m, tb):
            par = tb % 2
            q = ch % 2
            psl = prodb[:, (par * 4 + m) * INNER + ch * D:
                        (par * 4 + m) * INNER + (ch + 1) * D]
            qsl = pqtm[:, m * INNER + ch * D:m * INNER + (ch + 1) * D]
            wk = wrkq[:, q * D:(q + 1) * D]
            kib = kib4[:, q * D:(q + 1) * D]
            db = dbq[:, q * D:(q + 1) * D]
            sg = sgq[:, q * D:(q + 1) * D]
            ct = ctq[:, q * D:(q + 1) * D]
            mk = {}

            def s0():
                @blk.vector
                def _(vv):
                    if prev_mk is not None:
                        vv.wait_ge(CA.sem, prev_mk["f"])
                    CV(vv.tensor_scalar(wk, psl, st[:, 9:10], MAGIC, ALU.mult,
                                        ALU.add))
                    CV(vv.tensor_copy(kib, wk))
                    CV(vv.scalar_tensor_tensor(wk, kib, -1.0, wk, ALU.mult,
                                               ALU.add))
                    CV(vv.tensor_scalar(db, wk, 0.5, None, ALU.subtract))
                mk["base"] = CV.n

            def s1():
                @blk.scalar
                def _(sc):
                    sc.wait_ge(CV.sem, mk["base"])
                    if prev_mk is not None:
                        sc.wait_ge(CV.sem, prev_mk["fin"])
                    CA(sc.activation(sg, db, AF.Sign, bias=0.0, scale=1.0))
                mk["l1"] = CA.n

            def s2():
                @blk.vector
                def _(vv):
                    vv.wait_ge(CA.sem, mk["l1"])
                    CV(vv.scalar_tensor_tensor(db, sg, -4.0, db, ALU.mult,
                                               ALU.add))
                mk["a1"] = CV.n

            def s3():
                @blk.scalar
                def _(sc):
                    sc.wait_ge(CV.sem, mk["a1"])
                    CA(sc.activation(sg, db, AF.Sign, bias=0.0, scale=1.0))
                mk["l2"] = CA.n

            def s4():
                @blk.vector
                def _(vv):
                    vv.wait_ge(CA.sem, mk["l2"])
                    CV(vv.scalar_tensor_tensor(db, sg, -2.0, db, ALU.mult,
                                               ALU.add))
                mk["a2"] = CV.n

            def s5():
                @blk.scalar
                def _(sc):
                    sc.wait_ge(CV.sem, mk["a2"])
                    CA(sc.activation(sg, db, AF.Sign, bias=0.0, scale=1.0))
                mk["l3"] = CA.n

            def s6():
                @blk.vector
                def _(vv):
                    vv.wait_ge(CA.sem, mk["l3"])
                    CV(vv.scalar_tensor_tensor(db, sg, -1.0, db, ALU.mult,
                                               ALU.add))
                mk["tf"] = CV.n

            def s7():
                @blk.scalar
                def _(sc):
                    sc.wait_ge(CV.sem, mk["tf"])
                    CA(sc.activation(sg, db, AF.Sign, bias=cbias[:, 7:8],
                                     scale=1.0))
                    CA(sc.activation(ct, db, AF.Sign, bias=cbias[:, 8:9],
                                     scale=1.0))
                mk["f"] = CA.n

            def s8():
                @blk.vector
                def _(vv):
                    vv.wait_ge(CA.sem, mk["f"])
                    if tb >= 1:
                        vv.wait_ge(CT.sem, m_trmark.get((tb - 1, ch), 0))
                    CV(vv.scalar_tensor_tensor(wk, db, -1.0, wk, ALU.mult,
                                               ALU.add))
                    CV(vv.scalar_tensor_tensor(wk, sg, 0.5, wk, ALU.mult,
                                               ALU.add))
                    CV(vv.scalar_tensor_tensor(wk, ct, 0.5, wk, ALU.mult,
                                               ALU.add))
                    CV(vv.scalar_tensor_tensor(wk, kib, -256.5, wk, ALU.add,
                                               ALU.add))
                    CV(vv.tensor_scalar(qsl, wk, 127.0, -128.0, ALU.min,
                                        ALU.max))
                mk["fin"] = CV.n
                if m == 3:
                    s8mark[(tb, ch)] = mk["fin"]

            return [s0, s1, s2, s3, s4, s5, s6, s7, s8], mk

        def mk_quant_thunks(tb):
            thunks = []
            for m in range(4):
                tok = tb * 4 + m

                def amax(m=m, tok=tok, tb=tb):
                    par = tb % 2

                    @blk.vector
                    def _(vv):
                        CV(vv.tensor_reduce(
                            st[:, 13:14],
                            prodb[:, (par * 4 + m) * INNER:
                                  (par * 4 + m + 1) * INNER],
                            AX, ALU.max, apply_absolute_value=True))
                        CV(vv.tensor_scalar(st[:, 13:14], st[:, 13:14], 1e-5,
                                            None, ALU.max))
                        CV(vv.reciprocal(st[:, 10:11], st[:, 13:14]))
                        CV(vv.tensor_scalar(st[:, 11:12], st[:, 10:11], 127.0,
                                            None, ALU.mult))
                        CV(vv.reciprocal(rs2[:, tok:tok + 1], st[:, 11:12]))
                        CV(vv.tensor_scalar(st[:, 9:10], st[:, 11:12], 1.0,
                                            None, ALU.mult))
                thunks.append(amax)

                chstages = []
                chmks = []
                for ch in range(4):
                    prev = chmks[ch - 2] if ch >= 2 else None
                    stg, mkd = mkstages(ch, prev, m, tb)
                    chstages.append(stg)
                    chmks.append(mkd)
                nst = 9
                offs = (0, 3, 8, 11)
                for w in range(nst + offs[-1]):
                    for ch in range(4):
                        sidx2 = w - offs[ch]
                        if 0 <= sidx2 < nst:
                            thunks.append(chstages[ch][sidx2])
            return thunks

        def mk_transpose_thunks(tb):
            thunks = []
            for ic in range(32):
                def unit(ic=ic, tb=tb):
                    tpar = ic % 2
                    seq = tb * 32 + ic

                    @blk.tensor
                    def _(te, ic=ic, tb=tb, tpar=tpar, seq=seq):
                        te.wait_ge(CV.sem, s8mark[(tb, ic // 8)])
                        pe_ = m_ptrb_ev.get(seq - 2)
                        if pe_:
                            te.wait_ge(CV.sem, pe_)
                        for m in range(4):
                            CT(te.transpose(ptrb[tpar][:, m * 128:(m + 1) * 128],
                                            pqtm[:, m * INNER + ic * 128:
                                                 m * INNER + (ic + 1) * 128],
                                            idb[:]))
                    m_tr = CT.n
                    if ic % 8 == 7:
                        m_trmark[(tb, ic // 8)] = m_tr

                    @blk.vector
                    def _(vv, ic=ic, tb=tb, tpar=tpar, m_tr=m_tr, npq=n_pqs[0]):
                        vv.wait_ge(CT.sem, m_tr)
                        if npq >= 1:
                            vv.wait_ge(s_pqs, 16 * (npq - 1))
                        CV(vv.tensor_copy(pqstage[:, tpar * 512:(tpar + 1) * 512],
                                          ptrb[tpar][:, 0:512]))
                    m_ptrb_ev[seq] = CV.n

                    @blk.gpsimd
                    def _(g, ic=ic, tb=tb, tpar=tpar, m_ev=m_ptrb_ev[seq]):
                        g.wait_ge(CV.sem, m_ev)
                        g.dma_start(out=pqd[tb, ic],
                                    in_=pqstage[:, tpar * 512:(tpar + 1) * 512]
                                    ).then_inc(s_pqs, 16)
                    n_pqs[0] += 1
                thunks.append(unit)
            return thunks

        qth = []
        for tb in range(NTB + 1):
            if tb < NTB:
                par = tb % 2
                for span in range(8):
                    sidx = tb * 8 + span
                    spar = sidx % 2

                    @blk.sync
                    def _(sy, tb=tb, span=span, sidx=sidx, spar=spar):
                        prev = m_guuse.get(sidx - 2)
                        if prev:
                            sy.wait_ge(CT.sem, prev)
                        sy.dma_start(
                            out=wgu[:, (2 * spar) * 4096:(2 * spar) * 4096 + 4096],
                            in_=wgsp_d[0, span]).then_inc(s_gu[spar], 16)

                    @blk.gpsimd
                    def _(g, tb=tb, span=span, sidx=sidx, spar=spar):
                        prev = m_guuse.get(sidx - 2)
                        if prev:
                            g.wait_ge(CT.sem, prev)
                        g.dma_start(
                            out=wgu[:, (2 * spar + 1) * 4096:
                                    (2 * spar + 1) * 4096 + 4096],
                            in_=wgsp_d[1, span]).then_inc(s_gu2[spar], 16)

                    for m in range(4):
                        mpar = (span * 4 + m) % 2
                        tok = tb * 4 + m

                        @blk.tensor
                        def _(te, tb=tb, span=span, m=m, sidx=sidx, spar=spar,
                              mpar=mpar, tok=tok):
                            te.wait_ge(s_gu[spar], 16 * (sidx // 2 + 1))
                            te.wait_ge(s_gu2[spar], 16 * (sidx // 2 + 1))
                            pm = m_silu.get(mpar)
                            if pm:
                                te.wait_ge(CA.sem, pm[1])
                            for ic in range(8):
                                lhsT = y1qfm[:, ic * SO + tok * 128:
                                             ic * SO + (tok + 1) * 128]
                                rhs = wgu[:, (2 * spar) * 4096 + ic * 512:
                                          (2 * spar) * 4096 + (ic + 1) * 512]
                                te.matmul(pg[mpar][:], lhsT, rhs, start=(ic == 0),
                                          stop=(ic == 7))
                            for ic in range(8):
                                lhsT = y1qfm[:, ic * SO + tok * 128:
                                             ic * SO + (tok + 1) * 128]
                                rhs = wgu[:, (2 * spar + 1) * 4096 + ic * 512:
                                          (2 * spar + 1) * 4096 + (ic + 1) * 512]
                                mmi = te.matmul(pu[mpar][:], lhsT, rhs,
                                                start=(ic == 0), stop=(ic == 7))
                            m_guuse[sidx] = CT(mmi)
                        m_mmgu = CT.n

                        @blk.scalar
                        def _(sc, tb=tb, m=m, mpar=mpar, tok=tok, m_mmgu=m_mmgu):
                            sc.wait_ge(CT.sem, m_mmgu)
                            pv_ = m_produ.get(mpar)
                            if pv_:
                                sc.wait_ge(CV.sem, pv_)
                            CA(sc.activation(gsb[:, mpar * 512:(mpar + 1) * 512],
                                             pg[mpar][:], AF.Silu, bias=0.0,
                                             scale=rsg[:, tok:tok + 1]))
                            CA(sc.activation(usb[:, mpar * 512:(mpar + 1) * 512],
                                             pu[mpar][:], AF.Copy, bias=0.0,
                                             scale=rsu[:, tok:tok + 1]))
                            m_silu[mpar] = (CA.n - 1, CA.n)

                        @blk.vector
                        def _(vv, tb=tb, span=span, m=m, mpar=mpar, par=par):
                            vv.wait_ge(CA.sem, m_silu[mpar][1])
                            m_produ[mpar] = CV(vv.tensor_tensor(
                                prodb[:, (par * 4 + m) * INNER + span * 512:
                                      (par * 4 + m) * INNER + (span + 1) * 512],
                                gsb[:, mpar * 512:(mpar + 1) * 512],
                                usb[:, mpar * 512:(mpar + 1) * 512], ALU.mult))

                        k = (len(qth) + (31 - (span * 4 + m))) // (32 - (span * 4 + m))
                        for _ in range(min(k, len(qth))):
                            qth.pop(0)()
            while qth:
                qth.pop(0)()
            if tb < NTB:
                qth = mk_quant_thunks(tb) + mk_transpose_thunks(tb)

        @blk.sync
        def _(sy):
            sy.wait_ge(s_pqs, 16 * n_pqs[0])
            sy.nop().then_inc(s_bar1, 1)

        @blk.gpsimd
        def _(g):
            CP.gate(g.nop()).then_inc(s_bar1, 1)

        @blk.vector
        def _(vv):
            CV.gate(vv.nop()).then_inc(s_bar1, 1)

        @blk.scalar
        def _(sc):
            CA.gate(sc.nop()).then_inc(s_bar1, 1)

        @blk.tensor
        def _(te):
            CT.gate(te.nop()).then_inc(s_bar1, 1)

    p4p.close()
    p4.close()

    p5 = contextlib.ExitStack()
    sb5 = lambda n, sh, dt: p5.enter_context(nc.sbuf_tensor(n, sh, dt))
    wdT = sb5("wdT", [128, 32 * D], BF16)
    pqsl = sb5("pqsl", [128, 8 * 512], BF16)
    y1sl = sb5("y1sl", [128, 2 * D], BF16)
    y2b = sb5("y2b", [128, 2 * D], F32)
    h5 = sb5("h5", [128, 2 * D], F32)
    y1f5 = sb5("y1f5", [128, 2 * D], F32)
    sqj5 = sb5("sqj5", [128, D], F32)
    outb = sb5("outb", [128, 2 * D], F32)
    p5p = contextlib.ExitStack()
    y2ps = [p5p.enter_context(nc.psum_tensor(f"y2ps{m}", [128, 1024], F32))
            for m in range(4)]

    with nc.Block() as blk:
        @blk.sync
        def _(sy):
            sy.wait_ge(s_bar1, 15)
            for g4 in range(8):
                sy.dma_start(out=wdT[:, g4 * 4 * D:(g4 + 1) * 4 * D],
                             in_=wdsp_d[g4]).then_inc(s_wdt, 16)

        @blk.vector
        def _(vv):
            vv.wait_ge(s_bar1, 15)
            for m in range(NTO):
                CV(vv.tensor_tensor(rs2w[:, m:m + 1], rs2[:, m:m + 1], binv[:, 2:3],
                                    ALU.mult))
        m_rs2w = CV.n

        @blk.scalar
        def _(sc):
            sc.wait_ge(s_bar1, 15)
            CA(load_act_set(sc, nc, SET_LNEXP))

        @blk.tensor
        def _(te):
            te.wait_ge(s_bar1, 15)

        @blk.gpsimd
        def _(g):
            g.wait_ge(s_bar1, 15)

        n_pql = [0, 0]
        n_y1l = [0, 0]
        n_out = [0]
        m_pq_use = {}
        m_evy2 = {}
        m_y1use = {}
        m_y2use = {}
        m_v5d = {}
        for tb in range(NTB):
            for icg in range(8):
                gpar = icg % 2

                @blk.sync
                def _(sy, tb=tb, icg=icg, gpar=gpar):
                    prev = m_pq_use.get((tb * 8 + icg) - 2)
                    if prev:
                        sy.wait_ge(CT.sem, prev)
                    sy.dma_start(
                        out=pqsl[:, gpar * 2048:(gpar + 1) * 2048].rearrange(
                            "p (g t) -> p g t", g=4),
                        in_=pqd[tb, icg * 4:(icg + 1) * 4].rearrange("g p t -> p g t")
                    ).then_inc(s_pq2[gpar], 16)
                n_pql[gpar] += 1

                @blk.tensor
                def _(te, tb=tb, icg=icg, gpar=gpar, npl=n_pql[gpar]):
                    te.wait_ge(s_pq2[gpar], 16 * npl)
                    if tb == 0:
                        te.wait_ge(s_wdt, 16 * (icg + 1))
                    if icg == 0:
                        for mm in range(4):
                            pm = m_evy2.get(mm)
                            if pm:
                                te.wait_ge(CA.sem, pm)
                    for sub in range(4):
                        ic = icg * 4 + sub
                        for m in range(4):
                            lhsT = pqsl[:, gpar * 2048 + sub * 512 + m * 128:
                                        gpar * 2048 + sub * 512 + (m + 1) * 128]
                            te.matmul(y2ps[m][:, 0:512], lhsT,
                                      wdT[:, ic * D:ic * D + 512],
                                      start=(ic == 0), stop=(ic == 31))
                            mmi = te.matmul(y2ps[m][:, 512:1024], lhsT,
                                            wdT[:, ic * D + 512:(ic + 1) * D],
                                            start=(ic == 0), stop=(ic == 31))
                    m_pq_use[tb * 8 + icg] = CT(mmi)
            m_dn = CT.n

            for m in range(4):
                tok = tb * 4 + m
                e5 = tok % 2
                y2s = y2b[:, e5 * D:(e5 + 1) * D]
                h5s = h5[:, e5 * D:(e5 + 1) * D]
                y1s = y1f5[:, e5 * D:(e5 + 1) * D]

                @blk.gpsimd
                def _(g, tok=tok, e5=e5, m_dn=m_dn):
                    if tok >= 2:
                        g.wait_ge(CV.sem, m_y1use[tok - 2])
                    g.dma_start(out=y1sl[:, e5 * D:(e5 + 1) * D],
                                in_=y1_dram[tok * 128:(tok + 1) * 128, :]
                                ).then_inc(s_y1l[e5], 16)
                n_y1l[e5] += 1

                @blk.scalar
                def _(sc, tb=tb, m=m, tok=tok, e5=e5, m_dn=m_dn, y2s=y2s):
                    sc.wait_ge(CT.sem, m_dn)
                    if tok == 0:
                        sc.wait_ge(CV.sem, m_rs2w)
                    pv_ = m_y2use.get(tok - 2)
                    if pv_:
                        sc.wait_ge(CV.sem, pv_)
                    CA(sc.activation(y2s, y2ps[m][:], AF.Copy, bias=0.0,
                                     scale=rs2w[:, tok:tok + 1]))
                m_y2e = CA.n
                m_evy2[m] = m_y2e

                @blk.vector
                def _(vv, tok=tok, e5=e5, m_y2e=m_y2e, nyl=n_y1l[e5],
                      y2s=y2s, h5s=h5s, y1s=y1s):
                    vv.wait_ge(CA.sem, m_y2e)
                    vv.wait_ge(s_y1l[e5], 16 * nyl)
                    CV(vv.tensor_copy(y1s, y1sl[:, e5 * D:(e5 + 1) * D]))
                    m_y1use[tok] = CV.n
                    CV(vv.tensor_tensor(h5s, y2s, y1s, ALU.add))
                    m_y2use[tok] = CV.n
                    CV(vv.tensor_scalar(y1s, h5s, 1.0, None, ALU.mult, ALU.add,
                                        accum_out=st[:, e5:e5 + 1]))
                m_h5 = CV.n

                @blk.scalar
                def _(sc, tok=tok, e5=e5, m_h5=m_h5, h5s=h5s):
                    sc.wait_ge(CV.sem, m_h5)
                    pv_ = m_v5d.get(tok - 2)
                    if pv_:
                        sc.wait_ge(CV.sem, pv_)
                    CA(sc.activation(sqj5[:], h5s, AF.Square, bias=0.0, scale=1.0,
                                     accum_out=st[:, 2 + e5:3 + e5]))
                m_sq5 = CA.n

                @blk.vector
                def _(vv, tok=tok, e5=e5, m_sq5=m_sq5):
                    vv.wait_ge(CA.sem, m_sq5)
                    CV(vv.tensor_scalar(st[:, 10:11], st[:, 2 + e5:3 + e5], 1.0 / D,
                                        None, ALU.mult))
                    CV(vv.tensor_scalar(st[:, 4 + e5:5 + e5], st[:, e5:e5 + 1],
                                        1.0 / D, None, ALU.mult))
                    CV(vv.tensor_tensor(st[:, 11:12], st[:, 4 + e5:5 + e5],
                                        st[:, 4 + e5:5 + e5], ALU.mult))
                    CV(vv.tensor_tensor(st[:, 8 + e5:9 + e5], st[:, 10:11],
                                        st[:, 11:12], ALU.subtract))
                m_v5 = CV.n
                m_v5d[tok] = m_v5

                @blk.scalar
                def _(sc, e5=e5, m_v5=m_v5):
                    sc.wait_ge(CV.sem, m_v5)
                    CA(sc.activation(st[:, 12:13], st[:, 8 + e5:9 + e5], AF.Ln,
                                     bias=cbias[:, 1:2], scale=1.0))
                    CA(sc.activation(st[:, 6 + e5:7 + e5], st[:, 12:13], AF.Exp,
                                     bias=0.0, scale=-0.5))
                m_r5 = CA.n

                @blk.vector
                def _(vv, tok=tok, e5=e5, m_r5=m_r5, nout=n_out[0], h5s=h5s):
                    vv.wait_ge(CA.sem, m_r5)
                    if tok >= 2:
                        vv.wait_ge(s_out, 16 * nout)
                    CV(vv.tensor_scalar(outb[:, e5 * D:(e5 + 1) * D], h5s,
                                        st[:, 4 + e5:5 + e5], st[:, 6 + e5:7 + e5],
                                        ALU.subtract, ALU.mult))
                m_o5 = CV.n

                @blk.sync
                def _(sy, tok=tok, e5=e5, m_o5=m_o5):
                    sy.wait_ge(CV.sem, m_o5)
                    sy.dma_start(out=out_d[tok * 128:(tok + 1) * 128, :],
                                 in_=outb[:, e5 * D:(e5 + 1) * D]
                                 ).then_inc(s_out, 16)
                n_out[0] += 1

        @blk.sync
        def _(sy):
            sy.wait_ge(s_out, 16 * NTO)

    p5p.close()
    p5.close()
    es.close()
    return nc


_CACHE = {}


def _program(debug=()):
    key = tuple(sorted(debug))
    if key not in _CACHE:
        _CACHE[key] = build(debug)
    return _CACHE[key]


def kernel(L0_attn_x, L0_attn_ln_scale, L0_attn_ln_bias, L0_attn_wq, L0_attn_wk,
           L0_attn_wv, L0_ffn_w_gate, L0_ffn_w_up, L0_ffn_w_down, _debug=()):
    x = np.ascontiguousarray(np.asarray(L0_attn_x, dtype=np.float32))
    wq = np.ascontiguousarray(np.asarray(L0_attn_wq, dtype=np.float32))
    wk = np.ascontiguousarray(np.asarray(L0_attn_wk, dtype=np.float32))
    wv = np.ascontiguousarray(np.asarray(L0_attn_wv, dtype=np.float32))
    lnsc = np.ascontiguousarray(np.asarray(L0_attn_ln_scale, np.float32)).reshape(1, D)
    lnbi = np.ascontiguousarray(np.asarray(L0_attn_ln_bias, np.float32)).reshape(1, D)
    wg = np.ascontiguousarray(np.asarray(L0_ffn_w_gate, dtype=np.float32))
    wu = np.ascontiguousarray(np.asarray(L0_ffn_w_up, dtype=np.float32))
    wd = np.ascontiguousarray(np.asarray(L0_ffn_w_down, dtype=np.float32))
    idb = np.eye(128, dtype=BFD)
    idf = np.eye(128, dtype=np.float32)

    nc = _program(_debug)
    in_maps = []
    for c in range(8):
        b, h = c // 2, c % 2
        own = x[b, h * SO:(h + 1) * SO, :]
        ctx = x[b, (1 - h) * SO:(2 - h) * SO, :]
        xc = np.concatenate([own, ctx], axis=0)
        in_maps.append({"x": xc, "wq": wq, "wk": wk, "wv": wv,
                        "lnsc": lnsc, "lnbi": lnbi,
                        "wg": wg, "wu": wu, "wd": wd,
                        "idb": idb, "idf": idf})
    res = run_bass_kernel_spmd(nc, in_maps, core_ids=list(range(8)))
    out = np.empty((4, S, D), dtype=np.float32)
    for c in range(8):
        b, h = c // 2, c % 2
        out[b, h * SO:(h + 1) * SO, :] = res.results[c]["out"]
    if _debug:
        return out, [res.results[c] for c in range(8)]
    return out


# version 21
# speedup vs baseline: 1.0287x; 1.0287x over previous
import contextlib
import numpy as np
import ml_dtypes
import concourse.bass as bass
import concourse.mybir as mybir
from concourse.bass_utils import run_bass_kernel_spmd

F32 = mybir.dt.float32
BF16 = mybir.dt.bfloat16
I32 = mybir.dt.int32
AF = mybir.ActivationFunctionType
ALU = mybir.AluOpType
AX = mybir.AxisListType.X
BFD = ml_dtypes.bfloat16

D = 1024
S = 4096
SO = 2048
H = 16
HD = 64
NHP = 8
NT = 32
NTO = 16
QB = 4
KC = 32
INNER = 4096
NTB = 4
GRP = [(i, 1) for i in range(32)]
MAGIC = 256.5
SET_LNEXP = 6
SET_SILU = 18


def load_act_set(sc, nc, set_id):
    return sc.add_instruction(mybir.InstLoadActFuncSet(
        name=nc.get_next_instruction_name(), act_func_set_id=set_id, ins=[], outs=[]))


class Ch:

    def __init__(self, sem, serialize=True):
        self.sem = sem
        self.serialize = serialize
        self.n = 0

    def __call__(self, inst):
        if self.serialize and self.n:
            inst._wait_ge(self.sem, self.n)
        inst.then_inc(self.sem, 1)
        self.n += 1
        return self.n

    def gate(self, inst):
        if self.n:
            inst._wait_ge(self.sem, self.n)
        return inst


def build(debug=()):
    nc = bass.Bass("TRN2", target_bir_lowering=False,
                   detect_race_conditions=False)
    x_d = nc.dram_tensor("x", [S, D], F32, kind="ExternalInput").ap()
    wq_d = nc.dram_tensor("wq", [D, D], F32, kind="ExternalInput").ap()
    wk_d = nc.dram_tensor("wk", [D, D], F32, kind="ExternalInput").ap()
    wv_d = nc.dram_tensor("wv", [D, D], F32, kind="ExternalInput").ap()
    lnsc_d = nc.dram_tensor("lnsc", [1, D], F32, kind="ExternalInput").ap()
    lnbi_d = nc.dram_tensor("lnbi", [1, D], F32, kind="ExternalInput").ap()
    wg_d = nc.dram_tensor("wg", [INNER, D], F32, kind="ExternalInput").ap()
    wu_d = nc.dram_tensor("wu", [INNER, D], F32, kind="ExternalInput").ap()
    wd_d = nc.dram_tensor("wd", [D, INNER], F32, kind="ExternalInput").ap()
    idb_d = nc.dram_tensor("idb", [128, 128], BF16, kind="ExternalInput").ap()
    idf_d = nc.dram_tensor("idf", [128, 128], F32, kind="ExternalInput").ap()
    out_d = nc.dram_tensor("out", [SO, D], F32, kind="ExternalOutput").ap()

    def scratch(name, shape, dtype):
        kind = "ExternalOutput" if name in debug else "Internal"
        return nc.dram_tensor(name, shape, dtype, kind=kind).ap()

    z_dram = scratch("zd", [S, D], BF16)
    wg_nat = scratch("wgn", [INNER, D], BF16)
    wu_nat = scratch("wun", [INNER, D], BF16)
    wd_nat = scratch("wdn", [D, INNER], BF16)
    wgsp_d = nc.dram_tensor("wgsp", [2, 8, 128, 8 * 512], BF16, kind="Internal").ap()
    wdsp_d = nc.dram_tensor("wdsp", [8, 128, 4 * D], BF16, kind="Internal").ap()
    pqd = scratch("pqd", [NTB, 32, 128, 512], BF16)
    y1_dram = scratch("y1d", [SO, D], BF16)
    y1q_dram = scratch("y1qd", [SO, D], BF16)
    atn_dbg = scratch("atn", [SO, D], BF16) if "atn" in debug else None
    q_sp = scratch("qsp", [NHP, 128, SO], BF16)
    k_sp = scratch("ksp", [NHP, 128, S], BF16)
    v_sp = scratch("vsp", [NHP, 128, KC * 130], BF16)
    sres_dbg = scratch("sres", [SO, D], F32) if "sres" in debug else None
    n_dbg = [0]

    es = contextlib.ExitStack()
    sem = lambda n: es.enter_context(nc.semaphore(n))

    CV = Ch(sem("c_v"))
    CA = Ch(sem("c_a"))
    CP = Ch(sem("c_p"))
    CT = Ch(sem("c_t"), serialize=False)

    s_cst = sem("s_cst")
    s_cst2 = sem("s_cst2")
    s_w0 = sem("s_w0")
    s_xs = [sem(f"s_xs{i}") for i in range(3)]
    s_zds = [sem(f"s_zd{i}") for i in range(2)]
    s_zfm = sem("s_zfm")
    s_qksp = sem("s_qksp")
    s_vsp = sem("s_vsp")
    s_bar1 = sem("s_bar1")
    s_out = sem("s_out")
    s_dbg = sem("s_dbg")
    s_kq = [sem(f"s_kq{i}") for i in range(3)]
    s_vv = [sem(f"s_vv{i}") for i in range(2)]
    s_wp = sem("s_wp")
    s_wpb = sem("s_wpb")
    s_gu2 = [sem(f"s_gu2{i}") for i in range(2)]
    s_wq = sem("s_wq")
    s_wt = sem("s_wt")
    s_wt2 = sem("s_wt2")
    s_y1s = [sem(f"s_y1s{i}") for i in range(2)]
    s_y1t = sem("s_y1t")
    s_gu = [sem(f"s_gu{i}") for i in range(2)]
    s_pqs = sem("s_pqs")
    s_wdt = sem("s_wdt")
    s_pq2 = [sem(f"s_pq2{i}") for i in range(2)]
    s_y1l = [sem(f"s_y1l{i}") for i in range(2)]

    sbE = lambda n, sh, dt: es.enter_context(nc.sbuf_tensor(n, sh, dt))
    onesr = sbE("onesr", [1, 128], F32)
    onesc = sbE("onesc", [128, 1], F32)
    onesb = sbE("onesb", [128, 1], BF16)
    idb = sbE("idb_sb", [128, 128], BF16)
    idf = sbE("idf_sb", [128, 128], F32)
    cbias = sbE("cbias", [128, 10], F32)
    swcols = sbE("swcols", [128, 4], F32)
    binv = sbE("binv", [128, 4], F32)
    rsx = sbE("rsx", [128, NTO], F32)
    rs2 = sbE("rs2", [128, NTO], F32)
    rsg = sbE("rsg", [128, NTO], F32)
    rsu = sbE("rsu", [128, NTO], F32)
    rs2w = sbE("rs2w", [128, NTO], F32)
    st = sbE("st", [128, 14], F32)
    pA = contextlib.ExitStack()
    sres = pA.enter_context(nc.sbuf_tensor("sres_sb", [128, NTO * D], F32))
    p1 = contextlib.ExitStack()
    sbt = lambda n, sh, dt: p1.enter_context(nc.sbuf_tensor(n, sh, dt))
    pst = lambda n, sh: p1.enter_context(nc.psum_tensor(n, sh, F32))
    gb = sbt("gb", [128, D], F32)
    bbt = sbt("bbt", [128, D], F32)
    wslot = sbt("wslot", [128, 8 * D], BF16)
    xr = sbt("xr", [128, 3 * D], F32)
    xnr = sbt("xnr", [128, 2 * D], F32)
    sqjunk = sbt("sqjunk", [128, D], F32)
    zf32 = sbt("zf32", [128, D], F32)
    zg32 = sbt("zg32", [128, D], F32)
    zfm = sbt("zfm", [128, 8 * S], BF16)
    qkstage = sbt("qkstage", [128, S], BF16)
    vstage = sbt("vstage", [128, 4 * 1040], BF16)
    ps = [pst(f"ps{i}", [128, 512]) for i in range(8)]
    ps_qk, ps_v = ps[:4], ps[4:]

    with nc.Block() as blk:
        @blk.sync
        def _(sy):
            sy.dma_start(out=gb[0:1, :], in_=lnsc_d).then_inc(s_cst, 16)
            sy.dma_start(out=bbt[0:1, :], in_=lnbi_d).then_inc(s_cst, 16)
            sy.dma_start(out=idb[:], in_=idb_d).then_inc(s_cst2, 16)
            sy.dma_start(out=idf[:], in_=idf_d).then_inc(s_cst2, 16)

        @blk.gpsimd
        def _(g):
            for ic in range(8):
                g.dma_start(out=wslot[:, ic * D:(ic + 1) * D],
                            in_=wq_d[ic * 128:(ic + 1) * 128, :]).then_inc(s_w0, 16)

        @blk.vector
        def _(vv):
            CV(vv.memset(onesr[:], 1.0))
            CV(vv.memset(onesc[:], 1.0))
            CV(vv.memset(onesb[:], 1.0))
            CV(vv.memset(cbias[:, 0:1], 1e-6))
            CV(vv.memset(cbias[:, 1:2], 1e-5))
            CV(vv.memset(cbias[:, 2:3], -257.0))
            CV(vv.memset(cbias[:, 3:4], -256.0))
            CV(vv.memset(cbias[:, 4:5], 1e-4))
            CV(vv.memset(cbias[:, 5:6], -256.5))
            CV(vv.memset(cbias[:, 6:7], -1.0))
            CV(vv.memset(cbias[:, 7:8], -0.5))
            CV(vv.memset(cbias[:, 8:9], 0.5))
            for sl in range(4):
                CV(vv.memset(vstage[:, sl * 1040:(sl + 1) * 1040].rearrange(
                    "p (h c) -> p h c", h=16)[:, :, 64:65], 1.0))
        m_consts = CV.n

        @blk.scalar
        def _(sc):
            CA(load_act_set(sc, nc, SET_LNEXP))

        @blk.tensor
        def _(te):
            te.wait_ge(s_cst, 32)
            te.wait_ge(CV.sem, m_consts)
            te.matmul(ps[0][:], onesr[:], gb[0:1, 0:512], start=True, stop=True)
            te.matmul(ps[1][:], onesr[:], gb[0:1, 512:1024], start=True, stop=True)
            te.matmul(ps[2][:], onesr[:], bbt[0:1, 0:512], start=True, stop=True)
            CT(te.matmul(ps[3][:], onesr[:], bbt[0:1, 512:1024], start=True, stop=True))
        m_bc = CT.n

        @blk.vector
        def _(vv):
            vv.wait_ge(CT.sem, m_bc)
            CV(vv.tensor_copy(gb[:, 0:512], ps[0][:]))
            CV(vv.tensor_copy(gb[:, 512:1024], ps[1][:]))
            CV(vv.tensor_copy(bbt[:, 0:512], ps[2][:]))
            CV(vv.tensor_copy(bbt[:, 512:1024], ps[3][:]))
        m_bce = CV.n

        m_xn = [0] * NT
        m_xuse = [0] * NT
        m_sq = [0] * NT
        m_var = [0] * NT
        m_rstd = [0] * NT
        for i in range(NT):
            r, e = i % 3, i % 2

            @blk.sync
            def _(sy, i=i, r=r):
                if i >= 3:
                    sy.wait_ge(CV.sem, m_xuse[i - 3])
                    sy.wait_ge(CA.sem, m_sq[i - 3])
                sy.dma_start(out=xr[:, r * D:(r + 1) * D],
                             in_=x_d[i * 128:(i + 1) * 128, :]).then_inc(s_xs[r], 16)

            @blk.vector
            def _(vv, i=i, r=r, e=e):
                xt = xr[:, r * D:(r + 1) * D]
                vv.wait_ge(s_xs[r], 16 * (i // 3 + 1))
                CV(vv.tensor_scalar(zf32[:], xt, 1.0, None, ALU.mult, ALU.add,
                                    accum_out=st[:, e:e + 1]))

            @blk.scalar
            def _(sc, i=i, r=r, e=e):
                sc.wait_ge(s_xs[r], 16 * (i // 3 + 1))
                if i >= 2:
                    sc.wait_ge(CV.sem, m_var[i - 2])
                m_sq[i] = CA(sc.activation(sqjunk[:], xr[:, r * D:(r + 1) * D],
                                           AF.Square, bias=0.0, scale=1.0,
                                           accum_out=st[:, 2 + e:3 + e]))

            @blk.vector
            def _(vv, i=i, e=e):
                vv.wait_ge(CA.sem, m_sq[i])
                if i >= 1:
                    vv.wait_ge(CA.sem, m_rstd[i - 1])
                CV(vv.tensor_scalar(st[:, 10:11], st[:, 2 + e:3 + e], 1.0 / D, None,
                                    ALU.mult))
                CV(vv.tensor_scalar(st[:, 4 + e:5 + e], st[:, e:e + 1], 1.0 / D, None,
                                    ALU.mult))
                CV(vv.tensor_tensor(st[:, 11:12], st[:, 4 + e:5 + e],
                                    st[:, 4 + e:5 + e], ALU.mult))
                m_var[i] = CV(vv.tensor_tensor(st[:, 8 + e:9 + e], st[:, 10:11],
                                               st[:, 11:12], ALU.subtract))

            @blk.scalar
            def _(sc, i=i, e=e):
                sc.wait_ge(CV.sem, m_var[i])
                if i >= 2:
                    sc.wait_ge(CV.sem, m_xuse[i - 2])
                CA(sc.activation(st[:, 12:13], st[:, 8 + e:9 + e], AF.Ln,
                                 bias=cbias[:, 0:1], scale=1.0))
                m_rstd[i] = CA(sc.activation(st[:, 6 + e:7 + e], st[:, 12:13], AF.Exp,
                                             bias=0.0, scale=-0.5))

            @blk.vector
            def _(vv, i=i, r=r, e=e):
                xt = xr[:, r * D:(r + 1) * D]
                xnt = xnr[:, (i % 2) * D:(i % 2 + 1) * D]
                vv.wait_ge(CA.sem, m_rstd[i])
                CV(vv.tensor_scalar(zf32[:], xt, st[:, 4 + e:5 + e],
                                    st[:, 6 + e:7 + e], ALU.subtract, ALU.mult))
                CV(vv.tensor_tensor(zg32[:], zf32[:], gb[:], ALU.mult))
                if i >= 2:
                    vv.wait_ge(s_zds[i % 2], 16 * (i // 2))
                m_xn[i] = CV(vv.tensor_tensor(xnt, zg32[:], bbt[:], ALU.add))
                if i < NTO:
                    CV(vv.tensor_tensor(sres[:, i * D:(i + 1) * D], xt, xnt, ALU.add))
                m_xuse[i] = CV.n

            @blk.gpsimd
            def _(g, i=i):
                g.wait_ge(CV.sem, m_xn[i])
                g.dma_start(out=z_dram[i * 128:(i + 1) * 128, :],
                            in_=xnr[:, (i % 2) * D:(i % 2 + 1) * D]
                            ).then_inc(s_zds[i % 2], 16)

        if sres_dbg is not None:
            @blk.gpsimd
            def _(g):
                g.wait_ge(CV.sem, m_xuse[NT - 1])
                for i in range(NTO):
                    g.dma_start(out=sres_dbg[i * 128:(i + 1) * 128, :],
                                in_=sres[:, i * D:(i + 1) * D]).then_inc(s_dbg, 16)
                    n_dbg[0] += 1

        @blk.sync
        def _(sy):
            sy.wait_ge(s_zds[0], 16 * (NT // 2))
            sy.wait_ge(s_zds[1], 16 * (NT // 2))
            for ic in range(8):
                sy.dma_start_transpose(zfm[:, ic * S:(ic + 1) * S],
                                       z_dram[:, ic * 128:(ic + 1) * 128]
                                       ).then_inc(s_zfm, 16)

        qk_jobs = [("q", oc, col) for oc in range(8) for col in range(4)] + \
                  [("k", oc, col) for oc in range(8) for col in range(8)]
        v_jobs = [(tt, nb) for tt in range(NT) for nb in range(2)]
        m_mmqk = [0] * len(qk_jobs)
        m_evqk = [0] * len(qk_jobs)
        m_mmv = [0] * len(v_jobs)
        m_evv = [0] * len(v_jobs)

        @blk.gpsimd
        def _(g):
            pass

        mark_q_done = [0]
        for j, (kind, oc, col) in enumerate(qk_jobs):
            @blk.tensor
            def _(te, j=j, kind=kind, oc=oc, col=col):
                if j == 0:
                    te.wait_ge(s_zfm, 16 * 8)
                    te.wait_ge(CV.sem, m_bce)
                te.wait_ge(s_w0, 16 * 8 * (1 if kind == "q" else 2))
                if j >= 4:
                    te.wait_ge(CA.sem, m_evqk[j - 4])
                p = ps_qk[j % 4]
                for ic in range(8):
                    lhsT = wslot[:, ic * D + oc * 128:ic * D + (oc + 1) * 128]
                    rhs = zfm[:, ic * S + col * 512:ic * S + (col + 1) * 512]
                    mm = te.matmul(p[:], lhsT, rhs, start=(ic == 0), stop=(ic == 7))
                m_mmqk[j] = CT(mm)

            if kind == "q" and j == 31:
                mark_q_done[0] = m_mmqk[j]

                @blk.gpsimd
                def _(g, mk=m_mmqk[j]):
                    g.wait_ge(CT.sem, mk)
                    for ic in range(8):
                        g.dma_start(out=wslot[:, ic * D:(ic + 1) * D],
                                    in_=wk_d[ic * 128:(ic + 1) * 128, :]
                                    ).then_inc(s_w0, 16)

            @blk.scalar
            def _(sc, j=j, kind=kind, oc=oc, col=col):
                sc.wait_ge(CT.sem, m_mmqk[j])
                if col == 0:
                    nprev = oc if kind == "q" else 8 + oc
                    if nprev:
                        sc.wait_ge(s_qksp, 16 * nprev)
                m_evqk[j] = CA(sc.copy(qkstage[:, col * 512:(col + 1) * 512],
                                       ps_qk[j % 4][:]))

            last_col = 3 if kind == "q" else 7
            if col == last_col:
                @blk.sync
                def _(sy, j=j, kind=kind, oc=oc):
                    sy.wait_ge(CA.sem, m_evqk[j])
                    dst = q_sp[oc] if kind == "q" else k_sp[oc]
                    src = qkstage[:, 0:SO] if kind == "q" else qkstage[:, 0:S]
                    sy.dma_start(out=dst, in_=src).then_inc(s_qksp, 16)

        @blk.gpsimd
        def _(g, mk=m_mmqk[-1]):
            g.wait_ge(CT.sem, mk)
            for ic in range(8):
                g.dma_start(out=wslot[:, ic * D:(ic + 1) * D],
                            in_=wv_d[ic * 128:(ic + 1) * 128, :]).then_inc(s_w0, 16)

        for jv, (tt, nb) in enumerate(v_jobs):
            @blk.tensor
            def _(te, jv=jv, tt=tt, nb=nb):
                if jv == 0:
                    te.wait_ge(s_w0, 16 * 24)
                if jv >= 4:
                    te.wait_ge(CV.sem, m_evv[jv - 4])
                p = ps_v[jv % 4]
                for ic in range(8):
                    lhsT = zfm[:, ic * S + tt * 128:ic * S + (tt + 1) * 128]
                    rhs = wslot[:, ic * D + nb * 512:ic * D + (nb + 1) * 512]
                    mm = te.matmul(p[:], lhsT, rhs, start=(ic == 0), stop=(ic == 7))
                m_mmv[jv] = CT(mm)

            @blk.vector
            def _(vv, jv=jv, tt=tt, nb=nb):
                vv.wait_ge(CT.sem, m_mmv[jv])
                if tt >= 4 and nb == 0:
                    vv.wait_ge(s_vsp, 16 * 8 * (tt // 4))
                base = (tt % 4) * 1040 + nb * 520
                dst = vstage[:, base:base + 520].rearrange(
                    "p (h c) -> p h c", h=8)[:, :, 0:64]
                src = ps_v[jv % 4][:].rearrange("p (h c) -> p h c", h=8)
                m_evv[jv] = CV(vv.tensor_copy(dst, src))

            if nb == 1 and tt % 4 == 3:
                @blk.sync
                def _(sy, jv=jv, tt=tt):
                    sy.wait_ge(CV.sem, m_evv[jv])
                    grp = tt // 4
                    for hp in range(NHP):
                        src = vstage[:].rearrange("p (t c) -> p t c", t=4)[
                            :, :, hp * 130:(hp + 1) * 130]
                        dst = v_sp[hp][:, grp * 4 * 130:(grp + 1) * 4 * 130]
                        sy.dma_start(out=dst, in_=src).then_inc(s_vsp, 16)

        @blk.sync
        def _(sy):
            sy.wait_ge(s_qksp, 16 * 16)
            sy.wait_ge(s_vsp, 16 * 64)
            sy.nop().then_inc(s_bar1, 1)

        @blk.gpsimd
        def _(g):
            CP.gate(g.nop()).then_inc(s_bar1, 1)

        @blk.vector
        def _(vv):
            CV.gate(vv.nop()).then_inc(s_bar1, 1)

        @blk.scalar
        def _(sc):
            CA.gate(sc.nop()).then_inc(s_bar1, 1)

        @blk.tensor
        def _(te):
            CT.gate(te.nop()).then_inc(s_bar1, 1)

    p1.close()

    p2a = contextlib.ExitStack()
    attn_sb = p2a.enter_context(nc.sbuf_tensor("attn_sb", [128, NTO * D], BF16))
    p2 = contextlib.ExitStack()
    sb2 = lambda n, sh, dt: p2.enter_context(nc.sbuf_tensor(n, sh, dt))
    ps2 = lambda n, sh: p2.enter_context(nc.psum_tensor(n, sh, F32))
    kslot = sb2("kslot", [128, 3 * S], BF16)
    vslot = sb2("vslot", [128, 2 * KC * 130], BF16)
    qslot = sb2("qslot", [128, 3 * SO], BF16)
    pt = sb2("pt", [128, 4 * 512], BF16)
    sqbuf = sb2("sqbuf", [128, S], F32)
    csq2 = sb2("csq2", [128, SO], BF16)
    pvf = sb2("pvf", [128, 2 * 512], F32)
    rlb = sb2("rlb", [128, 2 * 4], F32)
    cbc = sb2("cbc", [128, 2], F32)
    srow = sb2("srow", [1, 8], F32)
    acc3 = sb2("acc3", [128, 4], F32)
    wbuf = sb2("wbuf", [128, 2048], F32)
    abuf = sb2("abuf", [128, 2048], BF16)
    wib = sb2("wib", [128, 2048], BF16)
    tstage = sb2("tstage", [128, S], BF16)
    sTp = [ps2(f"sT{s}", [128, 1024]) for s in range(3)]
    sT1 = sTp[0]
    pvps = [ps2(f"pvps{h}", [128, 512]) for h in range(2)]

    with nc.Block() as blk:
        @blk.sync
        def _(sy):
            sy.wait_ge(s_bar1, 5)

        @blk.vector
        def _(vv):
            vv.wait_ge(s_bar1, 5)
            CV(vv.memset(acc3[:], 0.0))

        @blk.scalar
        def _(sc):
            sc.wait_ge(s_bar1, 5)

        @blk.tensor
        def _(te):
            te.wait_ge(s_bar1, 5)
            te.wait_ge(s_cst2, 32)

        @blk.gpsimd
        def _(g):
            g.wait_ge(s_bar1, 5)

        wsrcs = [wg_d, wu_d, wd_d]
        wnats = [wg_nat, wu_nat, wd_nat]
        NW = 4194304.0
        n_wp = [0]
        n_wq = [0]
        n_wt = [0]
        n_wt2 = [0]
        m_wbuf = [0]
        m_wib = [0]

        m_wslot = [0, 0]
        n_piece = [0]
        n_wpb = [0]

        def wchunk_ap(wd_ap, c):
            flat = wd_ap.rearrange("r d -> (r d)")
            return flat[c * 131072:(c + 1) * 131072].rearrange("(p v) -> p v", v=1024)

        def wsl(sl):
            return wbuf[:, sl * 1024:(sl + 1) * 1024]

        def wibsl(sl):
            return wib[:, sl * 1024:(sl + 1) * 1024]

        def emit_jobA(w, c):
            sl = n_piece[0] % 2
            n_piece[0] += 1

            @blk.gpsimd
            def _(g, w=w, c=c, sl=sl, gate=m_wslot[sl]):
                g.wait_ge(CV.sem, gate)
                g.dma_start(out=wsl(sl), in_=wchunk_ap(wsrcs[w], c)).then_inc(s_wp, 16)
            n_wp[0] += 1

            @blk.vector
            def _(vv, w=w, c=c, sl=sl, nwp=n_wp[0]):
                vv.wait_ge(s_wp, 16 * nwp)
                CV(vv.tensor_reduce(acc3[:, 3:4], wsl(sl), AX, ALU.add,
                                    apply_absolute_value=True))
                CV(vv.tensor_tensor(acc3[:, w:w + 1], acc3[:, w:w + 1], acc3[:, 3:4],
                                    ALU.add))
                m_wslot[sl] = CV.n

        def emit_jobF(w, mark_bank_free):
            @blk.tensor
            def _(te, w=w):
                te.wait_ge(CV.sem, max(m_wslot[0], m_wslot[1], mark_bank_free[0]))
                CT(te.matmul(sT1[0:1, 0:1], acc3[:, w:w + 1], onesc[:], start=True,
                             stop=True))
            mk = CT.n

            @blk.vector
            def _(vv, w=w, mk=mk):
                vv.wait_ge(CT.sem, mk)
                CV(vv.tensor_scalar(srow[0:1, 3:4], sT1[0:1, 0:1], 1.0 / NW, None, ALU.mult))
                CV(vv.tensor_scalar(srow[0:1, 3:4], srow[0:1, 3:4], 1e-5, None, ALU.max))
                CV(vv.reciprocal(srow[0:1, w:w + 1], srow[0:1, 3:4]))
                CV(vv.reciprocal(srow[0:1, 4 + w:5 + w], srow[0:1, w:w + 1]))
            mkd = CV.n

            @blk.tensor
            def _(te, w=w, mkd=mkd):
                te.wait_ge(CV.sem, mkd)
                te.matmul(sT1[:, 0:1], onesr[:], srow[0:1, w:w + 1], start=True, stop=True)
                CT(te.matmul(sT1[:, 1:2], onesr[:], srow[0:1, 4 + w:5 + w], start=True,
                             stop=True))
            mk2 = CT.n

            @blk.vector
            def _(vv, w=w, mk2=mk2):
                vv.wait_ge(CT.sem, mk2)
                CV(vv.tensor_copy(swcols[:, w:w + 1], sT1[:, 0:1]))
                CV(vv.tensor_copy(binv[:, w:w + 1], sT1[:, 1:2]))
            mark_bank_free[0] = CV.n

        def emit_jobB(w, c):
            sl = n_piece[0] % 2
            n_piece[0] += 1

            @blk.sync
            def _(sy, w=w, c=c, sl=sl, gate=m_wslot[sl]):
                sy.wait_ge(CV.sem, gate)
                sy.dma_start(out=wsl(sl), in_=wchunk_ap(wsrcs[w], c)
                             ).then_inc(s_wpb, 16)
            n_wpb[0] += 1

            @blk.vector
            def _(vv, w=w, c=c, sl=sl, nwp=n_wpb[0], nwq=n_wq[0]):
                vv.wait_ge(s_wpb, 16 * nwp)
                CV(vv.tensor_scalar(wsl(sl), wsl(sl), swcols[:, w:w + 1], None,
                                    ALU.mult))
                if nwq >= 1:
                    vv.wait_ge(s_wq, 16 * (nwq - 1))
                CV(vv.tensor_scalar(wibsl(sl), wsl(sl), 0.5, None, ALU.is_ge))
                CV(vv.tensor_scalar(abuf[:, 0:1024], wsl(sl), -0.5, None, ALU.is_le))
                m_wslot[sl] = CV.n
                m_wib[0] = CV(vv.tensor_tensor(wibsl(sl), wibsl(sl), abuf[:, 0:1024],
                                               ALU.subtract))

            @blk.gpsimd
            def _(g, w=w, c=c, sl=sl, mwib=m_wib):
                g.wait_ge(CV.sem, mwib[0])
                g.dma_start(out=wchunk_ap(wnats[w], c), in_=wibsl(sl)).then_inc(s_wq, 16)
            n_wq[0] += 1

        def emit_jobD(g4):
            @blk.sync
            def _(sy, g4=g4, nw2=None):
                sy.wait_ge(s_wq, 16 * 32 * 3)
                sy.wait_ge(s_wt2, 16 * n_wt2[0])
                for sub in range(4):
                    ic = g4 * 4 + sub
                    sy.dma_start_transpose(tstage[:, sub * D:(sub + 1) * D],
                                           wd_nat[:, ic * 128:(ic + 1) * 128]
                                           ).then_inc(s_wt, 16)
            n_wt[0] += 4

            @blk.sync
            def _(sy, g4=g4, nwt=n_wt[0]):
                sy.wait_ge(s_wt, 16 * nwt)
                sy.dma_start(out=wdsp_d[g4], in_=tstage[:, 0:4 * D]
                             ).then_inc(s_wt2, 16)
            n_wt2[0] += 1

        def emit_jobC(w, ic):
            @blk.sync
            def _(sy, w=w, ic=ic):
                sy.wait_ge(s_wq, 16 * 32 * (w + 1))
                sy.wait_ge(s_wt2, 16 * n_wt2[0])
                sy.dma_start_transpose(tstage[:], wnats[w][:, ic * 128:(ic + 1) * 128]
                                       ).then_inc(s_wt, 16)
            n_wt[0] += 1

            @blk.sync
            def _(sy, w=w, ic=ic, nwt=n_wt[0]):
                sy.wait_ge(s_wt, 16 * nwt)
                for span in range(8):
                    sy.dma_start(out=wgsp_d[w, span][:, ic * 512:(ic + 1) * 512],
                                 in_=tstage[:, span * 512:(span + 1) * 512]
                                 ).then_inc(s_wt2, 16)
            n_wt2[0] += 8

        jobs = [("A", w, c) for w in range(3) for c in range(32)] + \
               [("B", w, c) for w in range(3) for c in range(32)] + \
               [("C", w, ic) for w in range(2) for ic in range(8)] + \
               [("D", g4, 0) for g4 in range(8)]
        jq = list(jobs)
        f_done = [False]
        bankfree_mark = [0]

        def pop_jobs(k):
            for _ in range(k):
                if not jq:
                    return
                kind, w, c = jq[0]
                if kind == "B" and not f_done[0]:
                    return
                jq.pop(0)
                if kind == "A":
                    emit_jobA(w, c)
                elif kind == "B":
                    emit_jobB(w, c)
                elif kind == "C":
                    emit_jobC(w, c)
                else:
                    emit_jobD(w)

        m_hp_last = [0] * NHP
        last_exp = [0]
        last_exp_set = {}
        last_pv = {}
        m_atm_prev = [0, 0]
        m_tp_prev = [0, 0]
        m_cnorm_read = [0]
        m_cmm_prev = [0]
        m_csq = {}
        m_cb_cur = [0]

        m_cpc = [0]

        def emit_c_piece(hpx, j, s):
            @blk.tensor
            def _(te, hpx=hpx, j=j, s=s):
                te.wait_ge(CV.sem, m_csq[hpx])
                if j < 2:
                    te.wait_ge(CV.sem, m_cnorm_read[0])
                if j < 8:
                    mmi = te.matmul(sTp[s][0:1, 0:512], onesc[:],
                                    sqbuf[:, j * 512:(j + 1) * 512],
                                    start=True, stop=True)
                else:
                    mmi = te.matmul(sTp[s][0:1, 0:512], onesb[:],
                                    csq2[:, (j - 8) * 512:(j - 7) * 512],
                                    start=True, stop=True)
                CT(mmi)
            mpc = CT.n
            m_cmm_prev[0] = mpc

            @blk.vector
            def _(vv, hpx=hpx, j=j, s=s, mpc=mpc):
                vv.wait_ge(CT.sem, mpc)
                if j == 0:
                    CV(vv.tensor_reduce(srow[0:1, 0:1], sTp[s][0:1, 0:512], AX,
                                        ALU.max))
                elif j == 8:
                    CV(vv.tensor_reduce(srow[0:1, 1:2], sTp[s][0:1, 0:512], AX,
                                        ALU.max))
                else:
                    d = 0 if j < 8 else 1
                    CV(vv.tensor_reduce(srow[0:1, 2:3], sTp[s][0:1, 0:512], AX,
                                        ALU.max))
                    CV(vv.tensor_tensor(srow[0:1, d:d + 1], srow[0:1, d:d + 1],
                                        srow[0:1, 2:3], ALU.max))
            m_cpc[0] = CV.n

        def emit_c_final(hpx):
            @blk.vector
            def _(vv, hpx=hpx):
                CV(vv.tensor_tensor(srow[0:1, 5:6], srow[0:1, 0:1], srow[0:1, 1:2],
                                    ALU.mult))
                m_cnorm_read[0] = CV.n

            @blk.scalar
            def _(sc, hpx=hpx):
                sc.wait_ge(CV.sem, m_cnorm_read[0])
                CA(sc.activation(srow[0:1, 6:7], srow[0:1, 5:6], AF.Ln, bias=0.0,
                                 scale=1.0))
                CA(sc.activation(srow[0:1, 7:8], srow[0:1, 6:7], AF.Exp,
                                 bias=cbias[0:1, 4:5], scale=0.5))
            m_cval = CA.n

            @blk.vector
            def _(vv, hpx=hpx, m_cval=m_cval):
                vv.wait_ge(CA.sem, m_cval)
                CV(vv.tensor_scalar(srow[0:1, 6:7], srow[0:1, 7:8], -1.0 / 32.0,
                                    None, ALU.mult))
            m_negc = CV.n

            @blk.tensor
            def _(te, hpx=hpx, m_negc=m_negc):
                te.wait_ge(CV.sem, m_negc)
                CT(te.matmul(sT1[:, 0:1], onesr[:], srow[0:1, 6:7], start=True,
                             stop=True))
            m_cbmm = CT.n

            @blk.vector
            def _(vv, hpx=hpx, m_cbmm=m_cbmm):
                vv.wait_ge(CT.sem, m_cbmm)
                CV(vv.tensor_copy(cbc[:, hpx % 2:hpx % 2 + 1], sT1[:, 0:1]))
            m_cnorm_read[0] = CV.n
            return CV.n

        def emit_c_squares(hpx):
            par = hpx % 3
            kS_ = kslot[:, par * S:(par + 1) * S]
            qS_ = qslot[:, par * SO:(par + 1) * SO]

            @blk.vector
            def _(vv, hpx=hpx, par=par, kS_=kS_, qS_=qS_):
                vv.wait_ge(s_kq[par], 32 * (hpx // 3 + 1))
                if m_cmm_prev[0]:
                    vv.wait_ge(CT.sem, m_cmm_prev[0])
                CV(vv.tensor_tensor(sqbuf[:], kS_, kS_, ALU.mult))
                CV(vv.tensor_tensor(csq2[:], qS_, qS_, ALU.mult))
            m_csq[hpx] = CV.n

        def emit_c_finish(hpx):
            csl = [sTp[s][0:1, c * 512:(c + 1) * 512]
                   for s in range(3) for c in range(2)]

            @blk.tensor
            def _(te, hpx=hpx):
                te.wait_ge(CV.sem, m_csq[hpx])
                if last_exp[0]:
                    te.wait_ge(CA.sem, last_exp[0])
                if m_cnorm_read[0]:
                    te.wait_ge(CV.sem, m_cnorm_read[0])
                for pc in range(6):
                    mmi = te.matmul(csl[pc], onesc[:],
                                    sqbuf[:, pc * 512:(pc + 1) * 512],
                                    start=True, stop=True)
                CT(mmi)
            m1 = CT.n

            @blk.vector
            def _(vv, hpx=hpx, m1=m1):
                vv.wait_ge(CT.sem, m1)
                CV(vv.tensor_reduce(srow[0:1, 0:1], csl[0][0:1, :], AX, ALU.max))
                for pc in range(1, 6):
                    CV(vv.tensor_reduce(srow[0:1, 2:3], csl[pc][0:1, :], AX, ALU.max))
                    CV(vv.tensor_tensor(srow[0:1, 0:1], srow[0:1, 0:1],
                                        srow[0:1, 2:3], ALU.max))
            r1 = CV.n

            @blk.tensor
            def _(te, hpx=hpx, r1=r1):
                te.wait_ge(CV.sem, r1)
                te.matmul(csl[0], onesc[:], sqbuf[:, 6 * 512:7 * 512],
                          start=True, stop=True)
                te.matmul(csl[1], onesc[:], sqbuf[:, 7 * 512:8 * 512],
                          start=True, stop=True)
                for pc in range(4):
                    mmi = te.matmul(csl[2 + pc], onesb[:],
                                    csq2[:, pc * 512:(pc + 1) * 512],
                                    start=True, stop=True)
                CT(mmi)
            m2 = CT.n
            m_cmm_prev[0] = m2

            @blk.vector
            def _(vv, hpx=hpx, m2=m2):
                vv.wait_ge(CT.sem, m2)
                for pc in range(2):
                    CV(vv.tensor_reduce(srow[0:1, 2:3], csl[pc][0:1, :], AX, ALU.max))
                    CV(vv.tensor_tensor(srow[0:1, 0:1], srow[0:1, 0:1],
                                        srow[0:1, 2:3], ALU.max))
                CV(vv.tensor_reduce(srow[0:1, 1:2], csl[2][0:1, :], AX, ALU.max))
                for pc in range(1, 4):
                    CV(vv.tensor_reduce(srow[0:1, 2:3], csl[2 + pc][0:1, :], AX,
                                        ALU.max))
                    CV(vv.tensor_tensor(srow[0:1, 1:2], srow[0:1, 1:2],
                                        srow[0:1, 2:3], ALU.max))
                CV(vv.tensor_tensor(srow[0:1, 5:6], srow[0:1, 0:1], srow[0:1, 1:2],
                                    ALU.mult))
                m_cnorm_read[0] = CV.n

            @blk.scalar
            def _(sc, hpx=hpx):
                sc.wait_ge(CV.sem, m_cnorm_read[0])
                CA(sc.activation(srow[0:1, 6:7], srow[0:1, 5:6], AF.Ln, bias=0.0,
                                 scale=1.0))
                CA(sc.activation(srow[0:1, 7:8], srow[0:1, 6:7], AF.Exp,
                                 bias=cbias[0:1, 4:5], scale=0.5))
            m_cval = CA.n

            @blk.vector
            def _(vv, hpx=hpx, m_cval=m_cval):
                vv.wait_ge(CA.sem, m_cval)
                CV(vv.tensor_scalar(srow[0:1, 6:7], srow[0:1, 7:8], -1.0 / 32.0,
                                    None, ALU.mult))
            m_negc = CV.n

            @blk.tensor
            def _(te, hpx=hpx, m_negc=m_negc):
                te.wait_ge(CV.sem, m_negc)
                CT(te.matmul(sT1[:, 0:1], onesr[:], srow[0:1, 6:7], start=True,
                             stop=True))
            m_cbmm = CT.n

            @blk.vector
            def _(vv, hpx=hpx, m_cbmm=m_cbmm):
                vv.wait_ge(CT.sem, m_cbmm)
                CV(vv.tensor_copy(cbc[:, hpx % 2:hpx % 2 + 1], sT1[:, 0:1]))
            m_cnorm_read[0] = CV.n
            return CV.n

        pending_epi = [None]

        def emit_epi(hp_e, qb_e, m_pvg_e):
            for h in range(2):
                @blk.vector
                def _(vv, hp=hp_e, qb=qb_e, h=h, m_pvg=m_pvg_e):
                    vv.wait_ge(CT.sem, m_pvg[(len(GRP) - 1, h)])
                    if m_tp_prev[h]:
                        vv.wait_ge(CT.sem, m_tp_prev[h])
                    CV(vv.tensor_copy(pvf[:, h * 512:(h + 1) * 512][0:65, :],
                                      pvps[h][0:65, :]))
                m_pvf = CV.n

                @blk.tensor
                def _(te, hp=hp_e, qb=qb_e, h=h, m_pvf=m_pvf):
                    te.wait_ge(CV.sem, m_pvf)
                    te.wait_ge(CV.sem, m_atm_prev[h])
                    pf = pvf[:, h * 512:(h + 1) * 512]
                    for j in range(4):
                        te.transpose(pvps[h][:, j * 64:(j + 1) * 64],
                                     pf[0:64, j * 128:(j + 1) * 128],
                                     idf[0:64, 0:64])
                    for j in range(4):
                        mmi = te.transpose(pvps[h][:, 256 + j:257 + j],
                                           pf[64:65, j * 128:(j + 1) * 128],
                                           idf[64:65, 64:65])
                    m_tp_prev[h] = CT(mmi)

                @blk.vector
                def _(vv, hp=hp_e, qb=qb_e, h=h):
                    vv.wait_ge(CT.sem, m_tp_prev[h])
                    CV(vv.reciprocal(rlb[:, h * 4:(h + 1) * 4],
                                     pvps[h][:, 256:260]))
                    for j in range(4):
                        CV(vv.tensor_scalar(
                            attn_sb[:, (qb * 4 + j) * D + (2 * hp + h) * 64:
                                    (qb * 4 + j) * D + (2 * hp + h) * 64 + 64],
                            pvps[h][:, j * 64:(j + 1) * 64],
                            rlb[:, h * 4 + j:h * 4 + j + 1], None, ALU.mult))
                    m_atm_prev[h] = CV.n

        def flush_epi():
            if pending_epi[0] is not None:
                emit_epi(*pending_epi[0])
                pending_epi[0] = None

        def stream_kq(hp, gate_ct=0, gate_cv=0):
            par = hp % 3

            @blk.sync
            def _(sy, hp=hp, par=par, gate_ct=gate_ct, gate_cv=gate_cv):
                if gate_ct:
                    sy.wait_ge(CT.sem, gate_ct)
                if gate_cv:
                    sy.wait_ge(CV.sem, gate_cv)
                sy.dma_start(out=kslot[:, par * S:(par + 1) * S], in_=k_sp[hp]
                             ).then_inc(s_kq[par], 16)
                sy.dma_start(out=qslot[:, par * SO:(par + 1) * SO], in_=q_sp[hp]
                             ).then_inc(s_kq[par], 16)

        def stream_v(hp, gate_ct=0):
            par = hp % 2

            @blk.sync
            def _(sy, hp=hp, par=par, gate_ct=gate_ct):
                if gate_ct:
                    sy.wait_ge(CT.sem, gate_ct)
                sy.dma_start(out=vslot[:, par * KC * 130:(par + 1) * KC * 130],
                             in_=v_sp[hp]).then_inc(s_vv[par], 16)

        stream_kq(0)
        stream_v(0)
        stream_kq(1)
        stream_v(1)
        stream_kq(2)
        emit_c_squares(0)

        for hp in range(NHP):
            par = hp % 3
            kS = kslot[:, par * S:(par + 1) * S]
            vS = vslot[:, (hp % 2) * KC * 130:(hp % 2 + 1) * KC * 130]
            qS = qslot[:, par * SO:(par + 1) * SO]
            ecb = hp % 2

            if hp == 0:
                m_cb_cur[0] = emit_c_finish(0)
            m_qkgate = m_cb_cur[0]

            if hp == 4:
                bankfree_mark[0] = m_cb_cur[0]
                emit_jobF(0, bankfree_mark)
                emit_jobF(1, bankfree_mark)
                emit_jobF(2, bankfree_mark)
                f_done[0] = True
                m_cnorm_read[0] = max(m_cnorm_read[0], bankfree_mark[0])
                m_qkgate = max(m_qkgate, bankfree_mark[0])

            for qb in range(QB):
                m_exp = {}
                m_pvg = {}

                def emit_qk(gi, hp=hp, qb=qb, kS=kS, qS=qS, m_qkgate=m_qkgate,
                            m_exp=m_exp, extra_cv=0):
                    kc0, glen = GRP[gi]
                    s = gi % 3

                    @blk.tensor
                    def _(te, gi=gi, kc0=kc0, glen=glen, s=s, extra_cv=extra_cv):
                        if gi < 3:
                            pe = last_exp_set.get(s)
                            if pe:
                                te.wait_ge(CA.sem, pe)
                            te.wait_ge(CV.sem, m_qkgate)
                        else:
                            te.wait_ge(CA.sem, m_exp[gi - 3])
                        if extra_cv:
                            te.wait_ge(CV.sem, extra_cv)
                        kc = kc0
                        te.matmul(sTp[s][:, 0:512],
                                  kS[0:64, kc * 128:(kc + 1) * 128],
                                  qS[0:64, qb * 512:(qb + 1) * 512],
                                  start=True, stop=True, tile_position=(0, 0))
                        mmi = te.matmul(sTp[s][:, 512:1024],
                                        kS[64:128, kc * 128:(kc + 1) * 128],
                                        qS[64:128, qb * 512:(qb + 1) * 512],
                                        start=True, stop=True,
                                        tile_position=(64, 0))
                        CT(mmi)
                    return CT.n

                def emit_exp(gi, m_qk, hp=hp, qb=qb, ecb=ecb, m_exp=m_exp):
                    s = gi % 3
                    slot = gi % 2

                    @blk.scalar
                    def _(sc, gi=gi, s=s, slot=slot, m_qk=m_qk):
                        sc.wait_ge(CT.sem, m_qk)
                        prev = last_pv.get(slot)
                        if prev is not None:
                            sc.wait_ge(CT.sem, prev)
                        m_exp[gi] = CA(sc.activation(
                            pt[:, slot * 1024:(slot + 1) * 1024],
                            sTp[s][:], AF.Exp, bias=cbc[:, ecb:ecb + 1],
                            scale=1.0 / 32.0))
                        last_exp[0] = m_exp[gi]
                    last_exp_set[s] = m_exp[gi]

                def emit_pv(gi, hp=hp, qb=qb, vS=vS, m_exp=m_exp, m_pvg=m_pvg):
                    kc0, glen = GRP[gi]
                    slot = gi % 2
                    for h in range(2):
                        @blk.tensor
                        def _(te, gi=gi, h=h, kc0=kc0, slot=slot, hp=hp):
                            if h == 0:
                                te.wait_ge(CA.sem, m_exp[gi])
                            if gi == 0:
                                te.wait_ge(CV.sem, m_atm_prev[h])
                                if h == 0:
                                    te.wait_ge(s_vv[hp % 2], 16 * (hp // 2 + 1))
                            kc = kc0
                            lhsT = vS[:, kc * 130 + h * 65:kc * 130 + (h + 1) * 65]
                            rhs = pt[:, slot * 1024 + h * 512:
                                     slot * 1024 + (h + 1) * 512]
                            mmi = te.matmul(pvps[h][0:65, :], lhsT, rhs,
                                            start=(kc == 0), stop=(kc == KC - 1))
                            m_pvg[(gi, h)] = CT(mmi)
                    last_pv[slot] = m_pvg[(gi, 1)]

                NG = len(GRP)
                mq = {0: emit_qk(0), 1: emit_qk(1), 2: emit_qk(2)}
                emit_exp(0, mq[0])
                flush_epi()
                if qb == 1 and hp + 1 < NHP:
                    emit_c_squares(hp + 1)
                pop_jobs(1)
                for gi in range(NG):
                    emit_pv(gi)
                    extra_cv = 0
                    if qb == 3 and hp + 1 < NHP and gi >= 8 and gi % 2 == 0:
                        emit_c_piece(hp + 1, (gi - 8) // 2, gi % 3)
                        extra_cv = m_cpc[0]
                    if gi + 3 < NG:
                        mq[gi + 3] = emit_qk(gi + 3, extra_cv=extra_cv)
                    if gi + 1 < NG:
                        emit_exp(gi + 1, mq[gi + 1])
                    pop_gis = ((5, 10, 15, 20, 25, 29) if hp < 4 else
                               (4, 8, 12, 16, 20, 24, 28))
                    if gi in pop_gis:
                        pop_jobs(1)

                pending_epi[0] = (hp, qb, m_pvg)

            m_hp_last[hp] = CT.n
            if hp + 1 < NHP:
                m_cb_cur[0] = emit_c_final(hp + 1)
            if hp + 3 < NHP:
                stream_kq(hp + 3, gate_ct=m_hp_last[hp], gate_cv=m_csq[hp])
            if hp + 2 < NHP:
                stream_v(hp + 2, gate_ct=m_hp_last[hp])

        flush_epi()

        while jq:
            pop_jobs(1)

        hb = sqbuf[:, 0:D]
        y1f = sqbuf[:, D:2 * D]
        ub = sqbuf[:, 2 * D:3 * D]
        sqj3 = sqbuf[:, 3 * D:4 * D]
        kfb = wbuf[:, 0:D]
        tb_ = wbuf[:, D:2 * D]
        sg1 = abuf[:, 0:D]
        sg2 = qslot[:, 0:D]
        y1bf = kslot[:, 0:2 * D]
        y1qb = kslot[:, 2 * D:4 * D]
        kib = p2.enter_context(nc.sbuf_tensor("kib", [128, D], I32))
        n_y1s = [0]

        @blk.vector
        def _(vv):
            vv.wait_ge(s_wp, 16 * n_wp[0])
            vv.wait_ge(s_wpb, 16 * n_wpb[0])
            vv.wait_ge(CT.sem, m_hp_last[NHP - 1])
        for i in range(NTO):
            e = i % 2

            @blk.vector
            def _(vv, i=i, e=e):
                if i == 0:
                    vv.wait_ge(CV.sem, m_atm_prev[0])
                    vv.wait_ge(CV.sem, m_atm_prev[1])
                CV(vv.tensor_copy(hb, attn_sb[:, i * D:(i + 1) * D]))
                CV(vv.tensor_tensor(hb, hb, sres[:, i * D:(i + 1) * D], ALU.add))
                CV(vv.tensor_scalar(y1f, hb, 1.0, None, ALU.mult, ALU.add,
                                    accum_out=st[:, 0:1]))
            m_h = CV.n

            @blk.scalar
            def _(sc, i=i, m_h=m_h):
                sc.wait_ge(CV.sem, m_h)
                CA(sc.activation(sqj3, hb, AF.Square, bias=0.0, scale=1.0,
                                 accum_out=st[:, 2:3]))
            m_sq3 = CA.n

            @blk.vector
            def _(vv, i=i, m_sq3=m_sq3):
                vv.wait_ge(CA.sem, m_sq3)
                CV(vv.tensor_scalar(st[:, 10:11], st[:, 2:3], 1.0 / D, None, ALU.mult))
                CV(vv.tensor_scalar(st[:, 4:5], st[:, 0:1], 1.0 / D, None, ALU.mult))
                CV(vv.tensor_tensor(st[:, 11:12], st[:, 4:5], st[:, 4:5], ALU.mult))
                CV(vv.tensor_tensor(st[:, 8:9], st[:, 10:11], st[:, 11:12], ALU.subtract))
            m_v3 = CV.n

            @blk.scalar
            def _(sc, i=i, m_v3=m_v3):
                sc.wait_ge(CV.sem, m_v3)
                CA(sc.activation(st[:, 12:13], st[:, 8:9], AF.Ln, bias=cbias[:, 1:2],
                                 scale=1.0))
                CA(sc.activation(st[:, 6:7], st[:, 12:13], AF.Exp, bias=0.0, scale=-0.5))
            m_r3 = CA.n

            @blk.vector
            def _(vv, i=i, e=e, m_r3=m_r3):
                vv.wait_ge(CA.sem, m_r3)
                CV(vv.tensor_scalar(y1f, hb, st[:, 4:5], st[:, 6:7],
                                    ALU.subtract, ALU.mult))
                if i >= 2:
                    vv.wait_ge(s_y1s[i % 2], 32 * (i // 2))
                CV(vv.tensor_copy(y1bf[:, e * D:(e + 1) * D], y1f))
                CV(vv.tensor_reduce(st[:, 13:14], y1f, AX, ALU.max,
                                    apply_absolute_value=True))
                CV(vv.tensor_scalar(st[:, 13:14], st[:, 13:14], 1e-5, None, ALU.max))
                CV(vv.reciprocal(st[:, 10:11], st[:, 13:14]))
                CV(vv.tensor_scalar(st[:, 11:12], st[:, 10:11], 127.0, None, ALU.mult))
                CV(vv.reciprocal(rsx[:, i:i + 1], st[:, 11:12]))
                CV(vv.tensor_scalar(ub, y1f, st[:, 11:12], MAGIC, ALU.mult,
                                    ALU.add))
                CV(vv.tensor_copy(kib[:], ub))
                CV(vv.tensor_scalar(kfb, kib[:], 256.0, None, ALU.subtract))
                CV(vv.tensor_tensor(tb_, ub, kfb, ALU.subtract))
            m_t3 = CV.n

            @blk.scalar
            def _(sc, i=i, m_t3=m_t3):
                sc.wait_ge(CV.sem, m_t3)
                CA(sc.activation(sg1, tb_, AF.Sign, bias=cbias[:, 2:3], scale=1.0))
                CA(sc.activation(sg2, tb_, AF.Sign, bias=cbias[:, 3:4], scale=1.0))
            m_s3 = CA.n

            @blk.vector
            def _(vv, i=i, e=e, m_s3=m_s3):
                vv.wait_ge(CA.sem, m_s3)
                CV(vv.tensor_tensor(sg1, sg1, sg2, ALU.add))
                CV(vv.tensor_scalar(sg1, sg1, 0.5, None, ALU.mult))
                CV(vv.tensor_tensor(kfb, kfb, sg1, ALU.add))
                CV(vv.tensor_scalar(y1qb[:, e * D:(e + 1) * D], kfb, 127.0, -128.0,
                                    ALU.min, ALU.max))
            m_q3 = CV.n

            @blk.gpsimd
            def _(g, i=i, e=e, m_q3=m_q3):
                g.wait_ge(CV.sem, m_q3)
                g.dma_start(out=y1_dram[i * 128:(i + 1) * 128, :],
                            in_=y1bf[:, e * D:(e + 1) * D]).then_inc(s_y1s[e], 16)
                g.dma_start(out=y1q_dram[i * 128:(i + 1) * 128, :],
                            in_=y1qb[:, e * D:(e + 1) * D]).then_inc(s_y1s[e], 16)
            n_y1s[0] += 2

        if atn_dbg is not None:
            @blk.gpsimd
            def _(g):
                g.wait_ge(CV.sem, m_atm_prev[0])
                g.wait_ge(CV.sem, m_atm_prev[1])
                for i in range(NTO):
                    g.dma_start(out=atn_dbg[i * 128:(i + 1) * 128, :],
                                in_=attn_sb[:, i * D:(i + 1) * D]).then_inc(s_dbg, 16)
                    n_dbg[0] += 1

        @blk.sync
        def _(sy):
            sy.wait_ge(s_y1s[0], 16 * NTO)
            sy.wait_ge(s_y1s[1], 16 * NTO)
            sy.wait_ge(s_wt2, 16 * n_wt2[0])
            if n_dbg[0]:
                sy.wait_ge(s_dbg, 16 * n_dbg[0])
            sy.nop().then_inc(s_bar1, 1)

        @blk.gpsimd
        def _(g):
            CP.gate(g.nop()).then_inc(s_bar1, 1)

        @blk.vector
        def _(vv):
            CV.gate(vv.nop()).then_inc(s_bar1, 1)

        @blk.scalar
        def _(sc):
            CA.gate(sc.nop()).then_inc(s_bar1, 1)

        @blk.tensor
        def _(te):
            CT.gate(te.nop()).then_inc(s_bar1, 1)

    p2.close()
    p2a.close()
    pA.close()

    p4 = contextlib.ExitStack()
    sb4 = lambda n, sh, dt: p4.enter_context(nc.sbuf_tensor(n, sh, dt))
    y1qfm = sb4("y1qfm", [128, 8 * SO], BF16)
    wgu = sb4("wgu", [128, 4 * 4096], BF16)
    prodb = sb4("prodb", [128, 8 * INNER], BF16)
    wrkq = sb4("wrkq", [128, 2 * D], F32)
    pqtm = sb4("pqtm", [128, 4 * INNER], BF16)
    gsb = sb4("gsb", [128, 2 * 512], F32)
    usb = sb4("usb", [128, 2 * 512], F32)
    kib4 = p4.enter_context(nc.sbuf_tensor("kib4", [128, 2 * D], I32))
    dbq = sb4("dbq", [128, 2 * D], BF16)
    sgq = sb4("sgq", [128, 2 * D], BF16)
    ctq = sb4("ctq", [128, 2 * D], BF16)
    pqstage = sb4("pqstage", [128, 1024], BF16)
    p4p = contextlib.ExitStack()
    pg = [p4p.enter_context(nc.psum_tensor(f"pg{i}", [128, 512], F32)) for i in range(2)]
    pu = [p4p.enter_context(nc.psum_tensor(f"pu{i}", [128, 512], F32)) for i in range(2)]
    ptrb = [p4p.enter_context(nc.psum_tensor(f"ptrb{i}", [128, 1024], BF16))
            for i in range(2)]

    with nc.Block() as blk:
        @blk.sync
        def _(sy):
            sy.wait_ge(s_bar1, 10)
            for ic in range(8):
                sy.dma_start_transpose(y1qfm[:, ic * SO:(ic + 1) * SO],
                                       y1q_dram[:, ic * 128:(ic + 1) * 128]
                                       ).then_inc(s_y1t, 16)

        @blk.vector
        def _(vv):
            vv.wait_ge(s_bar1, 10)
            for m in range(NTO):
                CV(vv.tensor_tensor(rsg[:, m:m + 1], rsx[:, m:m + 1], binv[:, 0:1],
                                    ALU.mult))
                CV(vv.tensor_tensor(rsu[:, m:m + 1], rsx[:, m:m + 1], binv[:, 1:2],
                                    ALU.mult))
        m_rs = CV.n

        @blk.scalar
        def _(sc):
            sc.wait_ge(s_bar1, 10)
            sc.wait_ge(CV.sem, m_rs)
            CA(load_act_set(sc, nc, SET_SILU))

        @blk.tensor
        def _(te):
            te.wait_ge(s_bar1, 10)
            te.wait_ge(s_y1t, 16 * 8)

        @blk.gpsimd
        def _(g):
            g.wait_ge(s_bar1, 10)

        n_gu = [0]
        m_guuse = {}

        m_silu = {}
        m_produ = {}
        n_pqs = [0]
        s8mark = {}
        m_trmark = {}
        m_ptrb_ev = {}

        def mkstages(ch, prev_mk, m, tb):
            par = tb % 2
            q = ch % 2
            psl = prodb[:, (par * 4 + m) * INNER + ch * D:
                        (par * 4 + m) * INNER + (ch + 1) * D]
            qsl = pqtm[:, m * INNER + ch * D:m * INNER + (ch + 1) * D]
            wk = wrkq[:, q * D:(q + 1) * D]
            kib = kib4[:, q * D:(q + 1) * D]
            db = dbq[:, q * D:(q + 1) * D]
            sg = sgq[:, q * D:(q + 1) * D]
            ct = ctq[:, q * D:(q + 1) * D]
            mk = {}

            def s0():
                @blk.vector
                def _(vv):
                    if prev_mk is not None:
                        vv.wait_ge(CA.sem, prev_mk["f"])
                    CV(vv.tensor_scalar(wk, psl, st[:, 9:10], MAGIC, ALU.mult,
                                        ALU.add))
                    CV(vv.tensor_copy(kib, wk))
                    CV(vv.scalar_tensor_tensor(wk, kib, -1.0, wk, ALU.mult,
                                               ALU.add))
                    CV(vv.tensor_scalar(db, wk, 0.5, None, ALU.subtract))
                mk["base"] = CV.n

            def s1():
                @blk.scalar
                def _(sc):
                    sc.wait_ge(CV.sem, mk["base"])
                    if prev_mk is not None:
                        sc.wait_ge(CV.sem, prev_mk["fin"])
                    CA(sc.activation(sg, db, AF.Sign, bias=0.0, scale=1.0))
                mk["l1"] = CA.n

            def s2():
                @blk.vector
                def _(vv):
                    vv.wait_ge(CA.sem, mk["l1"])
                    CV(vv.scalar_tensor_tensor(db, sg, -4.0, db, ALU.mult,
                                               ALU.add))
                mk["a1"] = CV.n

            def s3():
                @blk.scalar
                def _(sc):
                    sc.wait_ge(CV.sem, mk["a1"])
                    CA(sc.activation(sg, db, AF.Sign, bias=0.0, scale=1.0))
                mk["l2"] = CA.n

            def s4():
                @blk.vector
                def _(vv):
                    vv.wait_ge(CA.sem, mk["l2"])
                    CV(vv.scalar_tensor_tensor(db, sg, -2.0, db, ALU.mult,
                                               ALU.add))
                mk["a2"] = CV.n

            def s5():
                @blk.scalar
                def _(sc):
                    sc.wait_ge(CV.sem, mk["a2"])
                    CA(sc.activation(sg, db, AF.Sign, bias=0.0, scale=1.0))
                mk["l3"] = CA.n

            def s6():
                @blk.vector
                def _(vv):
                    vv.wait_ge(CA.sem, mk["l3"])
                    CV(vv.scalar_tensor_tensor(db, sg, -1.0, db, ALU.mult,
                                               ALU.add))
                mk["tf"] = CV.n

            def s7():
                @blk.scalar
                def _(sc):
                    sc.wait_ge(CV.sem, mk["tf"])
                    CA(sc.activation(sg, db, AF.Sign, bias=cbias[:, 7:8],
                                     scale=1.0))
                    CA(sc.activation(ct, db, AF.Sign, bias=cbias[:, 8:9],
                                     scale=1.0))
                mk["f"] = CA.n

            def s8():
                @blk.vector
                def _(vv):
                    vv.wait_ge(CA.sem, mk["f"])
                    if tb >= 1:
                        vv.wait_ge(CT.sem, m_trmark.get((tb - 1, ch), 0))
                    CV(vv.scalar_tensor_tensor(wk, db, -1.0, wk, ALU.mult,
                                               ALU.add))
                    CV(vv.scalar_tensor_tensor(wk, sg, 0.5, wk, ALU.mult,
                                               ALU.add))
                    CV(vv.scalar_tensor_tensor(wk, ct, 0.5, wk, ALU.mult,
                                               ALU.add))
                    CV(vv.scalar_tensor_tensor(wk, kib, -256.5, wk, ALU.add,
                                               ALU.add))
                    CV(vv.tensor_scalar(qsl, wk, 127.0, -128.0, ALU.min,
                                        ALU.max))
                mk["fin"] = CV.n
                if m == 3:
                    s8mark[(tb, ch)] = mk["fin"]

            return [s0, s1, s2, s3, s4, s5, s6, s7, s8], mk

        def mk_quant_thunks(tb):
            thunks = []
            for m in range(4):
                tok = tb * 4 + m

                def amax(m=m, tok=tok, tb=tb):
                    par = tb % 2

                    @blk.vector
                    def _(vv):
                        CV(vv.tensor_reduce(
                            st[:, 13:14],
                            prodb[:, (par * 4 + m) * INNER:
                                  (par * 4 + m + 1) * INNER],
                            AX, ALU.max, apply_absolute_value=True))
                        CV(vv.tensor_scalar(st[:, 13:14], st[:, 13:14], 1e-5,
                                            None, ALU.max))
                        CV(vv.reciprocal(st[:, 10:11], st[:, 13:14]))
                        CV(vv.tensor_scalar(st[:, 11:12], st[:, 10:11], 127.0,
                                            None, ALU.mult))
                        CV(vv.reciprocal(rs2[:, tok:tok + 1], st[:, 11:12]))
                        CV(vv.tensor_scalar(st[:, 9:10], st[:, 11:12], 1.0,
                                            None, ALU.mult))
                thunks.append(amax)

                chstages = []
                chmks = []
                for ch in range(4):
                    prev = chmks[ch - 2] if ch >= 2 else None
                    stg, mkd = mkstages(ch, prev, m, tb)
                    chstages.append(stg)
                    chmks.append(mkd)
                nst = 9
                offs = (0, 3, 8, 11)
                for w in range(nst + offs[-1]):
                    for ch in range(4):
                        sidx2 = w - offs[ch]
                        if 0 <= sidx2 < nst:
                            thunks.append(chstages[ch][sidx2])
            return thunks

        def mk_transpose_thunks(tb):
            thunks = []
            for ic in range(32):
                def unit(ic=ic, tb=tb):
                    tpar = ic % 2
                    seq = tb * 32 + ic

                    @blk.tensor
                    def _(te, ic=ic, tb=tb, tpar=tpar, seq=seq):
                        te.wait_ge(CV.sem, s8mark[(tb, ic // 8)])
                        pe_ = m_ptrb_ev.get(seq - 2)
                        if pe_:
                            te.wait_ge(CV.sem, pe_)
                        for m in range(4):
                            CT(te.transpose(ptrb[tpar][:, m * 128:(m + 1) * 128],
                                            pqtm[:, m * INNER + ic * 128:
                                                 m * INNER + (ic + 1) * 128],
                                            idb[:]))
                    m_tr = CT.n
                    if ic % 8 == 7:
                        m_trmark[(tb, ic // 8)] = m_tr

                    @blk.vector
                    def _(vv, ic=ic, tb=tb, tpar=tpar, m_tr=m_tr, npq=n_pqs[0]):
                        vv.wait_ge(CT.sem, m_tr)
                        if npq >= 1:
                            vv.wait_ge(s_pqs, 16 * (npq - 1))
                        CV(vv.tensor_copy(pqstage[:, tpar * 512:(tpar + 1) * 512],
                                          ptrb[tpar][:, 0:512]))
                    m_ptrb_ev[seq] = CV.n

                    @blk.gpsimd
                    def _(g, ic=ic, tb=tb, tpar=tpar, m_ev=m_ptrb_ev[seq]):
                        g.wait_ge(CV.sem, m_ev)
                        g.dma_start(out=pqd[tb, ic],
                                    in_=pqstage[:, tpar * 512:(tpar + 1) * 512]
                                    ).then_inc(s_pqs, 16)
                    n_pqs[0] += 1
                thunks.append(unit)
            return thunks

        qth = []
        for tb in range(NTB + 1):
            if tb < NTB:
                par = tb % 2
                for span in range(8):
                    sidx = tb * 8 + span
                    spar = sidx % 2

                    @blk.sync
                    def _(sy, tb=tb, span=span, sidx=sidx, spar=spar):
                        prev = m_guuse.get(sidx - 2)
                        if prev:
                            sy.wait_ge(CT.sem, prev)
                        sy.dma_start(
                            out=wgu[:, (2 * spar) * 4096:(2 * spar) * 4096 + 4096],
                            in_=wgsp_d[0, span]).then_inc(s_gu[spar], 16)

                    @blk.gpsimd
                    def _(g, tb=tb, span=span, sidx=sidx, spar=spar):
                        prev = m_guuse.get(sidx - 2)
                        if prev:
                            g.wait_ge(CT.sem, prev)
                        g.dma_start(
                            out=wgu[:, (2 * spar + 1) * 4096:
                                    (2 * spar + 1) * 4096 + 4096],
                            in_=wgsp_d[1, span]).then_inc(s_gu2[spar], 16)

                    for m in range(4):
                        mpar = (span * 4 + m) % 2
                        tok = tb * 4 + m

                        @blk.tensor
                        def _(te, tb=tb, span=span, m=m, sidx=sidx, spar=spar,
                              mpar=mpar, tok=tok):
                            te.wait_ge(s_gu[spar], 16 * (sidx // 2 + 1))
                            te.wait_ge(s_gu2[spar], 16 * (sidx // 2 + 1))
                            pm = m_silu.get(mpar)
                            if pm:
                                te.wait_ge(CA.sem, pm[1])
                            for ic in range(8):
                                lhsT = y1qfm[:, ic * SO + tok * 128:
                                             ic * SO + (tok + 1) * 128]
                                rhs = wgu[:, (2 * spar) * 4096 + ic * 512:
                                          (2 * spar) * 4096 + (ic + 1) * 512]
                                te.matmul(pg[mpar][:], lhsT, rhs, start=(ic == 0),
                                          stop=(ic == 7))
                            for ic in range(8):
                                lhsT = y1qfm[:, ic * SO + tok * 128:
                                             ic * SO + (tok + 1) * 128]
                                rhs = wgu[:, (2 * spar + 1) * 4096 + ic * 512:
                                          (2 * spar + 1) * 4096 + (ic + 1) * 512]
                                mmi = te.matmul(pu[mpar][:], lhsT, rhs,
                                                start=(ic == 0), stop=(ic == 7))
                            m_guuse[sidx] = CT(mmi)
                        m_mmgu = CT.n

                        @blk.scalar
                        def _(sc, tb=tb, m=m, mpar=mpar, tok=tok, m_mmgu=m_mmgu):
                            sc.wait_ge(CT.sem, m_mmgu)
                            pv_ = m_produ.get(mpar)
                            if pv_:
                                sc.wait_ge(CV.sem, pv_)
                            CA(sc.activation(gsb[:, mpar * 512:(mpar + 1) * 512],
                                             pg[mpar][:], AF.Silu, bias=0.0,
                                             scale=rsg[:, tok:tok + 1]))
                            CA(sc.activation(usb[:, mpar * 512:(mpar + 1) * 512],
                                             pu[mpar][:], AF.Copy, bias=0.0,
                                             scale=rsu[:, tok:tok + 1]))
                            m_silu[mpar] = (CA.n - 1, CA.n)

                        @blk.vector
                        def _(vv, tb=tb, span=span, m=m, mpar=mpar, par=par):
                            vv.wait_ge(CA.sem, m_silu[mpar][1])
                            m_produ[mpar] = CV(vv.tensor_tensor(
                                prodb[:, (par * 4 + m) * INNER + span * 512:
                                      (par * 4 + m) * INNER + (span + 1) * 512],
                                gsb[:, mpar * 512:(mpar + 1) * 512],
                                usb[:, mpar * 512:(mpar + 1) * 512], ALU.mult))

                        k = (len(qth) + (31 - (span * 4 + m))) // (32 - (span * 4 + m))
                        for _ in range(min(k, len(qth))):
                            qth.pop(0)()
            while qth:
                qth.pop(0)()
            if tb < NTB:
                qth = mk_quant_thunks(tb) + mk_transpose_thunks(tb)

        @blk.sync
        def _(sy):
            sy.wait_ge(s_pqs, 16 * n_pqs[0])
            sy.nop().then_inc(s_bar1, 1)

        @blk.gpsimd
        def _(g):
            CP.gate(g.nop()).then_inc(s_bar1, 1)

        @blk.vector
        def _(vv):
            CV.gate(vv.nop()).then_inc(s_bar1, 1)

        @blk.scalar
        def _(sc):
            CA.gate(sc.nop()).then_inc(s_bar1, 1)

        @blk.tensor
        def _(te):
            CT.gate(te.nop()).then_inc(s_bar1, 1)

    p4p.close()
    p4.close()

    p5 = contextlib.ExitStack()
    sb5 = lambda n, sh, dt: p5.enter_context(nc.sbuf_tensor(n, sh, dt))
    wdT = sb5("wdT", [128, 32 * D], BF16)
    pqsl = sb5("pqsl", [128, 8 * 512], BF16)
    y1sl = sb5("y1sl", [128, 2 * D], BF16)
    y2b = sb5("y2b", [128, 2 * D], F32)
    h5 = sb5("h5", [128, 2 * D], F32)
    y1f5 = sb5("y1f5", [128, 2 * D], F32)
    sqj5 = sb5("sqj5", [128, D], F32)
    outb = sb5("outb", [128, 2 * D], F32)
    p5p = contextlib.ExitStack()
    y2ps = [p5p.enter_context(nc.psum_tensor(f"y2ps{m}", [128, 1024], F32))
            for m in range(4)]

    with nc.Block() as blk:
        @blk.sync
        def _(sy):
            sy.wait_ge(s_bar1, 15)
            for g4 in range(8):
                sy.dma_start(out=wdT[:, g4 * 4 * D:(g4 + 1) * 4 * D],
                             in_=wdsp_d[g4]).then_inc(s_wdt, 16)

        @blk.vector
        def _(vv):
            vv.wait_ge(s_bar1, 15)
            for m in range(NTO):
                CV(vv.tensor_tensor(rs2w[:, m:m + 1], rs2[:, m:m + 1], binv[:, 2:3],
                                    ALU.mult))
        m_rs2w = CV.n

        @blk.scalar
        def _(sc):
            sc.wait_ge(s_bar1, 15)
            CA(load_act_set(sc, nc, SET_LNEXP))

        @blk.tensor
        def _(te):
            te.wait_ge(s_bar1, 15)

        @blk.gpsimd
        def _(g):
            g.wait_ge(s_bar1, 15)

        n_pql = [0, 0]
        n_y1l = [0, 0]
        n_out = [0]
        m_pq_use = {}
        m_evy2 = {}
        m_y1use = {}
        m_y2use = {}
        m_v5d = {}
        for tb in range(NTB):
            for icg in range(8):
                gpar = icg % 2

                @blk.sync
                def _(sy, tb=tb, icg=icg, gpar=gpar):
                    prev = m_pq_use.get((tb * 8 + icg) - 2)
                    if prev:
                        sy.wait_ge(CT.sem, prev)
                    sy.dma_start(
                        out=pqsl[:, gpar * 2048:(gpar + 1) * 2048].rearrange(
                            "p (g t) -> p g t", g=4),
                        in_=pqd[tb, icg * 4:(icg + 1) * 4].rearrange("g p t -> p g t")
                    ).then_inc(s_pq2[gpar], 16)
                n_pql[gpar] += 1

                @blk.tensor
                def _(te, tb=tb, icg=icg, gpar=gpar, npl=n_pql[gpar]):
                    te.wait_ge(s_pq2[gpar], 16 * npl)
                    if tb == 0:
                        te.wait_ge(s_wdt, 16 * (icg + 1))
                    if icg == 0:
                        for mm in range(4):
                            pm = m_evy2.get(mm)
                            if pm:
                                te.wait_ge(CA.sem, pm)
                    for sub in range(4):
                        ic = icg * 4 + sub
                        for m in range(4):
                            lhsT = pqsl[:, gpar * 2048 + sub * 512 + m * 128:
                                        gpar * 2048 + sub * 512 + (m + 1) * 128]
                            te.matmul(y2ps[m][:, 0:512], lhsT,
                                      wdT[:, ic * D:ic * D + 512],
                                      start=(ic == 0), stop=(ic == 31))
                            mmi = te.matmul(y2ps[m][:, 512:1024], lhsT,
                                            wdT[:, ic * D + 512:(ic + 1) * D],
                                            start=(ic == 0), stop=(ic == 31))
                    m_pq_use[tb * 8 + icg] = CT(mmi)
            m_dn = CT.n

            for m in range(4):
                tok = tb * 4 + m
                e5 = tok % 2
                y2s = y2b[:, e5 * D:(e5 + 1) * D]
                h5s = h5[:, e5 * D:(e5 + 1) * D]
                y1s = y1f5[:, e5 * D:(e5 + 1) * D]

                @blk.gpsimd
                def _(g, tok=tok, e5=e5, m_dn=m_dn):
                    if tok >= 2:
                        g.wait_ge(CV.sem, m_y1use[tok - 2])
                    g.dma_start(out=y1sl[:, e5 * D:(e5 + 1) * D],
                                in_=y1_dram[tok * 128:(tok + 1) * 128, :]
                                ).then_inc(s_y1l[e5], 16)
                n_y1l[e5] += 1

                @blk.scalar
                def _(sc, tb=tb, m=m, tok=tok, e5=e5, m_dn=m_dn, y2s=y2s):
                    sc.wait_ge(CT.sem, m_dn)
                    if tok == 0:
                        sc.wait_ge(CV.sem, m_rs2w)
                    pv_ = m_y2use.get(tok - 2)
                    if pv_:
                        sc.wait_ge(CV.sem, pv_)
                    CA(sc.activation(y2s, y2ps[m][:], AF.Copy, bias=0.0,
                                     scale=rs2w[:, tok:tok + 1]))
                m_y2e = CA.n
                m_evy2[m] = m_y2e

                @blk.vector
                def _(vv, tok=tok, e5=e5, m_y2e=m_y2e, nyl=n_y1l[e5],
                      y2s=y2s, h5s=h5s, y1s=y1s):
                    vv.wait_ge(CA.sem, m_y2e)
                    vv.wait_ge(s_y1l[e5], 16 * nyl)
                    CV(vv.tensor_copy(y1s, y1sl[:, e5 * D:(e5 + 1) * D]))
                    m_y1use[tok] = CV.n
                    CV(vv.tensor_tensor(h5s, y2s, y1s, ALU.add))
                    m_y2use[tok] = CV.n
                    CV(vv.tensor_scalar(y1s, h5s, 1.0, None, ALU.mult, ALU.add,
                                        accum_out=st[:, e5:e5 + 1]))
                m_h5 = CV.n

                @blk.scalar
                def _(sc, tok=tok, e5=e5, m_h5=m_h5, h5s=h5s):
                    sc.wait_ge(CV.sem, m_h5)
                    pv_ = m_v5d.get(tok - 2)
                    if pv_:
                        sc.wait_ge(CV.sem, pv_)
                    CA(sc.activation(sqj5[:], h5s, AF.Square, bias=0.0, scale=1.0,
                                     accum_out=st[:, 2 + e5:3 + e5]))
                m_sq5 = CA.n

                @blk.vector
                def _(vv, tok=tok, e5=e5, m_sq5=m_sq5):
                    vv.wait_ge(CA.sem, m_sq5)
                    CV(vv.tensor_scalar(st[:, 10:11], st[:, 2 + e5:3 + e5], 1.0 / D,
                                        None, ALU.mult))
                    CV(vv.tensor_scalar(st[:, 4 + e5:5 + e5], st[:, e5:e5 + 1],
                                        1.0 / D, None, ALU.mult))
                    CV(vv.tensor_tensor(st[:, 11:12], st[:, 4 + e5:5 + e5],
                                        st[:, 4 + e5:5 + e5], ALU.mult))
                    CV(vv.tensor_tensor(st[:, 8 + e5:9 + e5], st[:, 10:11],
                                        st[:, 11:12], ALU.subtract))
                m_v5 = CV.n
                m_v5d[tok] = m_v5

                @blk.scalar
                def _(sc, e5=e5, m_v5=m_v5):
                    sc.wait_ge(CV.sem, m_v5)
                    CA(sc.activation(st[:, 12:13], st[:, 8 + e5:9 + e5], AF.Ln,
                                     bias=cbias[:, 1:2], scale=1.0))
                    CA(sc.activation(st[:, 6 + e5:7 + e5], st[:, 12:13], AF.Exp,
                                     bias=0.0, scale=-0.5))
                m_r5 = CA.n

                @blk.vector
                def _(vv, tok=tok, e5=e5, m_r5=m_r5, nout=n_out[0], h5s=h5s):
                    vv.wait_ge(CA.sem, m_r5)
                    if tok >= 2:
                        vv.wait_ge(s_out, 16 * nout)
                    CV(vv.tensor_scalar(outb[:, e5 * D:(e5 + 1) * D], h5s,
                                        st[:, 4 + e5:5 + e5], st[:, 6 + e5:7 + e5],
                                        ALU.subtract, ALU.mult))
                m_o5 = CV.n

                @blk.sync
                def _(sy, tok=tok, e5=e5, m_o5=m_o5):
                    sy.wait_ge(CV.sem, m_o5)
                    sy.dma_start(out=out_d[tok * 128:(tok + 1) * 128, :],
                                 in_=outb[:, e5 * D:(e5 + 1) * D]
                                 ).then_inc(s_out, 16)
                n_out[0] += 1

        @blk.sync
        def _(sy):
            sy.wait_ge(s_out, 16 * NTO)

    p5p.close()
    p5.close()
    es.close()
    return nc


_CACHE = {}


def _program(debug=()):
    key = tuple(sorted(debug))
    if key not in _CACHE:
        _CACHE[key] = build(debug)
    return _CACHE[key]


def kernel(L0_attn_x, L0_attn_ln_scale, L0_attn_ln_bias, L0_attn_wq, L0_attn_wk,
           L0_attn_wv, L0_ffn_w_gate, L0_ffn_w_up, L0_ffn_w_down, _debug=()):
    x = np.ascontiguousarray(np.asarray(L0_attn_x, dtype=np.float32))
    wq = np.ascontiguousarray(np.asarray(L0_attn_wq, dtype=np.float32))
    wk = np.ascontiguousarray(np.asarray(L0_attn_wk, dtype=np.float32))
    wv = np.ascontiguousarray(np.asarray(L0_attn_wv, dtype=np.float32))
    lnsc = np.ascontiguousarray(np.asarray(L0_attn_ln_scale, np.float32)).reshape(1, D)
    lnbi = np.ascontiguousarray(np.asarray(L0_attn_ln_bias, np.float32)).reshape(1, D)
    wg = np.ascontiguousarray(np.asarray(L0_ffn_w_gate, dtype=np.float32))
    wu = np.ascontiguousarray(np.asarray(L0_ffn_w_up, dtype=np.float32))
    wd = np.ascontiguousarray(np.asarray(L0_ffn_w_down, dtype=np.float32))
    idb = np.eye(128, dtype=BFD)
    idf = np.eye(128, dtype=np.float32)

    nc = _program(_debug)
    in_maps = []
    for c in range(8):
        b, h = c // 2, c % 2
        own = x[b, h * SO:(h + 1) * SO, :]
        ctx = x[b, (1 - h) * SO:(2 - h) * SO, :]
        xc = np.concatenate([own, ctx], axis=0)
        in_maps.append({"x": xc, "wq": wq, "wk": wk, "wv": wv,
                        "lnsc": lnsc, "lnbi": lnbi,
                        "wg": wg, "wu": wu, "wd": wd,
                        "idb": idb, "idf": idf})
    res = run_bass_kernel_spmd(nc, in_maps, core_ids=list(range(8)))
    out = np.empty((4, S, D), dtype=np.float32)
    for c in range(8):
        b, h = c // 2, c % 2
        out[b, h * SO:(h + 1) * SO, :] = res.results[c]["out"]
    if _debug:
        return out, [res.results[c] for c in range(8)]
    return out


# version 22
# speedup vs baseline: 1.0449x; 1.0157x over previous
import contextlib
import numpy as np
import ml_dtypes
import concourse.bass as bass
import concourse.mybir as mybir
from concourse.bass_utils import run_bass_kernel_spmd

F32 = mybir.dt.float32
BF16 = mybir.dt.bfloat16
I32 = mybir.dt.int32
AF = mybir.ActivationFunctionType
ALU = mybir.AluOpType
AX = mybir.AxisListType.X
BFD = ml_dtypes.bfloat16

D = 1024
S = 4096
SO = 2048
H = 16
HD = 64
NHP = 8
NT = 32
NTO = 16
QB = 4
KC = 32
INNER = 4096
NTB = 4
GRP = [(i, 1) for i in range(32)]
MAGIC = 256.5
SET_LNEXP = 6
SET_SILU = 18


def load_act_set(sc, nc, set_id):
    return sc.add_instruction(mybir.InstLoadActFuncSet(
        name=nc.get_next_instruction_name(), act_func_set_id=set_id, ins=[], outs=[]))


class Ch:

    def __init__(self, sem, serialize=True):
        self.sem = sem
        self.serialize = serialize
        self.n = 0

    def __call__(self, inst):
        if self.serialize and self.n:
            inst._wait_ge(self.sem, self.n)
        inst.then_inc(self.sem, 1)
        self.n += 1
        return self.n

    def gate(self, inst):
        if self.n:
            inst._wait_ge(self.sem, self.n)
        return inst


def build(debug=()):
    nc = bass.Bass("TRN2", target_bir_lowering=False,
                   detect_race_conditions=False)
    x_d = nc.dram_tensor("x", [S, D], F32, kind="ExternalInput").ap()
    wq_d = nc.dram_tensor("wq", [D, D], F32, kind="ExternalInput").ap()
    wk_d = nc.dram_tensor("wk", [D, D], F32, kind="ExternalInput").ap()
    wv_d = nc.dram_tensor("wv", [D, D], F32, kind="ExternalInput").ap()
    lnsc_d = nc.dram_tensor("lnsc", [1, D], F32, kind="ExternalInput").ap()
    lnbi_d = nc.dram_tensor("lnbi", [1, D], F32, kind="ExternalInput").ap()
    wg_d = nc.dram_tensor("wg", [INNER, D], F32, kind="ExternalInput").ap()
    wu_d = nc.dram_tensor("wu", [INNER, D], F32, kind="ExternalInput").ap()
    wd_d = nc.dram_tensor("wd", [D, INNER], F32, kind="ExternalInput").ap()
    idb_d = nc.dram_tensor("idb", [128, 128], BF16, kind="ExternalInput").ap()
    idf_d = nc.dram_tensor("idf", [128, 128], F32, kind="ExternalInput").ap()
    out_d = nc.dram_tensor("out", [SO, D], F32, kind="ExternalOutput").ap()

    def scratch(name, shape, dtype):
        kind = "ExternalOutput" if name in debug else "Internal"
        return nc.dram_tensor(name, shape, dtype, kind=kind).ap()

    z_dram = scratch("zd", [S, D], BF16)
    wg_nat = scratch("wgn", [INNER, D], BF16)
    wu_nat = scratch("wun", [INNER, D], BF16)
    wd_nat = scratch("wdn", [D, INNER], BF16)
    wgsp_d = nc.dram_tensor("wgsp", [2, 8, 128, 8 * 512], BF16, kind="Internal").ap()
    wdsp_d = nc.dram_tensor("wdsp", [8, 128, 4 * D], BF16, kind="Internal").ap()
    pqd = scratch("pqd", [NTB, 32, 128, 512], BF16)
    y1_dram = scratch("y1d", [SO, D], BF16)
    y1q_dram = scratch("y1qd", [SO, D], BF16)
    atn_dbg = scratch("atn", [SO, D], BF16) if "atn" in debug else None
    q_sp = scratch("qsp", [NHP, 128, SO], BF16)
    k_sp = scratch("ksp", [NHP, 128, S], BF16)
    v_sp = scratch("vsp", [NHP, 128, KC * 130], BF16)
    sres_dbg = scratch("sres", [SO, D], F32) if "sres" in debug else None
    n_dbg = [0]

    es = contextlib.ExitStack()
    sem = lambda n: es.enter_context(nc.semaphore(n))

    CV = Ch(sem("c_v"))
    CA = Ch(sem("c_a"))
    CP = Ch(sem("c_p"))
    CT = Ch(sem("c_t"), serialize=False)

    s_cst = sem("s_cst")
    s_cst2 = sem("s_cst2")
    s_w0 = sem("s_w0")
    s_xs = [sem(f"s_xs{i}") for i in range(3)]
    s_zds = [sem(f"s_zd{i}") for i in range(2)]
    s_zfm = sem("s_zfm")
    s_qksp = sem("s_qksp")
    s_vsp = sem("s_vsp")
    s_bar1 = sem("s_bar1")
    s_out = sem("s_out")
    s_dbg = sem("s_dbg")
    s_kq = [sem(f"s_kq{i}") for i in range(3)]
    s_vv = [sem(f"s_vv{i}") for i in range(2)]
    s_wp = sem("s_wp")
    s_wpb = sem("s_wpb")
    s_gu2 = [sem(f"s_gu2{i}") for i in range(2)]
    s_wq = sem("s_wq")
    s_wt = sem("s_wt")
    s_wt2 = sem("s_wt2")
    s_y1s = [sem(f"s_y1s{i}") for i in range(2)]
    s_y1t = sem("s_y1t")
    s_gu = [sem(f"s_gu{i}") for i in range(2)]
    s_pqs = sem("s_pqs")
    s_wdt = sem("s_wdt")
    s_pq2 = [sem(f"s_pq2{i}") for i in range(2)]
    s_y1l = [sem(f"s_y1l{i}") for i in range(2)]

    sbE = lambda n, sh, dt: es.enter_context(nc.sbuf_tensor(n, sh, dt))
    onesr = sbE("onesr", [1, 128], F32)
    onesc = sbE("onesc", [128, 1], F32)
    onesb = sbE("onesb", [128, 1], BF16)
    idb = sbE("idb_sb", [128, 128], BF16)
    idf = sbE("idf_sb", [128, 128], F32)
    cbias = sbE("cbias", [128, 10], F32)
    swcols = sbE("swcols", [128, 4], F32)
    binv = sbE("binv", [128, 4], F32)
    rsx = sbE("rsx", [128, NTO], F32)
    rs2 = sbE("rs2", [128, NTO], F32)
    rsg = sbE("rsg", [128, NTO], F32)
    rsu = sbE("rsu", [128, NTO], F32)
    rs2w = sbE("rs2w", [128, NTO], F32)
    st = sbE("st", [128, 14], F32)
    pA = contextlib.ExitStack()
    sres = pA.enter_context(nc.sbuf_tensor("sres_sb", [128, NTO * D], F32))
    p1 = contextlib.ExitStack()
    sbt = lambda n, sh, dt: p1.enter_context(nc.sbuf_tensor(n, sh, dt))
    pst = lambda n, sh: p1.enter_context(nc.psum_tensor(n, sh, F32))
    gb = sbt("gb", [128, D], F32)
    bbt = sbt("bbt", [128, D], F32)
    wslot = sbt("wslot", [128, 8 * D], BF16)
    xr = sbt("xr", [128, 3 * D], F32)
    xnr = sbt("xnr", [128, 2 * D], F32)
    sqjunk = sbt("sqjunk", [128, D], F32)
    zf32 = sbt("zf32", [128, D], F32)
    zg32 = sbt("zg32", [128, D], F32)
    zfm = sbt("zfm", [128, 8 * S], BF16)
    qkstage = sbt("qkstage", [128, S], BF16)
    vstage = sbt("vstage", [128, 4 * 1040], BF16)
    ps = [pst(f"ps{i}", [128, 512]) for i in range(8)]
    ps_qk, ps_v = ps[:4], ps[4:]

    with nc.Block() as blk:
        @blk.sync
        def _(sy):
            sy.dma_start(out=gb[0:1, :], in_=lnsc_d).then_inc(s_cst, 16)
            sy.dma_start(out=bbt[0:1, :], in_=lnbi_d).then_inc(s_cst, 16)
            sy.dma_start(out=idb[:], in_=idb_d).then_inc(s_cst2, 16)
            sy.dma_start(out=idf[:], in_=idf_d).then_inc(s_cst2, 16)

        @blk.gpsimd
        def _(g):
            for ic in range(8):
                g.dma_start(out=wslot[:, ic * D:(ic + 1) * D],
                            in_=wq_d[ic * 128:(ic + 1) * 128, :]).then_inc(s_w0, 16)

        @blk.vector
        def _(vv):
            CV(vv.memset(onesr[:], 1.0))
            CV(vv.memset(onesc[:], 1.0))
            CV(vv.memset(onesb[:], 1.0))
            CV(vv.memset(cbias[:, 0:1], 1e-6))
            CV(vv.memset(cbias[:, 1:2], 1e-5))
            CV(vv.memset(cbias[:, 2:3], -257.0))
            CV(vv.memset(cbias[:, 3:4], -256.0))
            CV(vv.memset(cbias[:, 4:5], 1e-4))
            CV(vv.memset(cbias[:, 5:6], -256.5))
            CV(vv.memset(cbias[:, 6:7], -1.0))
            CV(vv.memset(cbias[:, 7:8], -0.5))
            CV(vv.memset(cbias[:, 8:9], 0.5))
            for sl in range(4):
                CV(vv.memset(vstage[:, sl * 1040:(sl + 1) * 1040].rearrange(
                    "p (h c) -> p h c", h=16)[:, :, 64:65], 1.0))
        m_consts = CV.n

        @blk.scalar
        def _(sc):
            CA(load_act_set(sc, nc, SET_LNEXP))

        @blk.tensor
        def _(te):
            te.wait_ge(s_cst, 32)
            te.wait_ge(CV.sem, m_consts)
            te.matmul(ps[0][:], onesr[:], gb[0:1, 0:512], start=True, stop=True)
            te.matmul(ps[1][:], onesr[:], gb[0:1, 512:1024], start=True, stop=True)
            te.matmul(ps[2][:], onesr[:], bbt[0:1, 0:512], start=True, stop=True)
            CT(te.matmul(ps[3][:], onesr[:], bbt[0:1, 512:1024], start=True, stop=True))
        m_bc = CT.n

        @blk.vector
        def _(vv):
            vv.wait_ge(CT.sem, m_bc)
            CV(vv.tensor_copy(gb[:, 0:512], ps[0][:]))
            CV(vv.tensor_copy(gb[:, 512:1024], ps[1][:]))
            CV(vv.tensor_copy(bbt[:, 0:512], ps[2][:]))
            CV(vv.tensor_copy(bbt[:, 512:1024], ps[3][:]))
        m_bce = CV.n

        m_xn = [0] * NT
        m_xuse = [0] * NT
        m_sq = [0] * NT
        m_var = [0] * NT
        m_rstd = [0] * NT
        for i in range(NT):
            r, e = i % 3, i % 2

            @blk.sync
            def _(sy, i=i, r=r):
                if i >= 3:
                    sy.wait_ge(CV.sem, m_xuse[i - 3])
                    sy.wait_ge(CA.sem, m_sq[i - 3])
                sy.dma_start(out=xr[:, r * D:(r + 1) * D],
                             in_=x_d[i * 128:(i + 1) * 128, :]).then_inc(s_xs[r], 16)

            @blk.vector
            def _(vv, i=i, r=r, e=e):
                xt = xr[:, r * D:(r + 1) * D]
                vv.wait_ge(s_xs[r], 16 * (i // 3 + 1))
                CV(vv.tensor_scalar(zf32[:], xt, 1.0, None, ALU.mult, ALU.add,
                                    accum_out=st[:, e:e + 1]))

            @blk.scalar
            def _(sc, i=i, r=r, e=e):
                sc.wait_ge(s_xs[r], 16 * (i // 3 + 1))
                if i >= 2:
                    sc.wait_ge(CV.sem, m_var[i - 2])
                m_sq[i] = CA(sc.activation(sqjunk[:], xr[:, r * D:(r + 1) * D],
                                           AF.Square, bias=0.0, scale=1.0,
                                           accum_out=st[:, 2 + e:3 + e]))

            @blk.vector
            def _(vv, i=i, e=e):
                vv.wait_ge(CA.sem, m_sq[i])
                if i >= 1:
                    vv.wait_ge(CA.sem, m_rstd[i - 1])
                CV(vv.tensor_scalar(st[:, 10:11], st[:, 2 + e:3 + e], 1.0 / D, None,
                                    ALU.mult))
                CV(vv.tensor_scalar(st[:, 4 + e:5 + e], st[:, e:e + 1], 1.0 / D, None,
                                    ALU.mult))
                CV(vv.tensor_tensor(st[:, 11:12], st[:, 4 + e:5 + e],
                                    st[:, 4 + e:5 + e], ALU.mult))
                m_var[i] = CV(vv.tensor_tensor(st[:, 8 + e:9 + e], st[:, 10:11],
                                               st[:, 11:12], ALU.subtract))

            @blk.scalar
            def _(sc, i=i, e=e):
                sc.wait_ge(CV.sem, m_var[i])
                if i >= 2:
                    sc.wait_ge(CV.sem, m_xuse[i - 2])
                CA(sc.activation(st[:, 12:13], st[:, 8 + e:9 + e], AF.Ln,
                                 bias=cbias[:, 0:1], scale=1.0))
                m_rstd[i] = CA(sc.activation(st[:, 6 + e:7 + e], st[:, 12:13], AF.Exp,
                                             bias=0.0, scale=-0.5))

            @blk.vector
            def _(vv, i=i, r=r, e=e):
                xt = xr[:, r * D:(r + 1) * D]
                xnt = xnr[:, (i % 2) * D:(i % 2 + 1) * D]
                vv.wait_ge(CA.sem, m_rstd[i])
                CV(vv.tensor_scalar(zf32[:], xt, st[:, 4 + e:5 + e],
                                    st[:, 6 + e:7 + e], ALU.subtract, ALU.mult))
                CV(vv.tensor_tensor(zg32[:], zf32[:], gb[:], ALU.mult))
                if i >= 2:
                    vv.wait_ge(s_zds[i % 2], 16 * (i // 2))
                m_xn[i] = CV(vv.tensor_tensor(xnt, zg32[:], bbt[:], ALU.add))
                if i < NTO:
                    CV(vv.tensor_tensor(sres[:, i * D:(i + 1) * D], xt, xnt, ALU.add))
                m_xuse[i] = CV.n

            @blk.gpsimd
            def _(g, i=i):
                g.wait_ge(CV.sem, m_xn[i])
                g.dma_start(out=z_dram[i * 128:(i + 1) * 128, :],
                            in_=xnr[:, (i % 2) * D:(i % 2 + 1) * D]
                            ).then_inc(s_zds[i % 2], 16)

        if sres_dbg is not None:
            @blk.gpsimd
            def _(g):
                g.wait_ge(CV.sem, m_xuse[NT - 1])
                for i in range(NTO):
                    g.dma_start(out=sres_dbg[i * 128:(i + 1) * 128, :],
                                in_=sres[:, i * D:(i + 1) * D]).then_inc(s_dbg, 16)
                    n_dbg[0] += 1

        @blk.sync
        def _(sy):
            sy.wait_ge(s_zds[0], 16 * (NT // 2))
            sy.wait_ge(s_zds[1], 16 * (NT // 2))
            for ic in range(8):
                sy.dma_start_transpose(zfm[:, ic * S:(ic + 1) * S],
                                       z_dram[:, ic * 128:(ic + 1) * 128]
                                       ).then_inc(s_zfm, 16)

        qk_jobs = [("q", oc, col) for oc in range(8) for col in range(4)] + \
                  [("k", oc, col) for oc in range(8) for col in range(8)]
        v_jobs = [(tt, nb) for tt in range(NT) for nb in range(2)]
        m_mmqk = [0] * len(qk_jobs)
        m_evqk = [0] * len(qk_jobs)
        m_mmv = [0] * len(v_jobs)
        m_evv = [0] * len(v_jobs)

        @blk.gpsimd
        def _(g):
            pass

        mark_q_done = [0]
        for j, (kind, oc, col) in enumerate(qk_jobs):
            @blk.tensor
            def _(te, j=j, kind=kind, oc=oc, col=col):
                if j == 0:
                    te.wait_ge(s_zfm, 16 * 8)
                    te.wait_ge(CV.sem, m_bce)
                te.wait_ge(s_w0, 16 * 8 * (1 if kind == "q" else 2))
                if j >= 8:
                    te.wait_ge(CA.sem, m_evqk[j - 8])
                p = ps[j % 8]
                for ic in range(8):
                    lhsT = wslot[:, ic * D + oc * 128:ic * D + (oc + 1) * 128]
                    rhs = zfm[:, ic * S + col * 512:ic * S + (col + 1) * 512]
                    mm = te.matmul(p[:], lhsT, rhs, start=(ic == 0), stop=(ic == 7))
                m_mmqk[j] = CT(mm)

            if kind == "q" and j == 31:
                mark_q_done[0] = m_mmqk[j]

                @blk.gpsimd
                def _(g, mk=m_mmqk[j]):
                    g.wait_ge(CT.sem, mk)
                    for ic in range(8):
                        g.dma_start(out=wslot[:, ic * D:(ic + 1) * D],
                                    in_=wk_d[ic * 128:(ic + 1) * 128, :]
                                    ).then_inc(s_w0, 16)

            @blk.scalar
            def _(sc, j=j, kind=kind, oc=oc, col=col):
                sc.wait_ge(CT.sem, m_mmqk[j])
                if col == 0:
                    nprev = oc if kind == "q" else 8 + oc
                    if nprev:
                        sc.wait_ge(s_qksp, 16 * nprev)
                m_evqk[j] = CA(sc.copy(qkstage[:, col * 512:(col + 1) * 512],
                                       ps[j % 8][:]))

            last_col = 3 if kind == "q" else 7
            if col == last_col:
                @blk.sync
                def _(sy, j=j, kind=kind, oc=oc):
                    sy.wait_ge(CA.sem, m_evqk[j])
                    dst = q_sp[oc] if kind == "q" else k_sp[oc]
                    src = qkstage[:, 0:SO] if kind == "q" else qkstage[:, 0:S]
                    sy.dma_start(out=dst, in_=src).then_inc(s_qksp, 16)

        @blk.gpsimd
        def _(g, mk=m_mmqk[-1]):
            g.wait_ge(CT.sem, mk)
            for ic in range(8):
                g.dma_start(out=wslot[:, ic * D:(ic + 1) * D],
                            in_=wv_d[ic * 128:(ic + 1) * 128, :]).then_inc(s_w0, 16)

        for jv, (tt, nb) in enumerate(v_jobs):
            @blk.tensor
            def _(te, jv=jv, tt=tt, nb=nb):
                if jv == 0:
                    te.wait_ge(s_w0, 16 * 24)
                if jv < 8:
                    te.wait_ge(CA.sem, m_evqk[-1])
                else:
                    te.wait_ge(CV.sem, m_evv[jv - 8])
                p = ps[jv % 8]
                for ic in range(8):
                    lhsT = zfm[:, ic * S + tt * 128:ic * S + (tt + 1) * 128]
                    rhs = wslot[:, ic * D + nb * 512:ic * D + (nb + 1) * 512]
                    mm = te.matmul(p[:], lhsT, rhs, start=(ic == 0), stop=(ic == 7))
                m_mmv[jv] = CT(mm)

            @blk.vector
            def _(vv, jv=jv, tt=tt, nb=nb):
                vv.wait_ge(CT.sem, m_mmv[jv])
                if tt >= 4 and nb == 0:
                    vv.wait_ge(s_vsp, 16 * 8 * (tt // 4))
                base = (tt % 4) * 1040 + nb * 520
                dst = vstage[:, base:base + 520].rearrange(
                    "p (h c) -> p h c", h=8)[:, :, 0:64]
                src = ps[jv % 8][:].rearrange("p (h c) -> p h c", h=8)
                m_evv[jv] = CV(vv.tensor_copy(dst, src))

            if nb == 1 and tt % 4 == 3:
                @blk.sync
                def _(sy, jv=jv, tt=tt):
                    sy.wait_ge(CV.sem, m_evv[jv])
                    grp = tt // 4
                    for hp in range(NHP):
                        src = vstage[:].rearrange("p (t c) -> p t c", t=4)[
                            :, :, hp * 130:(hp + 1) * 130]
                        dst = v_sp[hp][:, grp * 4 * 130:(grp + 1) * 4 * 130]
                        sy.dma_start(out=dst, in_=src).then_inc(s_vsp, 16)

        @blk.sync
        def _(sy):
            sy.wait_ge(s_qksp, 16 * 16)
            sy.wait_ge(s_vsp, 16 * 64)
            sy.nop().then_inc(s_bar1, 1)

        @blk.gpsimd
        def _(g):
            CP.gate(g.nop()).then_inc(s_bar1, 1)

        @blk.vector
        def _(vv):
            CV.gate(vv.nop()).then_inc(s_bar1, 1)

        @blk.scalar
        def _(sc):
            CA.gate(sc.nop()).then_inc(s_bar1, 1)

        @blk.tensor
        def _(te):
            CT.gate(te.nop()).then_inc(s_bar1, 1)

    p1.close()

    p2a = contextlib.ExitStack()
    attn_sb = p2a.enter_context(nc.sbuf_tensor("attn_sb", [128, NTO * D], BF16))
    p2 = contextlib.ExitStack()
    sb2 = lambda n, sh, dt: p2.enter_context(nc.sbuf_tensor(n, sh, dt))
    ps2 = lambda n, sh: p2.enter_context(nc.psum_tensor(n, sh, F32))
    kslot = sb2("kslot", [128, 3 * S], BF16)
    vslot = sb2("vslot", [128, 2 * KC * 130], BF16)
    qslot = sb2("qslot", [128, 3 * SO], BF16)
    pt = sb2("pt", [128, 4 * 512], BF16)
    sqbuf = sb2("sqbuf", [128, S], F32)
    csq2 = sb2("csq2", [128, SO], BF16)
    pvf = sb2("pvf", [128, 2 * 512], F32)
    rlb = sb2("rlb", [128, 2 * 4], F32)
    cbc = sb2("cbc", [128, 2], F32)
    srow = sb2("srow", [1, 8], F32)
    acc3 = sb2("acc3", [128, 4], F32)
    wbuf = sb2("wbuf", [128, 2048], F32)
    abuf = sb2("abuf", [128, 2048], BF16)
    wib = sb2("wib", [128, 2048], BF16)
    tstage = sb2("tstage", [128, S], BF16)
    sTp = [ps2(f"sT{s}", [128, 1024]) for s in range(3)]
    sT1 = sTp[0]
    pvps = [ps2(f"pvps{h}", [128, 512]) for h in range(2)]

    with nc.Block() as blk:
        @blk.sync
        def _(sy):
            sy.wait_ge(s_bar1, 5)

        @blk.vector
        def _(vv):
            vv.wait_ge(s_bar1, 5)
            CV(vv.memset(acc3[:], 0.0))

        @blk.scalar
        def _(sc):
            sc.wait_ge(s_bar1, 5)

        @blk.tensor
        def _(te):
            te.wait_ge(s_bar1, 5)
            te.wait_ge(s_cst2, 32)

        @blk.gpsimd
        def _(g):
            g.wait_ge(s_bar1, 5)

        wsrcs = [wg_d, wu_d, wd_d]
        wnats = [wg_nat, wu_nat, wd_nat]
        NW = 4194304.0
        n_wp = [0]
        n_wq = [0]
        n_wt = [0]
        n_wt2 = [0]
        m_wbuf = [0]
        m_wib = [0]

        m_wslot = [0, 0]
        n_piece = [0]
        n_wpb = [0]

        def wchunk_ap(wd_ap, c):
            flat = wd_ap.rearrange("r d -> (r d)")
            return flat[c * 131072:(c + 1) * 131072].rearrange("(p v) -> p v", v=1024)

        def wsl(sl):
            return wbuf[:, sl * 1024:(sl + 1) * 1024]

        def wibsl(sl):
            return wib[:, sl * 1024:(sl + 1) * 1024]

        def emit_jobA(w, c):
            sl = n_piece[0] % 2
            n_piece[0] += 1

            @blk.gpsimd
            def _(g, w=w, c=c, sl=sl, gate=m_wslot[sl]):
                g.wait_ge(CV.sem, gate)
                g.dma_start(out=wsl(sl), in_=wchunk_ap(wsrcs[w], c)).then_inc(s_wp, 16)
            n_wp[0] += 1

            @blk.vector
            def _(vv, w=w, c=c, sl=sl, nwp=n_wp[0]):
                vv.wait_ge(s_wp, 16 * nwp)
                CV(vv.tensor_reduce(acc3[:, 3:4], wsl(sl), AX, ALU.add,
                                    apply_absolute_value=True))
                CV(vv.tensor_tensor(acc3[:, w:w + 1], acc3[:, w:w + 1], acc3[:, 3:4],
                                    ALU.add))
                m_wslot[sl] = CV.n

        def emit_jobF(w, mark_bank_free):
            @blk.tensor
            def _(te, w=w):
                te.wait_ge(CV.sem, max(m_wslot[0], m_wslot[1], mark_bank_free[0]))
                CT(te.matmul(sT1[0:1, 0:1], acc3[:, w:w + 1], onesc[:], start=True,
                             stop=True))
            mk = CT.n

            @blk.vector
            def _(vv, w=w, mk=mk):
                vv.wait_ge(CT.sem, mk)
                CV(vv.tensor_scalar(srow[0:1, 3:4], sT1[0:1, 0:1], 1.0 / NW, None, ALU.mult))
                CV(vv.tensor_scalar(srow[0:1, 3:4], srow[0:1, 3:4], 1e-5, None, ALU.max))
                CV(vv.reciprocal(srow[0:1, w:w + 1], srow[0:1, 3:4]))
                CV(vv.reciprocal(srow[0:1, 4 + w:5 + w], srow[0:1, w:w + 1]))
            mkd = CV.n

            @blk.tensor
            def _(te, w=w, mkd=mkd):
                te.wait_ge(CV.sem, mkd)
                te.matmul(sT1[:, 0:1], onesr[:], srow[0:1, w:w + 1], start=True, stop=True)
                CT(te.matmul(sT1[:, 1:2], onesr[:], srow[0:1, 4 + w:5 + w], start=True,
                             stop=True))
            mk2 = CT.n

            @blk.vector
            def _(vv, w=w, mk2=mk2):
                vv.wait_ge(CT.sem, mk2)
                CV(vv.tensor_copy(swcols[:, w:w + 1], sT1[:, 0:1]))
                CV(vv.tensor_copy(binv[:, w:w + 1], sT1[:, 1:2]))
            mark_bank_free[0] = CV.n

        def emit_jobB(w, c):
            sl = n_piece[0] % 2
            n_piece[0] += 1

            @blk.sync
            def _(sy, w=w, c=c, sl=sl, gate=m_wslot[sl]):
                sy.wait_ge(CV.sem, gate)
                sy.dma_start(out=wsl(sl), in_=wchunk_ap(wsrcs[w], c)
                             ).then_inc(s_wpb, 16)
            n_wpb[0] += 1

            @blk.vector
            def _(vv, w=w, c=c, sl=sl, nwp=n_wpb[0], nwq=n_wq[0]):
                vv.wait_ge(s_wpb, 16 * nwp)
                CV(vv.tensor_scalar(wsl(sl), wsl(sl), swcols[:, w:w + 1], None,
                                    ALU.mult))
                if nwq >= 1:
                    vv.wait_ge(s_wq, 16 * (nwq - 1))
                CV(vv.tensor_scalar(wibsl(sl), wsl(sl), 0.5, None, ALU.is_ge))
                CV(vv.tensor_scalar(abuf[:, 0:1024], wsl(sl), -0.5, None, ALU.is_le))
                m_wslot[sl] = CV.n
                m_wib[0] = CV(vv.tensor_tensor(wibsl(sl), wibsl(sl), abuf[:, 0:1024],
                                               ALU.subtract))

            @blk.gpsimd
            def _(g, w=w, c=c, sl=sl, mwib=m_wib):
                g.wait_ge(CV.sem, mwib[0])
                g.dma_start(out=wchunk_ap(wnats[w], c), in_=wibsl(sl)).then_inc(s_wq, 16)
            n_wq[0] += 1

        def emit_jobD(g4):
            @blk.sync
            def _(sy, g4=g4, nw2=None):
                sy.wait_ge(s_wq, 16 * 32 * 3)
                sy.wait_ge(s_wt2, 16 * n_wt2[0])
                for sub in range(4):
                    ic = g4 * 4 + sub
                    sy.dma_start_transpose(tstage[:, sub * D:(sub + 1) * D],
                                           wd_nat[:, ic * 128:(ic + 1) * 128]
                                           ).then_inc(s_wt, 16)
            n_wt[0] += 4

            @blk.sync
            def _(sy, g4=g4, nwt=n_wt[0]):
                sy.wait_ge(s_wt, 16 * nwt)
                sy.dma_start(out=wdsp_d[g4], in_=tstage[:, 0:4 * D]
                             ).then_inc(s_wt2, 16)
            n_wt2[0] += 1

        def emit_jobC(w, ic):
            @blk.sync
            def _(sy, w=w, ic=ic):
                sy.wait_ge(s_wq, 16 * 32 * (w + 1))
                sy.wait_ge(s_wt2, 16 * n_wt2[0])
                sy.dma_start_transpose(tstage[:], wnats[w][:, ic * 128:(ic + 1) * 128]
                                       ).then_inc(s_wt, 16)
            n_wt[0] += 1

            @blk.sync
            def _(sy, w=w, ic=ic, nwt=n_wt[0]):
                sy.wait_ge(s_wt, 16 * nwt)
                for span in range(8):
                    sy.dma_start(out=wgsp_d[w, span][:, ic * 512:(ic + 1) * 512],
                                 in_=tstage[:, span * 512:(span + 1) * 512]
                                 ).then_inc(s_wt2, 16)
            n_wt2[0] += 8

        jobs = [("A", w, c) for w in range(3) for c in range(32)] + \
               [("B", w, c) for w in range(3) for c in range(32)] + \
               [("C", w, ic) for w in range(2) for ic in range(8)] + \
               [("D", g4, 0) for g4 in range(8)]
        jq = list(jobs)
        f_done = [False]
        bankfree_mark = [0]

        def pop_jobs(k):
            for _ in range(k):
                if not jq:
                    return
                kind, w, c = jq[0]
                if kind == "B" and not f_done[0]:
                    return
                jq.pop(0)
                if kind == "A":
                    emit_jobA(w, c)
                elif kind == "B":
                    emit_jobB(w, c)
                elif kind == "C":
                    emit_jobC(w, c)
                else:
                    emit_jobD(w)

        m_hp_last = [0] * NHP
        last_exp = [0]
        last_exp_set = {}
        last_pv = {}
        m_atm_prev = [0, 0]
        m_tp_prev = [0, 0]
        m_cnorm_read = [0]
        m_cmm_prev = [0]
        m_csq = {}
        m_cb_cur = [0]

        m_cpc = [0]

        def emit_c_piece(hpx, j, s):
            @blk.tensor
            def _(te, hpx=hpx, j=j, s=s):
                te.wait_ge(CV.sem, m_csq[hpx])
                if j < 2:
                    te.wait_ge(CV.sem, m_cnorm_read[0])
                if j < 8:
                    mmi = te.matmul(sTp[s][0:1, 0:512], onesc[:],
                                    sqbuf[:, j * 512:(j + 1) * 512],
                                    start=True, stop=True)
                else:
                    mmi = te.matmul(sTp[s][0:1, 0:512], onesb[:],
                                    csq2[:, (j - 8) * 512:(j - 7) * 512],
                                    start=True, stop=True)
                CT(mmi)
            mpc = CT.n
            m_cmm_prev[0] = mpc

            @blk.vector
            def _(vv, hpx=hpx, j=j, s=s, mpc=mpc):
                vv.wait_ge(CT.sem, mpc)
                if j == 0:
                    CV(vv.tensor_reduce(srow[0:1, 0:1], sTp[s][0:1, 0:512], AX,
                                        ALU.max))
                elif j == 8:
                    CV(vv.tensor_reduce(srow[0:1, 1:2], sTp[s][0:1, 0:512], AX,
                                        ALU.max))
                else:
                    d = 0 if j < 8 else 1
                    CV(vv.tensor_reduce(srow[0:1, 2:3], sTp[s][0:1, 0:512], AX,
                                        ALU.max))
                    CV(vv.tensor_tensor(srow[0:1, d:d + 1], srow[0:1, d:d + 1],
                                        srow[0:1, 2:3], ALU.max))
            m_cpc[0] = CV.n

        def emit_c_final(hpx):
            @blk.vector
            def _(vv, hpx=hpx):
                CV(vv.tensor_tensor(srow[0:1, 5:6], srow[0:1, 0:1], srow[0:1, 1:2],
                                    ALU.mult))
                m_cnorm_read[0] = CV.n

            @blk.scalar
            def _(sc, hpx=hpx):
                sc.wait_ge(CV.sem, m_cnorm_read[0])
                CA(sc.activation(srow[0:1, 6:7], srow[0:1, 5:6], AF.Ln, bias=0.0,
                                 scale=1.0))
                CA(sc.activation(srow[0:1, 7:8], srow[0:1, 6:7], AF.Exp,
                                 bias=cbias[0:1, 4:5], scale=0.5))
            m_cval = CA.n

            @blk.vector
            def _(vv, hpx=hpx, m_cval=m_cval):
                vv.wait_ge(CA.sem, m_cval)
                CV(vv.tensor_scalar(srow[0:1, 6:7], srow[0:1, 7:8], -1.0 / 32.0,
                                    None, ALU.mult))
            m_negc = CV.n

            @blk.tensor
            def _(te, hpx=hpx, m_negc=m_negc):
                te.wait_ge(CV.sem, m_negc)
                CT(te.matmul(sT1[:, 0:1], onesr[:], srow[0:1, 6:7], start=True,
                             stop=True))
            m_cbmm = CT.n

            @blk.vector
            def _(vv, hpx=hpx, m_cbmm=m_cbmm):
                vv.wait_ge(CT.sem, m_cbmm)
                CV(vv.tensor_copy(cbc[:, hpx % 2:hpx % 2 + 1], sT1[:, 0:1]))
            m_cnorm_read[0] = CV.n
            return CV.n

        def emit_c_squares(hpx):
            par = hpx % 3
            kS_ = kslot[:, par * S:(par + 1) * S]
            qS_ = qslot[:, par * SO:(par + 1) * SO]

            @blk.vector
            def _(vv, hpx=hpx, par=par, kS_=kS_, qS_=qS_):
                vv.wait_ge(s_kq[par], 32 * (hpx // 3 + 1))
                if m_cmm_prev[0]:
                    vv.wait_ge(CT.sem, m_cmm_prev[0])
                CV(vv.tensor_tensor(sqbuf[:], kS_, kS_, ALU.mult))
                CV(vv.tensor_tensor(csq2[:], qS_, qS_, ALU.mult))
            m_csq[hpx] = CV.n

        def emit_c_finish(hpx):
            csl = [sTp[s][0:1, c * 512:(c + 1) * 512]
                   for s in range(3) for c in range(2)]

            @blk.tensor
            def _(te, hpx=hpx):
                te.wait_ge(CV.sem, m_csq[hpx])
                if last_exp[0]:
                    te.wait_ge(CA.sem, last_exp[0])
                if m_cnorm_read[0]:
                    te.wait_ge(CV.sem, m_cnorm_read[0])
                for pc in range(6):
                    mmi = te.matmul(csl[pc], onesc[:],
                                    sqbuf[:, pc * 512:(pc + 1) * 512],
                                    start=True, stop=True)
                CT(mmi)
            m1 = CT.n

            @blk.vector
            def _(vv, hpx=hpx, m1=m1):
                vv.wait_ge(CT.sem, m1)
                CV(vv.tensor_reduce(srow[0:1, 0:1], csl[0][0:1, :], AX, ALU.max))
                for pc in range(1, 6):
                    CV(vv.tensor_reduce(srow[0:1, 2:3], csl[pc][0:1, :], AX, ALU.max))
                    CV(vv.tensor_tensor(srow[0:1, 0:1], srow[0:1, 0:1],
                                        srow[0:1, 2:3], ALU.max))
            r1 = CV.n

            @blk.tensor
            def _(te, hpx=hpx, r1=r1):
                te.wait_ge(CV.sem, r1)
                te.matmul(csl[0], onesc[:], sqbuf[:, 6 * 512:7 * 512],
                          start=True, stop=True)
                te.matmul(csl[1], onesc[:], sqbuf[:, 7 * 512:8 * 512],
                          start=True, stop=True)
                for pc in range(4):
                    mmi = te.matmul(csl[2 + pc], onesb[:],
                                    csq2[:, pc * 512:(pc + 1) * 512],
                                    start=True, stop=True)
                CT(mmi)
            m2 = CT.n
            m_cmm_prev[0] = m2

            @blk.vector
            def _(vv, hpx=hpx, m2=m2):
                vv.wait_ge(CT.sem, m2)
                for pc in range(2):
                    CV(vv.tensor_reduce(srow[0:1, 2:3], csl[pc][0:1, :], AX, ALU.max))
                    CV(vv.tensor_tensor(srow[0:1, 0:1], srow[0:1, 0:1],
                                        srow[0:1, 2:3], ALU.max))
                CV(vv.tensor_reduce(srow[0:1, 1:2], csl[2][0:1, :], AX, ALU.max))
                for pc in range(1, 4):
                    CV(vv.tensor_reduce(srow[0:1, 2:3], csl[2 + pc][0:1, :], AX,
                                        ALU.max))
                    CV(vv.tensor_tensor(srow[0:1, 1:2], srow[0:1, 1:2],
                                        srow[0:1, 2:3], ALU.max))
                CV(vv.tensor_tensor(srow[0:1, 5:6], srow[0:1, 0:1], srow[0:1, 1:2],
                                    ALU.mult))
                m_cnorm_read[0] = CV.n

            @blk.scalar
            def _(sc, hpx=hpx):
                sc.wait_ge(CV.sem, m_cnorm_read[0])
                CA(sc.activation(srow[0:1, 6:7], srow[0:1, 5:6], AF.Ln, bias=0.0,
                                 scale=1.0))
                CA(sc.activation(srow[0:1, 7:8], srow[0:1, 6:7], AF.Exp,
                                 bias=cbias[0:1, 4:5], scale=0.5))
            m_cval = CA.n

            @blk.vector
            def _(vv, hpx=hpx, m_cval=m_cval):
                vv.wait_ge(CA.sem, m_cval)
                CV(vv.tensor_scalar(srow[0:1, 6:7], srow[0:1, 7:8], -1.0 / 32.0,
                                    None, ALU.mult))
            m_negc = CV.n

            @blk.tensor
            def _(te, hpx=hpx, m_negc=m_negc):
                te.wait_ge(CV.sem, m_negc)
                CT(te.matmul(sT1[:, 0:1], onesr[:], srow[0:1, 6:7], start=True,
                             stop=True))
            m_cbmm = CT.n

            @blk.vector
            def _(vv, hpx=hpx, m_cbmm=m_cbmm):
                vv.wait_ge(CT.sem, m_cbmm)
                CV(vv.tensor_copy(cbc[:, hpx % 2:hpx % 2 + 1], sT1[:, 0:1]))
            m_cnorm_read[0] = CV.n
            return CV.n

        pending_epi = [None]

        def emit_epi(hp_e, qb_e, m_pvg_e):
            for h in range(2):
                @blk.vector
                def _(vv, hp=hp_e, qb=qb_e, h=h, m_pvg=m_pvg_e):
                    vv.wait_ge(CT.sem, m_pvg[(len(GRP) - 1, h)])
                    if m_tp_prev[h]:
                        vv.wait_ge(CT.sem, m_tp_prev[h])
                    CV(vv.tensor_copy(pvf[:, h * 512:(h + 1) * 512][0:65, :],
                                      pvps[h][0:65, :]))
                m_pvf = CV.n

                @blk.tensor
                def _(te, hp=hp_e, qb=qb_e, h=h, m_pvf=m_pvf):
                    te.wait_ge(CV.sem, m_pvf)
                    te.wait_ge(CV.sem, m_atm_prev[h])
                    pf = pvf[:, h * 512:(h + 1) * 512]
                    for j in range(4):
                        te.transpose(pvps[h][:, j * 64:(j + 1) * 64],
                                     pf[0:64, j * 128:(j + 1) * 128],
                                     idf[0:64, 0:64])
                    for j in range(4):
                        mmi = te.transpose(pvps[h][:, 256 + j:257 + j],
                                           pf[64:65, j * 128:(j + 1) * 128],
                                           idf[64:65, 64:65])
                    m_tp_prev[h] = CT(mmi)

                @blk.vector
                def _(vv, hp=hp_e, qb=qb_e, h=h):
                    vv.wait_ge(CT.sem, m_tp_prev[h])
                    CV(vv.reciprocal(rlb[:, h * 4:(h + 1) * 4],
                                     pvps[h][:, 256:260]))
                    for j in range(4):
                        CV(vv.tensor_scalar(
                            attn_sb[:, (qb * 4 + j) * D + (2 * hp + h) * 64:
                                    (qb * 4 + j) * D + (2 * hp + h) * 64 + 64],
                            pvps[h][:, j * 64:(j + 1) * 64],
                            rlb[:, h * 4 + j:h * 4 + j + 1], None, ALU.mult))
                    m_atm_prev[h] = CV.n

        def flush_epi():
            if pending_epi[0] is not None:
                emit_epi(*pending_epi[0])
                pending_epi[0] = None

        def stream_kq(hp, gate_ct=0, gate_cv=0):
            par = hp % 3

            @blk.sync
            def _(sy, hp=hp, par=par, gate_ct=gate_ct, gate_cv=gate_cv):
                if gate_ct:
                    sy.wait_ge(CT.sem, gate_ct)
                if gate_cv:
                    sy.wait_ge(CV.sem, gate_cv)
                sy.dma_start(out=kslot[:, par * S:(par + 1) * S], in_=k_sp[hp]
                             ).then_inc(s_kq[par], 16)
                sy.dma_start(out=qslot[:, par * SO:(par + 1) * SO], in_=q_sp[hp]
                             ).then_inc(s_kq[par], 16)

        def stream_v(hp, gate_ct=0):
            par = hp % 2

            @blk.sync
            def _(sy, hp=hp, par=par, gate_ct=gate_ct):
                if gate_ct:
                    sy.wait_ge(CT.sem, gate_ct)
                sy.dma_start(out=vslot[:, par * KC * 130:(par + 1) * KC * 130],
                             in_=v_sp[hp]).then_inc(s_vv[par], 16)

        stream_kq(0)
        stream_v(0)
        stream_kq(1)
        stream_v(1)
        stream_kq(2)
        emit_c_squares(0)

        for hp in range(NHP):
            par = hp % 3
            kS = kslot[:, par * S:(par + 1) * S]
            vS = vslot[:, (hp % 2) * KC * 130:(hp % 2 + 1) * KC * 130]
            qS = qslot[:, par * SO:(par + 1) * SO]
            ecb = hp % 2

            if hp == 0:
                m_cb_cur[0] = emit_c_finish(0)
            m_qkgate = m_cb_cur[0]

            if hp == 4:
                bankfree_mark[0] = m_cb_cur[0]
                emit_jobF(0, bankfree_mark)
                emit_jobF(1, bankfree_mark)
                emit_jobF(2, bankfree_mark)
                f_done[0] = True
                m_cnorm_read[0] = max(m_cnorm_read[0], bankfree_mark[0])
                m_qkgate = max(m_qkgate, bankfree_mark[0])

            for qb in range(QB):
                m_exp = {}
                m_pvg = {}

                def emit_qk(gi, hp=hp, qb=qb, kS=kS, qS=qS, m_qkgate=m_qkgate,
                            m_exp=m_exp, extra_cv=0):
                    kc0, glen = GRP[gi]
                    s = gi % 3

                    @blk.tensor
                    def _(te, gi=gi, kc0=kc0, glen=glen, s=s, extra_cv=extra_cv):
                        if gi < 3:
                            pe = last_exp_set.get(s)
                            if pe:
                                te.wait_ge(CA.sem, pe)
                            te.wait_ge(CV.sem, m_qkgate)
                        else:
                            te.wait_ge(CA.sem, m_exp[gi - 3])
                        if extra_cv:
                            te.wait_ge(CV.sem, extra_cv)
                        kc = kc0
                        te.matmul(sTp[s][:, 0:512],
                                  kS[0:64, kc * 128:(kc + 1) * 128],
                                  qS[0:64, qb * 512:(qb + 1) * 512],
                                  start=True, stop=True, tile_position=(0, 0))
                        mmi = te.matmul(sTp[s][:, 512:1024],
                                        kS[64:128, kc * 128:(kc + 1) * 128],
                                        qS[64:128, qb * 512:(qb + 1) * 512],
                                        start=True, stop=True,
                                        tile_position=(64, 0))
                        CT(mmi)
                    return CT.n

                def emit_exp(gi, m_qk, hp=hp, qb=qb, ecb=ecb, m_exp=m_exp):
                    s = gi % 3
                    slot = gi % 2

                    @blk.scalar
                    def _(sc, gi=gi, s=s, slot=slot, m_qk=m_qk):
                        sc.wait_ge(CT.sem, m_qk)
                        prev = last_pv.get(slot)
                        if prev is not None:
                            sc.wait_ge(CT.sem, prev)
                        m_exp[gi] = CA(sc.activation(
                            pt[:, slot * 1024:(slot + 1) * 1024],
                            sTp[s][:], AF.Exp, bias=cbc[:, ecb:ecb + 1],
                            scale=1.0 / 32.0))
                        last_exp[0] = m_exp[gi]
                    last_exp_set[s] = m_exp[gi]

                def emit_pv(gi, hp=hp, qb=qb, vS=vS, m_exp=m_exp, m_pvg=m_pvg):
                    kc0, glen = GRP[gi]
                    slot = gi % 2
                    for h in range(2):
                        @blk.tensor
                        def _(te, gi=gi, h=h, kc0=kc0, slot=slot, hp=hp):
                            if h == 0:
                                te.wait_ge(CA.sem, m_exp[gi])
                            if gi == 0:
                                te.wait_ge(CV.sem, m_atm_prev[h])
                                if h == 0:
                                    te.wait_ge(s_vv[hp % 2], 16 * (hp // 2 + 1))
                            kc = kc0
                            lhsT = vS[:, kc * 130 + h * 65:kc * 130 + (h + 1) * 65]
                            rhs = pt[:, slot * 1024 + h * 512:
                                     slot * 1024 + (h + 1) * 512]
                            mmi = te.matmul(pvps[h][0:65, :], lhsT, rhs,
                                            start=(kc == 0), stop=(kc == KC - 1))
                            m_pvg[(gi, h)] = CT(mmi)
                    last_pv[slot] = m_pvg[(gi, 1)]

                NG = len(GRP)
                mq = {0: emit_qk(0), 1: emit_qk(1), 2: emit_qk(2)}
                emit_exp(0, mq[0])
                flush_epi()
                if qb == 1 and hp + 1 < NHP:
                    emit_c_squares(hp + 1)
                pop_jobs(1)
                for gi in range(NG):
                    emit_pv(gi)
                    extra_cv = 0
                    if qb == 3 and hp + 1 < NHP and gi >= 8 and gi % 2 == 0:
                        emit_c_piece(hp + 1, (gi - 8) // 2, gi % 3)
                        extra_cv = m_cpc[0]
                    if gi + 3 < NG:
                        mq[gi + 3] = emit_qk(gi + 3, extra_cv=extra_cv)
                    if gi + 1 < NG:
                        emit_exp(gi + 1, mq[gi + 1])
                    pop_gis = ((5, 10, 15, 20, 25, 29) if hp < 4 else
                               (4, 8, 12, 16, 20, 24, 28))
                    if gi in pop_gis:
                        pop_jobs(1)

                pending_epi[0] = (hp, qb, m_pvg)

            m_hp_last[hp] = CT.n
            if hp + 1 < NHP:
                m_cb_cur[0] = emit_c_final(hp + 1)
            if hp + 3 < NHP:
                stream_kq(hp + 3, gate_ct=m_hp_last[hp], gate_cv=m_csq[hp])
            if hp + 2 < NHP:
                stream_v(hp + 2, gate_ct=m_hp_last[hp])

        flush_epi()

        while jq:
            pop_jobs(1)

        hb = sqbuf[:, 0:D]
        y1f = sqbuf[:, D:2 * D]
        ub = sqbuf[:, 2 * D:3 * D]
        sqj3 = sqbuf[:, 3 * D:4 * D]
        kfb = wbuf[:, 0:D]
        tb_ = wbuf[:, D:2 * D]
        sg1 = abuf[:, 0:D]
        sg2 = qslot[:, 0:D]
        y1bf = kslot[:, 0:2 * D]
        y1qb = kslot[:, 2 * D:4 * D]
        kib = p2.enter_context(nc.sbuf_tensor("kib", [128, D], I32))
        n_y1s = [0]

        @blk.vector
        def _(vv):
            vv.wait_ge(s_wp, 16 * n_wp[0])
            vv.wait_ge(s_wpb, 16 * n_wpb[0])
            vv.wait_ge(CT.sem, m_hp_last[NHP - 1])
        for i in range(NTO):
            e = i % 2

            @blk.vector
            def _(vv, i=i, e=e):
                if i == 0:
                    vv.wait_ge(CV.sem, m_atm_prev[0])
                    vv.wait_ge(CV.sem, m_atm_prev[1])
                CV(vv.tensor_copy(hb, attn_sb[:, i * D:(i + 1) * D]))
                CV(vv.tensor_tensor(hb, hb, sres[:, i * D:(i + 1) * D], ALU.add))
                CV(vv.tensor_scalar(y1f, hb, 1.0, None, ALU.mult, ALU.add,
                                    accum_out=st[:, 0:1]))
            m_h = CV.n

            @blk.scalar
            def _(sc, i=i, m_h=m_h):
                sc.wait_ge(CV.sem, m_h)
                CA(sc.activation(sqj3, hb, AF.Square, bias=0.0, scale=1.0,
                                 accum_out=st[:, 2:3]))
            m_sq3 = CA.n

            @blk.vector
            def _(vv, i=i, m_sq3=m_sq3):
                vv.wait_ge(CA.sem, m_sq3)
                CV(vv.tensor_scalar(st[:, 10:11], st[:, 2:3], 1.0 / D, None, ALU.mult))
                CV(vv.tensor_scalar(st[:, 4:5], st[:, 0:1], 1.0 / D, None, ALU.mult))
                CV(vv.tensor_tensor(st[:, 11:12], st[:, 4:5], st[:, 4:5], ALU.mult))
                CV(vv.tensor_tensor(st[:, 8:9], st[:, 10:11], st[:, 11:12], ALU.subtract))
            m_v3 = CV.n

            @blk.scalar
            def _(sc, i=i, m_v3=m_v3):
                sc.wait_ge(CV.sem, m_v3)
                CA(sc.activation(st[:, 12:13], st[:, 8:9], AF.Ln, bias=cbias[:, 1:2],
                                 scale=1.0))
                CA(sc.activation(st[:, 6:7], st[:, 12:13], AF.Exp, bias=0.0, scale=-0.5))
            m_r3 = CA.n

            @blk.vector
            def _(vv, i=i, e=e, m_r3=m_r3):
                vv.wait_ge(CA.sem, m_r3)
                CV(vv.tensor_scalar(y1f, hb, st[:, 4:5], st[:, 6:7],
                                    ALU.subtract, ALU.mult))
                if i >= 2:
                    vv.wait_ge(s_y1s[i % 2], 32 * (i // 2))
                CV(vv.tensor_copy(y1bf[:, e * D:(e + 1) * D], y1f))
                CV(vv.tensor_reduce(st[:, 13:14], y1f, AX, ALU.max,
                                    apply_absolute_value=True))
                CV(vv.tensor_scalar(st[:, 13:14], st[:, 13:14], 1e-5, None, ALU.max))
                CV(vv.reciprocal(st[:, 10:11], st[:, 13:14]))
                CV(vv.tensor_scalar(st[:, 11:12], st[:, 10:11], 127.0, None, ALU.mult))
                CV(vv.reciprocal(rsx[:, i:i + 1], st[:, 11:12]))
                CV(vv.tensor_scalar(ub, y1f, st[:, 11:12], MAGIC, ALU.mult,
                                    ALU.add))
                CV(vv.tensor_copy(kib[:], ub))
                CV(vv.tensor_scalar(kfb, kib[:], 256.0, None, ALU.subtract))
                CV(vv.tensor_tensor(tb_, ub, kfb, ALU.subtract))
            m_t3 = CV.n

            @blk.scalar
            def _(sc, i=i, m_t3=m_t3):
                sc.wait_ge(CV.sem, m_t3)
                CA(sc.activation(sg1, tb_, AF.Sign, bias=cbias[:, 2:3], scale=1.0))
                CA(sc.activation(sg2, tb_, AF.Sign, bias=cbias[:, 3:4], scale=1.0))
            m_s3 = CA.n

            @blk.vector
            def _(vv, i=i, e=e, m_s3=m_s3):
                vv.wait_ge(CA.sem, m_s3)
                CV(vv.tensor_tensor(sg1, sg1, sg2, ALU.add))
                CV(vv.tensor_scalar(sg1, sg1, 0.5, None, ALU.mult))
                CV(vv.tensor_tensor(kfb, kfb, sg1, ALU.add))
                CV(vv.tensor_scalar(y1qb[:, e * D:(e + 1) * D], kfb, 127.0, -128.0,
                                    ALU.min, ALU.max))
            m_q3 = CV.n

            @blk.gpsimd
            def _(g, i=i, e=e, m_q3=m_q3):
                g.wait_ge(CV.sem, m_q3)
                g.dma_start(out=y1_dram[i * 128:(i + 1) * 128, :],
                            in_=y1bf[:, e * D:(e + 1) * D]).then_inc(s_y1s[e], 16)
                g.dma_start(out=y1q_dram[i * 128:(i + 1) * 128, :],
                            in_=y1qb[:, e * D:(e + 1) * D]).then_inc(s_y1s[e], 16)
            n_y1s[0] += 2

        if atn_dbg is not None:
            @blk.gpsimd
            def _(g):
                g.wait_ge(CV.sem, m_atm_prev[0])
                g.wait_ge(CV.sem, m_atm_prev[1])
                for i in range(NTO):
                    g.dma_start(out=atn_dbg[i * 128:(i + 1) * 128, :],
                                in_=attn_sb[:, i * D:(i + 1) * D]).then_inc(s_dbg, 16)
                    n_dbg[0] += 1

        @blk.sync
        def _(sy):
            sy.wait_ge(s_y1s[0], 16 * NTO)
            sy.wait_ge(s_y1s[1], 16 * NTO)
            sy.wait_ge(s_wt2, 16 * n_wt2[0])
            if n_dbg[0]:
                sy.wait_ge(s_dbg, 16 * n_dbg[0])
            sy.nop().then_inc(s_bar1, 1)

        @blk.gpsimd
        def _(g):
            CP.gate(g.nop()).then_inc(s_bar1, 1)

        @blk.vector
        def _(vv):
            CV.gate(vv.nop()).then_inc(s_bar1, 1)

        @blk.scalar
        def _(sc):
            CA.gate(sc.nop()).then_inc(s_bar1, 1)

        @blk.tensor
        def _(te):
            CT.gate(te.nop()).then_inc(s_bar1, 1)

    p2.close()
    p2a.close()
    pA.close()

    p4 = contextlib.ExitStack()
    sb4 = lambda n, sh, dt: p4.enter_context(nc.sbuf_tensor(n, sh, dt))
    y1qfm = sb4("y1qfm", [128, 8 * SO], BF16)
    wgu = sb4("wgu", [128, 4 * 4096], BF16)
    prodb = sb4("prodb", [128, 8 * INNER], BF16)
    wrkq = sb4("wrkq", [128, 2 * D], F32)
    pqtm = sb4("pqtm", [128, 4 * INNER], BF16)
    gsb = sb4("gsb", [128, 2 * 512], F32)
    usb = sb4("usb", [128, 2 * 512], F32)
    kib4 = p4.enter_context(nc.sbuf_tensor("kib4", [128, 2 * D], I32))
    dbq = sb4("dbq", [128, 2 * D], BF16)
    sgq = sb4("sgq", [128, 2 * D], BF16)
    ctq = sb4("ctq", [128, 2 * D], BF16)
    pqstage = sb4("pqstage", [128, 1024], BF16)
    p4p = contextlib.ExitStack()
    pg = [p4p.enter_context(nc.psum_tensor(f"pg{i}", [128, 512], F32)) for i in range(2)]
    pu = [p4p.enter_context(nc.psum_tensor(f"pu{i}", [128, 512], F32)) for i in range(2)]
    ptrb = [p4p.enter_context(nc.psum_tensor(f"ptrb{i}", [128, 1024], BF16))
            for i in range(2)]

    with nc.Block() as blk:
        @blk.sync
        def _(sy):
            sy.wait_ge(s_bar1, 10)
            for ic in range(8):
                sy.dma_start_transpose(y1qfm[:, ic * SO:(ic + 1) * SO],
                                       y1q_dram[:, ic * 128:(ic + 1) * 128]
                                       ).then_inc(s_y1t, 16)

        @blk.vector
        def _(vv):
            vv.wait_ge(s_bar1, 10)
            for m in range(NTO):
                CV(vv.tensor_tensor(rsg[:, m:m + 1], rsx[:, m:m + 1], binv[:, 0:1],
                                    ALU.mult))
                CV(vv.tensor_tensor(rsu[:, m:m + 1], rsx[:, m:m + 1], binv[:, 1:2],
                                    ALU.mult))
        m_rs = CV.n

        @blk.scalar
        def _(sc):
            sc.wait_ge(s_bar1, 10)
            sc.wait_ge(CV.sem, m_rs)
            CA(load_act_set(sc, nc, SET_SILU))

        @blk.tensor
        def _(te):
            te.wait_ge(s_bar1, 10)
            te.wait_ge(s_y1t, 16 * 8)

        @blk.gpsimd
        def _(g):
            g.wait_ge(s_bar1, 10)

        n_gu = [0]
        m_guuse = {}

        m_silu = {}
        m_produ = {}
        n_pqs = [0]
        s8mark = {}
        m_trmark = {}
        m_ptrb_ev = {}

        def mkstages(ch, prev_mk, m, tb):
            par = tb % 2
            q = ch % 2
            psl = prodb[:, (par * 4 + m) * INNER + ch * D:
                        (par * 4 + m) * INNER + (ch + 1) * D]
            qsl = pqtm[:, m * INNER + ch * D:m * INNER + (ch + 1) * D]
            wk = wrkq[:, q * D:(q + 1) * D]
            kib = kib4[:, q * D:(q + 1) * D]
            db = dbq[:, q * D:(q + 1) * D]
            sg = sgq[:, q * D:(q + 1) * D]
            ct = ctq[:, q * D:(q + 1) * D]
            mk = {}

            def s0():
                @blk.vector
                def _(vv):
                    if prev_mk is not None:
                        vv.wait_ge(CA.sem, prev_mk["f"])
                    CV(vv.tensor_scalar(wk, psl, st[:, 9:10], MAGIC, ALU.mult,
                                        ALU.add))
                    CV(vv.tensor_copy(kib, wk))
                    CV(vv.scalar_tensor_tensor(wk, kib, -1.0, wk, ALU.mult,
                                               ALU.add))
                    CV(vv.tensor_scalar(db, wk, 0.5, None, ALU.subtract))
                mk["base"] = CV.n

            def s1():
                @blk.scalar
                def _(sc):
                    sc.wait_ge(CV.sem, mk["base"])
                    if prev_mk is not None:
                        sc.wait_ge(CV.sem, prev_mk["fin"])
                    CA(sc.activation(sg, db, AF.Sign, bias=0.0, scale=1.0))
                mk["l1"] = CA.n

            def s2():
                @blk.vector
                def _(vv):
                    vv.wait_ge(CA.sem, mk["l1"])
                    CV(vv.scalar_tensor_tensor(db, sg, -4.0, db, ALU.mult,
                                               ALU.add))
                mk["a1"] = CV.n

            def s3():
                @blk.scalar
                def _(sc):
                    sc.wait_ge(CV.sem, mk["a1"])
                    CA(sc.activation(sg, db, AF.Sign, bias=0.0, scale=1.0))
                mk["l2"] = CA.n

            def s4():
                @blk.vector
                def _(vv):
                    vv.wait_ge(CA.sem, mk["l2"])
                    CV(vv.scalar_tensor_tensor(db, sg, -2.0, db, ALU.mult,
                                               ALU.add))
                mk["a2"] = CV.n

            def s5():
                @blk.scalar
                def _(sc):
                    sc.wait_ge(CV.sem, mk["a2"])
                    CA(sc.activation(sg, db, AF.Sign, bias=0.0, scale=1.0))
                mk["l3"] = CA.n

            def s6():
                @blk.vector
                def _(vv):
                    vv.wait_ge(CA.sem, mk["l3"])
                    CV(vv.scalar_tensor_tensor(db, sg, -1.0, db, ALU.mult,
                                               ALU.add))
                mk["tf"] = CV.n

            def s7():
                @blk.scalar
                def _(sc):
                    sc.wait_ge(CV.sem, mk["tf"])
                    CA(sc.activation(sg, db, AF.Sign, bias=cbias[:, 7:8],
                                     scale=1.0))
                    CA(sc.activation(ct, db, AF.Sign, bias=cbias[:, 8:9],
                                     scale=1.0))
                mk["f"] = CA.n

            def s8():
                @blk.vector
                def _(vv):
                    vv.wait_ge(CA.sem, mk["f"])
                    if tb >= 1:
                        vv.wait_ge(CT.sem, m_trmark.get((tb - 1, ch), 0))
                    CV(vv.scalar_tensor_tensor(wk, db, -1.0, wk, ALU.mult,
                                               ALU.add))
                    CV(vv.scalar_tensor_tensor(wk, sg, 0.5, wk, ALU.mult,
                                               ALU.add))
                    CV(vv.scalar_tensor_tensor(wk, ct, 0.5, wk, ALU.mult,
                                               ALU.add))
                    CV(vv.scalar_tensor_tensor(wk, kib, -256.5, wk, ALU.add,
                                               ALU.add))
                    CV(vv.tensor_scalar(qsl, wk, 127.0, -128.0, ALU.min,
                                        ALU.max))
                mk["fin"] = CV.n
                if m == 3:
                    s8mark[(tb, ch)] = mk["fin"]

            return [s0, s1, s2, s3, s4, s5, s6, s7, s8], mk

        def mk_quant_thunks(tb):
            thunks = []
            for m in range(4):
                tok = tb * 4 + m

                def amax(m=m, tok=tok, tb=tb):
                    par = tb % 2

                    @blk.vector
                    def _(vv):
                        CV(vv.tensor_reduce(
                            st[:, 13:14],
                            prodb[:, (par * 4 + m) * INNER:
                                  (par * 4 + m + 1) * INNER],
                            AX, ALU.max, apply_absolute_value=True))
                        CV(vv.tensor_scalar(st[:, 13:14], st[:, 13:14], 1e-5,
                                            None, ALU.max))
                        CV(vv.reciprocal(st[:, 10:11], st[:, 13:14]))
                        CV(vv.tensor_scalar(st[:, 11:12], st[:, 10:11], 127.0,
                                            None, ALU.mult))
                        CV(vv.reciprocal(rs2[:, tok:tok + 1], st[:, 11:12]))
                        CV(vv.tensor_scalar(st[:, 9:10], st[:, 11:12], 1.0,
                                            None, ALU.mult))
                thunks.append(amax)

                chstages = []
                chmks = []
                for ch in range(4):
                    prev = chmks[ch - 2] if ch >= 2 else None
                    stg, mkd = mkstages(ch, prev, m, tb)
                    chstages.append(stg)
                    chmks.append(mkd)
                nst = 9
                offs = (0, 3, 8, 11)
                for w in range(nst + offs[-1]):
                    for ch in range(4):
                        sidx2 = w - offs[ch]
                        if 0 <= sidx2 < nst:
                            thunks.append(chstages[ch][sidx2])
            return thunks

        def mk_transpose_thunks(tb):
            thunks = []
            for ic in range(32):
                def unit(ic=ic, tb=tb):
                    tpar = ic % 2
                    seq = tb * 32 + ic

                    @blk.tensor
                    def _(te, ic=ic, tb=tb, tpar=tpar, seq=seq):
                        te.wait_ge(CV.sem, s8mark[(tb, ic // 8)])
                        pe_ = m_ptrb_ev.get(seq - 2)
                        if pe_:
                            te.wait_ge(CV.sem, pe_)
                        for m in range(4):
                            CT(te.transpose(ptrb[tpar][:, m * 128:(m + 1) * 128],
                                            pqtm[:, m * INNER + ic * 128:
                                                 m * INNER + (ic + 1) * 128],
                                            idb[:]))
                    m_tr = CT.n
                    if ic % 8 == 7:
                        m_trmark[(tb, ic // 8)] = m_tr

                    @blk.vector
                    def _(vv, ic=ic, tb=tb, tpar=tpar, m_tr=m_tr, npq=n_pqs[0]):
                        vv.wait_ge(CT.sem, m_tr)
                        if npq >= 1:
                            vv.wait_ge(s_pqs, 16 * (npq - 1))
                        CV(vv.tensor_copy(pqstage[:, tpar * 512:(tpar + 1) * 512],
                                          ptrb[tpar][:, 0:512]))
                    m_ptrb_ev[seq] = CV.n

                    @blk.gpsimd
                    def _(g, ic=ic, tb=tb, tpar=tpar, m_ev=m_ptrb_ev[seq]):
                        g.wait_ge(CV.sem, m_ev)
                        g.dma_start(out=pqd[tb, ic],
                                    in_=pqstage[:, tpar * 512:(tpar + 1) * 512]
                                    ).then_inc(s_pqs, 16)
                    n_pqs[0] += 1
                thunks.append(unit)
            return thunks

        qth = []
        for tb in range(NTB + 1):
            if tb < NTB:
                par = tb % 2
                for span in range(8):
                    sidx = tb * 8 + span
                    spar = sidx % 2

                    @blk.sync
                    def _(sy, tb=tb, span=span, sidx=sidx, spar=spar):
                        prev = m_guuse.get(sidx - 2)
                        if prev:
                            sy.wait_ge(CT.sem, prev)
                        sy.dma_start(
                            out=wgu[:, (2 * spar) * 4096:(2 * spar) * 4096 + 4096],
                            in_=wgsp_d[0, span]).then_inc(s_gu[spar], 16)

                    @blk.gpsimd
                    def _(g, tb=tb, span=span, sidx=sidx, spar=spar):
                        prev = m_guuse.get(sidx - 2)
                        if prev:
                            g.wait_ge(CT.sem, prev)
                        g.dma_start(
                            out=wgu[:, (2 * spar + 1) * 4096:
                                    (2 * spar + 1) * 4096 + 4096],
                            in_=wgsp_d[1, span]).then_inc(s_gu2[spar], 16)

                    for m in range(4):
                        mpar = (span * 4 + m) % 2
                        tok = tb * 4 + m

                        @blk.tensor
                        def _(te, tb=tb, span=span, m=m, sidx=sidx, spar=spar,
                              mpar=mpar, tok=tok):
                            te.wait_ge(s_gu[spar], 16 * (sidx // 2 + 1))
                            te.wait_ge(s_gu2[spar], 16 * (sidx // 2 + 1))
                            pm = m_silu.get(mpar)
                            if pm:
                                te.wait_ge(CA.sem, pm[1])
                            for ic in range(8):
                                lhsT = y1qfm[:, ic * SO + tok * 128:
                                             ic * SO + (tok + 1) * 128]
                                rhs = wgu[:, (2 * spar) * 4096 + ic * 512:
                                          (2 * spar) * 4096 + (ic + 1) * 512]
                                te.matmul(pg[mpar][:], lhsT, rhs, start=(ic == 0),
                                          stop=(ic == 7))
                            for ic in range(8):
                                lhsT = y1qfm[:, ic * SO + tok * 128:
                                             ic * SO + (tok + 1) * 128]
                                rhs = wgu[:, (2 * spar + 1) * 4096 + ic * 512:
                                          (2 * spar + 1) * 4096 + (ic + 1) * 512]
                                mmi = te.matmul(pu[mpar][:], lhsT, rhs,
                                                start=(ic == 0), stop=(ic == 7))
                            m_guuse[sidx] = CT(mmi)
                        m_mmgu = CT.n

                        @blk.scalar
                        def _(sc, tb=tb, m=m, mpar=mpar, tok=tok, m_mmgu=m_mmgu):
                            sc.wait_ge(CT.sem, m_mmgu)
                            pv_ = m_produ.get(mpar)
                            if pv_:
                                sc.wait_ge(CV.sem, pv_)
                            CA(sc.activation(gsb[:, mpar * 512:(mpar + 1) * 512],
                                             pg[mpar][:], AF.Silu, bias=0.0,
                                             scale=rsg[:, tok:tok + 1]))
                            CA(sc.activation(usb[:, mpar * 512:(mpar + 1) * 512],
                                             pu[mpar][:], AF.Copy, bias=0.0,
                                             scale=rsu[:, tok:tok + 1]))
                            m_silu[mpar] = (CA.n - 1, CA.n)

                        @blk.vector
                        def _(vv, tb=tb, span=span, m=m, mpar=mpar, par=par):
                            vv.wait_ge(CA.sem, m_silu[mpar][1])
                            m_produ[mpar] = CV(vv.tensor_tensor(
                                prodb[:, (par * 4 + m) * INNER + span * 512:
                                      (par * 4 + m) * INNER + (span + 1) * 512],
                                gsb[:, mpar * 512:(mpar + 1) * 512],
                                usb[:, mpar * 512:(mpar + 1) * 512], ALU.mult))

                        k = (len(qth) + (31 - (span * 4 + m))) // (32 - (span * 4 + m))
                        for _ in range(min(k, len(qth))):
                            qth.pop(0)()
            while qth:
                qth.pop(0)()
            if tb < NTB:
                qth = mk_quant_thunks(tb) + mk_transpose_thunks(tb)

        @blk.sync
        def _(sy):
            sy.wait_ge(s_pqs, 16 * n_pqs[0])
            sy.nop().then_inc(s_bar1, 1)

        @blk.gpsimd
        def _(g):
            CP.gate(g.nop()).then_inc(s_bar1, 1)

        @blk.vector
        def _(vv):
            CV.gate(vv.nop()).then_inc(s_bar1, 1)

        @blk.scalar
        def _(sc):
            CA.gate(sc.nop()).then_inc(s_bar1, 1)

        @blk.tensor
        def _(te):
            CT.gate(te.nop()).then_inc(s_bar1, 1)

    p4p.close()
    p4.close()

    p5 = contextlib.ExitStack()
    sb5 = lambda n, sh, dt: p5.enter_context(nc.sbuf_tensor(n, sh, dt))
    wdT = sb5("wdT", [128, 32 * D], BF16)
    pqsl = sb5("pqsl", [128, 8 * 512], BF16)
    y1sl = sb5("y1sl", [128, 2 * D], BF16)
    y2b = sb5("y2b", [128, 2 * D], F32)
    h5 = sb5("h5", [128, 2 * D], F32)
    y1f5 = sb5("y1f5", [128, 2 * D], F32)
    sqj5 = sb5("sqj5", [128, D], F32)
    outb = sb5("outb", [128, 2 * D], F32)
    p5p = contextlib.ExitStack()
    y2ps = [p5p.enter_context(nc.psum_tensor(f"y2ps{m}", [128, 1024], F32))
            for m in range(4)]

    with nc.Block() as blk:
        @blk.sync
        def _(sy):
            sy.wait_ge(s_bar1, 15)
            for g4 in range(8):
                sy.dma_start(out=wdT[:, g4 * 4 * D:(g4 + 1) * 4 * D],
                             in_=wdsp_d[g4]).then_inc(s_wdt, 16)

        @blk.vector
        def _(vv):
            vv.wait_ge(s_bar1, 15)
            for m in range(NTO):
                CV(vv.tensor_tensor(rs2w[:, m:m + 1], rs2[:, m:m + 1], binv[:, 2:3],
                                    ALU.mult))
        m_rs2w = CV.n

        @blk.scalar
        def _(sc):
            sc.wait_ge(s_bar1, 15)
            CA(load_act_set(sc, nc, SET_LNEXP))

        @blk.tensor
        def _(te):
            te.wait_ge(s_bar1, 15)

        @blk.gpsimd
        def _(g):
            g.wait_ge(s_bar1, 15)

        n_pql = [0, 0]
        n_y1l = [0, 0]
        n_out = [0]
        m_pq_use = {}
        m_evy2 = {}
        m_y1use = {}
        m_y2use = {}
        m_v5d = {}
        for tb in range(NTB):
            for icg in range(8):
                gpar = icg % 2

                @blk.sync
                def _(sy, tb=tb, icg=icg, gpar=gpar):
                    prev = m_pq_use.get((tb * 8 + icg) - 2)
                    if prev:
                        sy.wait_ge(CT.sem, prev)
                    sy.dma_start(
                        out=pqsl[:, gpar * 2048:(gpar + 1) * 2048].rearrange(
                            "p (g t) -> p g t", g=4),
                        in_=pqd[tb, icg * 4:(icg + 1) * 4].rearrange("g p t -> p g t")
                    ).then_inc(s_pq2[gpar], 16)
                n_pql[gpar] += 1

                @blk.tensor
                def _(te, tb=tb, icg=icg, gpar=gpar, npl=n_pql[gpar]):
                    te.wait_ge(s_pq2[gpar], 16 * npl)
                    if tb == 0:
                        te.wait_ge(s_wdt, 16 * (icg + 1))
                    if icg == 0:
                        for mm in range(4):
                            pm = m_evy2.get(mm)
                            if pm:
                                te.wait_ge(CA.sem, pm)
                    for sub in range(4):
                        ic = icg * 4 + sub
                        for m in range(4):
                            lhsT = pqsl[:, gpar * 2048 + sub * 512 + m * 128:
                                        gpar * 2048 + sub * 512 + (m + 1) * 128]
                            te.matmul(y2ps[m][:, 0:512], lhsT,
                                      wdT[:, ic * D:ic * D + 512],
                                      start=(ic == 0), stop=(ic == 31))
                            mmi = te.matmul(y2ps[m][:, 512:1024], lhsT,
                                            wdT[:, ic * D + 512:(ic + 1) * D],
                                            start=(ic == 0), stop=(ic == 31))
                    m_pq_use[tb * 8 + icg] = CT(mmi)
            m_dn = CT.n

            for m in range(4):
                tok = tb * 4 + m
                e5 = tok % 2
                y2s = y2b[:, e5 * D:(e5 + 1) * D]
                h5s = h5[:, e5 * D:(e5 + 1) * D]
                y1s = y1f5[:, e5 * D:(e5 + 1) * D]

                @blk.gpsimd
                def _(g, tok=tok, e5=e5, m_dn=m_dn):
                    if tok >= 2:
                        g.wait_ge(CV.sem, m_y1use[tok - 2])
                    g.dma_start(out=y1sl[:, e5 * D:(e5 + 1) * D],
                                in_=y1_dram[tok * 128:(tok + 1) * 128, :]
                                ).then_inc(s_y1l[e5], 16)
                n_y1l[e5] += 1

                @blk.scalar
                def _(sc, tb=tb, m=m, tok=tok, e5=e5, m_dn=m_dn, y2s=y2s):
                    sc.wait_ge(CT.sem, m_dn)
                    if tok == 0:
                        sc.wait_ge(CV.sem, m_rs2w)
                    pv_ = m_y2use.get(tok - 2)
                    if pv_:
                        sc.wait_ge(CV.sem, pv_)
                    CA(sc.activation(y2s, y2ps[m][:], AF.Copy, bias=0.0,
                                     scale=rs2w[:, tok:tok + 1]))
                m_y2e = CA.n
                m_evy2[m] = m_y2e

                @blk.vector
                def _(vv, tok=tok, e5=e5, m_y2e=m_y2e, nyl=n_y1l[e5],
                      y2s=y2s, h5s=h5s, y1s=y1s):
                    vv.wait_ge(CA.sem, m_y2e)
                    vv.wait_ge(s_y1l[e5], 16 * nyl)
                    CV(vv.tensor_copy(y1s, y1sl[:, e5 * D:(e5 + 1) * D]))
                    m_y1use[tok] = CV.n
                    CV(vv.tensor_tensor(h5s, y2s, y1s, ALU.add))
                    m_y2use[tok] = CV.n
                    CV(vv.tensor_scalar(y1s, h5s, 1.0, None, ALU.mult, ALU.add,
                                        accum_out=st[:, e5:e5 + 1]))
                m_h5 = CV.n

                @blk.scalar
                def _(sc, tok=tok, e5=e5, m_h5=m_h5, h5s=h5s):
                    sc.wait_ge(CV.sem, m_h5)
                    pv_ = m_v5d.get(tok - 2)
                    if pv_:
                        sc.wait_ge(CV.sem, pv_)
                    CA(sc.activation(sqj5[:], h5s, AF.Square, bias=0.0, scale=1.0,
                                     accum_out=st[:, 2 + e5:3 + e5]))
                m_sq5 = CA.n

                @blk.vector
                def _(vv, tok=tok, e5=e5, m_sq5=m_sq5):
                    vv.wait_ge(CA.sem, m_sq5)
                    CV(vv.tensor_scalar(st[:, 10:11], st[:, 2 + e5:3 + e5], 1.0 / D,
                                        None, ALU.mult))
                    CV(vv.tensor_scalar(st[:, 4 + e5:5 + e5], st[:, e5:e5 + 1],
                                        1.0 / D, None, ALU.mult))
                    CV(vv.tensor_tensor(st[:, 11:12], st[:, 4 + e5:5 + e5],
                                        st[:, 4 + e5:5 + e5], ALU.mult))
                    CV(vv.tensor_tensor(st[:, 8 + e5:9 + e5], st[:, 10:11],
                                        st[:, 11:12], ALU.subtract))
                m_v5 = CV.n
                m_v5d[tok] = m_v5

                @blk.scalar
                def _(sc, e5=e5, m_v5=m_v5):
                    sc.wait_ge(CV.sem, m_v5)
                    CA(sc.activation(st[:, 12:13], st[:, 8 + e5:9 + e5], AF.Ln,
                                     bias=cbias[:, 1:2], scale=1.0))
                    CA(sc.activation(st[:, 6 + e5:7 + e5], st[:, 12:13], AF.Exp,
                                     bias=0.0, scale=-0.5))
                m_r5 = CA.n

                @blk.vector
                def _(vv, tok=tok, e5=e5, m_r5=m_r5, nout=n_out[0], h5s=h5s):
                    vv.wait_ge(CA.sem, m_r5)
                    if tok >= 2:
                        vv.wait_ge(s_out, 16 * nout)
                    CV(vv.tensor_scalar(outb[:, e5 * D:(e5 + 1) * D], h5s,
                                        st[:, 4 + e5:5 + e5], st[:, 6 + e5:7 + e5],
                                        ALU.subtract, ALU.mult))
                m_o5 = CV.n

                @blk.sync
                def _(sy, tok=tok, e5=e5, m_o5=m_o5):
                    sy.wait_ge(CV.sem, m_o5)
                    sy.dma_start(out=out_d[tok * 128:(tok + 1) * 128, :],
                                 in_=outb[:, e5 * D:(e5 + 1) * D]
                                 ).then_inc(s_out, 16)
                n_out[0] += 1

        @blk.sync
        def _(sy):
            sy.wait_ge(s_out, 16 * NTO)

    p5p.close()
    p5.close()
    es.close()
    return nc


_CACHE = {}


def _program(debug=()):
    key = tuple(sorted(debug))
    if key not in _CACHE:
        _CACHE[key] = build(debug)
    return _CACHE[key]


def kernel(L0_attn_x, L0_attn_ln_scale, L0_attn_ln_bias, L0_attn_wq, L0_attn_wk,
           L0_attn_wv, L0_ffn_w_gate, L0_ffn_w_up, L0_ffn_w_down, _debug=()):
    x = np.ascontiguousarray(np.asarray(L0_attn_x, dtype=np.float32))
    wq = np.ascontiguousarray(np.asarray(L0_attn_wq, dtype=np.float32))
    wk = np.ascontiguousarray(np.asarray(L0_attn_wk, dtype=np.float32))
    wv = np.ascontiguousarray(np.asarray(L0_attn_wv, dtype=np.float32))
    lnsc = np.ascontiguousarray(np.asarray(L0_attn_ln_scale, np.float32)).reshape(1, D)
    lnbi = np.ascontiguousarray(np.asarray(L0_attn_ln_bias, np.float32)).reshape(1, D)
    wg = np.ascontiguousarray(np.asarray(L0_ffn_w_gate, dtype=np.float32))
    wu = np.ascontiguousarray(np.asarray(L0_ffn_w_up, dtype=np.float32))
    wd = np.ascontiguousarray(np.asarray(L0_ffn_w_down, dtype=np.float32))
    idb = np.eye(128, dtype=BFD)
    idf = np.eye(128, dtype=np.float32)

    nc = _program(_debug)
    in_maps = []
    for c in range(8):
        b, h = c // 2, c % 2
        own = x[b, h * SO:(h + 1) * SO, :]
        ctx = x[b, (1 - h) * SO:(2 - h) * SO, :]
        xc = np.concatenate([own, ctx], axis=0)
        in_maps.append({"x": xc, "wq": wq, "wk": wk, "wv": wv,
                        "lnsc": lnsc, "lnbi": lnbi,
                        "wg": wg, "wu": wu, "wd": wd,
                        "idb": idb, "idf": idf})
    res = run_bass_kernel_spmd(nc, in_maps, core_ids=list(range(8)))
    out = np.empty((4, S, D), dtype=np.float32)
    for c in range(8):
        b, h = c // 2, c % 2
        out[b, h * SO:(h + 1) * SO, :] = res.results[c]["out"]
    if _debug:
        return out, [res.results[c] for c in range(8)]
    return out
